# version 02
# baseline (speedup 1.0000x reference)
import numpy as np
import ml_dtypes
import contextlib
import concourse.bass as bass
import concourse.mybir as mybir
from concourse.bass_utils import run_bass_kernel_spmd

F32 = mybir.dt.float32
F32R = mybir.dt.float32r
BF16 = mybir.dt.bfloat16
AF = mybir.ActivationFunctionType
ALU = mybir.AluOpType
AX = mybir.AxisListType

T, D, DIN, DST, DCONV, DTR = 4096, 1024, 2048, 16, 4, 64
NH, DH, WIN, HID = 8, 64, 512, 2730
CH, SC = 512, 1024
NCH, NSC = T // CH, T // SC
DT_J, DJ = DIN // 128, D // 128
NPAIR = (HID + 127) // 128
RAG = HID - 128 * (NPAIR - 1)
NEG = -50000.0
EPS = 1e-5

_TAPS = {}
_LAST_LABS = {}
_LAST_COUNTS = {}


def host_constants():
    c = {}
    c["ident_b"] = np.eye(128, dtype=ml_dtypes.bfloat16)
    c["ident_r"] = np.eye(128, dtype=np.float32)
    P = np.zeros((128, 128), np.float32)
    for h0 in (0, 64):
        for i in range(32):
            P[h0 + i, h0 + i + 32] = -1.0
            P[h0 + i + 32, h0 + i] = 1.0
    c["rperm_r"] = np.ascontiguousarray(P.T)
    c["onesrow_r"] = np.ones((1, 128), np.float32)
    c["oneskcol_r"] = np.ones((128, 1), np.float32)
    c["oneskcol_b"] = np.ones((128, 1), dtype=ml_dtypes.bfloat16)
    ho = np.zeros((128, 2), np.float32)
    ho[0:64, 0] = 1.0
    ho[64:128, 1] = 1.0
    c["headones_r"] = ho
    qq = np.arange(128)[:, None]
    cc = np.arange(128)[None, :]
    c["mask_d0"] = np.where(qq <= cc, 0.0, NEG).astype(np.float32)
    c["mask_d4"] = np.where(qq >= cc, 0.0, NEG).astype(np.float32)
    s = np.arange(1408)[None, :]
    c["maskT"] = np.where((s - qq >= 384) & (s - qq <= 896), 0.0, NEG).astype(np.float32)
    hi = np.arange(0, DH, 2, dtype=np.float64)
    inv = 1.0 / 10000.0 ** (hi / DH)
    t = np.arange(T, dtype=np.float64)
    fr = np.concatenate([t[:, None] * inv[None, :]] * 2, 1)
    base = (hi + 0.4 * DH) / (1.4 * DH)
    power = (t - T // 2) / (WIN // 2)
    scl = np.concatenate([base[None, :] ** power[:, None]] * 2, 1)
    cos, sin = np.cos(fr), np.sin(fr)
    for nm, tab in (("csq", 8.0 * cos * scl), ("ssq", 8.0 * sin * scl),
                    ("csk", cos / scl), ("ssk", sin / scl)):
        tt = tab.T.astype(np.float32)
        c[nm] = np.ascontiguousarray(np.concatenate([tt, tt], 0))
    c["zeros_c"] = np.zeros((128, 4), np.float32)
    return c


def host_layout(inp):
    f = np.ascontiguousarray
    h = {}
    h["inwT"] = f(inp["L0_attn_in_proj_w"].T)
    h["outwT"] = f(inp["L0_attn_out_proj_w"].T)
    h["xpwT"] = f(inp["L0_attn_x_proj_w"].T)
    h["dtwT"] = f(inp["L0_attn_dt_proj_w"].T)
    cw = inp["L0_attn_conv_w"]
    cd = np.zeros((DT_J, DCONV, 128, 128), np.float32)
    for j in range(DT_J):
        for k in range(DCONV):
            np.fill_diagonal(cd[j, k], cw[128 * j:128 * (j + 1), k])
    h["cdiag"] = cd
    dd = np.zeros((DT_J, 128, 128), np.float32)
    for j in range(DT_J):
        np.fill_diagonal(dd[j], inp["L0_attn_Dp"][128 * j:128 * (j + 1)])
    h["ddiag"] = dd
    h["convb2"] = f(inp["L0_attn_conv_b"].reshape(DT_J, 128).T)
    h["dtb2"] = f(inp["L0_attn_dt_proj_b"].reshape(DT_J, 128).T)
    h["alog2"] = f(inp["L0_attn_A_log"].reshape(DT_J, 128, DST)
                   .transpose(1, 0, 2).reshape(128, DT_J * DST))
    for l in range(2):
        h[f"wqkv{l}"] = f(inp["L0_ffn_wqkv"][l])
        h[f"wo{l}"] = f(inp["L0_ffn_wo"][l])
        h[f"qs{l}"] = f(np.tile(inp["L0_ffn_q_scale"][l], 2)[:, None])
        h[f"ks{l}"] = f(np.tile(inp["L0_ffn_k_scale"][l], 2)[:, None])
        h[f"lng{l}"] = f(inp["L0_ffn_ln_g"][l].reshape(DJ, 128).T)
        h[f"lnb{l}"] = f(inp["L0_ffn_ln_b"][l].reshape(DJ, 128).T)
        h[f"b2t{l}"] = f(inp["L0_ffn_b2"][l].reshape(DJ, 128).T)
        w1 = inp["L0_ffn_w1"][l]
        b1 = inp["L0_ffn_b1"][l]
        perm = []
        bt = np.zeros((128, 2 * NPAIR), np.float32)
        for p in range(NPAIR):
            wdt = min(128, HID - 128 * p)
            ia = np.arange(128 * p, 128 * p + wdt)
            perm.extend(ia)
            perm.extend(ia + HID)
            bt[:wdt, 2 * p] = b1[ia]
            bt[:wdt, 2 * p + 1] = b1[ia + HID]
        h[f"w1p{l}"] = f(w1[:, np.array(perm)])
        h[f"b1t{l}"] = bt
        h[f"w2_{l}"] = f(inp["L0_ffn_w2"][l])
    return h


class Buf:
    __slots__ = ("w", "r", "name")

    def __init__(self, name=""):
        self.w, self.r, self.name = None, {}, name


class Sched:
    ENGS = ("T", "V", "A", "P", "S")
    SELF_WAIT = ("A", "P", "V")

    def __init__(self):
        self.cnt = {e: 0 for e in self.ENGS}
        self.seen = {e: {} for e in self.ENGS}
        self.items = {e: [] for e in self.ENGS}
        self.curlab = ""
        self.labs = {e: [] for e in self.ENGS}

    def mark(self, name):
        for e in self.ENGS:
            self.items[e].append(("m", name))
            self.labs[e].append("mark")

    def _wait(self, e, sem, v):
        if v > self.seen[e].get(sem, 0):
            self.items[e].append(("w", sem, v))
            self.labs[e].append("w." + self.curlab + ":" + sem)
            self.seen[e][sem] = v

    def op(self, e, fn, reads=(), writes=(), inc=1, dma=False):
        if dma:
            assert writes, "DMA needs a write target buf"
            sem = "D:" + writes[0].name
            self.cnt.setdefault(sem, 0)
        else:
            sem = e
        deps = {}
        for b in reads:
            if b.w is not None:
                deps[b.w[0]] = max(deps.get(b.w[0], 0), b.w[1])
        for b in writes:
            if b.w is not None:
                deps[b.w[0]] = max(deps.get(b.w[0], 0), b.w[1])
            for s2, v in b.r.items():
                deps[s2] = max(deps.get(s2, 0), v)
        for s2, v in deps.items():
            if s2 == e:
                if e in self.SELF_WAIT:
                    self._wait(e, s2, v)
                continue
            self._wait(e, s2, v)
        self.cnt[sem] += inc
        self.items[e].append(("i", fn, sem, inc))
        self.labs[e].append("i." + self.curlab)
        rmark = self.cnt[sem] + (1 if inc == 0 and not dma else 0)
        for b in reads:
            b.r[sem] = max(b.r.get(sem, 0), rmark)
        if inc > 0:
            for b in writes:
                b.w = (sem, self.cnt[sem])
                b.r = {}

    def barrier(self):
        tgt = dict(self.cnt)
        for e in self.ENGS:
            for sem, v in tgt.items():
                if sem != e and v > 0:
                    self._wait(e, sem, v)


def build(taps=None, trunc=None):
    taps = taps or {}
    nc = bass.Bass("TRN2", target_bir_lowering=False, debug=False)
    dram = {}

    def din(name, shape, dt):
        dram[name] = nc.dram_tensor(name, shape, dt, kind="ExternalInput").ap()
        return dram[name]

    def dscr(name, shape, dt):
        dram[name] = nc.dram_tensor(name, shape, dt, kind="Internal").ap()
        return dram[name]

    x_in = din("x", [D, T], F32R)
    inwT = din("inwT", [D, 2 * DIN], F32)
    outwT = din("outwT", [DIN, D], F32)
    xpwT = din("xpwT", [DIN, DTR + 2 * DST], F32R)
    dtwT = din("dtwT", [DTR, DIN], F32R)
    cdiagd = din("cdiag", [DT_J, DCONV, 128, 128], F32)
    ddiagd = din("ddiag", [DT_J, 128, 128], F32)
    convb2 = din("convb2", [128, DT_J], F32)
    dtb2 = din("dtb2", [128, DT_J], F32)
    alog2 = din("alog2", [128, DT_J * DST], F32)
    wqkv = [din(f"wqkv{l}", [D, 3 * 512], F32R) for l in range(2)]
    wod = [din(f"wo{l}", [512, D], F32R) for l in range(2)]
    qs2 = [din(f"qs{l}", [128, 1], F32) for l in range(2)]
    ks2 = [din(f"ks{l}", [128, 1], F32) for l in range(2)]
    lng2 = [din(f"lng{l}", [128, DJ], F32) for l in range(2)]
    lnb2 = [din(f"lnb{l}", [128, DJ], F32) for l in range(2)]
    b2t2 = [din(f"b2t{l}", [128, DJ], F32) for l in range(2)]
    w1p = [din(f"w1p{l}", [D, 2 * HID], F32R) for l in range(2)]
    b1t2 = [din(f"b1t{l}", [128, 2 * NPAIR], F32) for l in range(2)]
    w2d = [din(f"w2_{l}", [HID, D], F32) for l in range(2)]
    cmeta = {"ident_b": ([128, 128], BF16), "ident_r": ([128, 128], F32R),
             "rperm_r": ([128, 128], F32R), "onesrow_r": ([1, 128], F32R),
             "oneskcol_r": ([128, 1], F32R), "oneskcol_b": ([128, 1], BF16),
             "headones_r": ([128, 2], F32R), "mask_d0": ([128, 128], F32R),
             "mask_d4": ([128, 128], F32R), "maskT": ([128, 1408], F32R),
             "csq": ([128, T], F32), "ssq": ([128, T], F32),
             "csk": ([128, T], F32), "ssk": ([128, T], F32),
             "zeros_c": ([128, 4], F32)}
    for nm, (shape, dt) in cmeta.items():
        din(nm, shape, dt)

    rowsB = dscr("rowsB", [NCH, DST, CH], F32)
    rowsC = dscr("rowsC", [NCH, DST, CH], BF16)
    rr = dscr("rr", [48, SC], F32)
    x1s = dscr("x1s", [D, T], F32)
    xms = [dscr(f"xm{l}", [D, T], F32) for l in range(2)]
    xl0 = dscr("xl0", [D, T], F32)
    out = nc.dram_tensor("out", [D, T], F32, kind="ExternalOutput").ap()
    dram["out"] = out
    tap_out = {nm: nc.dram_tensor("tap_" + nm, sh, dt, kind="ExternalOutput").ap()
               for nm, (sh, dt) in taps.items()}

    sch = Sched()
    es = contextlib.ExitStack()
    with es:
        AR = es.enter_context(nc.sbuf_tensor("AR", [128, 19456], F32R))
        AB = es.enter_context(nc.sbuf_tensor("AB", [128, 29824], BF16))
        AXS = es.enter_context(nc.sbuf_tensor("AX", [128, 11328], F32))
        SM = es.enter_context(nc.sbuf_tensor("SM", [128, 1024], F32))
        CST = es.enter_context(nc.sbuf_tensor("CST", [128, 3712], F32R))
        CSB = es.enter_context(nc.sbuf_tensor("CSB", [128, 256], BF16))
        DTT = es.enter_context(nc.sbuf_tensor("DTT", [64, 2048], F32R))
        PS = es.enter_context(nc.psum_tensor("PS", [128, 4096], F32))
        sems = {}
        block = None

        bufs = {}

        def B(name):
            if name not in bufs:
                bufs[name] = Buf(name)
            return bufs[name]

        CVo, _co = {}, [0]

        def _cal(nm, n):
            CVo[nm] = _co[0]
            _co[0] += n

        for nm_, n_ in [("maskd0", 128), ("maskd4", 128), ("maskT", 1408),
                        ("identr", 128), ("rperm", 128), ("onesrow", 128),
                        ("oneskr", 1), ("headones", 2), ("xpt", 96 * DT_J)]:
            _cal(nm_, n_)
        assert _co[0] <= 3712, _co[0]

        def cvv(nm, n):
            return CST[:, CVo[nm]:CVo[nm] + n]

        maskd0, maskd4, maskTv = cvv("maskd0", 128), cvv("maskd4", 128), cvv("maskT", 1408)
        identr, rperm = cvv("identr", 128), cvv("rperm", 128)
        onesrow = cvv("onesrow", 128)[0:1, 0:128]
        oneskr, headones = cvv("oneskr", 1), cvv("headones", 2)
        XPT = cvv("xpt", 96 * DT_J)
        identb, oneskb = CSB[:, 0:128], CSB[:, 128:129]

        SVdef = [("negA", 256), ("convb", 16), ("dtb", 16), ("hcarry", 256),
                 ("qs", 2), ("ks", 2), ("lng", 16), ("lnb", 16), ("b2c", 16),
                 ("b1c", 2 * NPAIR * 2), ("altmp", 256), ("sp1", 16), ("xit", 24), ("epsc", 1), ("onesf", 1)]
        so = {nm: sum(n for _, n in SVdef[:i]) for i, (nm, _) in enumerate(SVdef)}
        assert sum(n for _, n in SVdef) <= 1024

        def svv(nm):
            n = dict(SVdef)[nm]
            return SM[:, so[nm]:so[nm] + n]

        def ld(dst, src, writes, reads=(), eng="S"):
            sch.op(eng, (lambda d, s: (lambda E: E.dma_start(out=d, in_=s)))(dst, src),
                   reads=reads, writes=writes, inc=16, dma=True)

        def st(dst, src, reads, writes=(), eng="S"):
            wr = list(writes) if writes else [B("stsink")]
            sch.op(eng, (lambda d, s: (lambda E: E.dma_start(out=d, in_=s)))(dst, src),
                   reads=reads, writes=wr, inc=16, dma=True)

        def ldw(lw2d, src2d, nk, ncc, writes, reads=(), eng="S"):
            ld(lw2d.rearrange("p (k c) -> p k c", k=nk),
               src2d.rearrange("(k p) c -> p k c", p=128), writes, reads, eng)

        def mm(ps_ap, lhsT, rhs, reads, writes, start=True, stop=True, inc=1):
            sch.op("T", (lambda o_, a_, b_, s_, p_:
                         (lambda E: E.matmul(o_, a_, b_, start=s_, stop=p_)))
                   (ps_ap, lhsT, rhs, start, stop), reads=reads, writes=writes, inc=1)

        def act(out_ap, in_ap, fn, reads, writes, bias=0.0, scale=1.0, accum=None):
            sch.op("A", (lambda o_, i_, f_, bi, s_, ac:
                         (lambda E: E.activation(o_, i_, f_, bias=bi, scale=s_,
                                                 accum_out=ac)))
                   (out_ap, in_ap, fn, bias, scale, accum), reads=reads, writes=writes)

        def vtt(out_ap, a_ap, b_ap, op, reads, writes, eng="V"):
            sch.op(eng, (lambda o_, a_, b_, p_: (lambda E: E.tensor_tensor(o_, a_, b_, p_)))
                   (out_ap, a_ap, b_ap, op), reads=reads, writes=writes)

        def vstt(out_ap, in0, scal, in1, op0, op1, reads, writes, eng="V"):
            sch.op(eng, (lambda o_, a_, s_, b_, p0, p1:
                         (lambda E: E.scalar_tensor_tensor(o_, a_, s_, b_, p0, p1)))
                   (out_ap, in0, scal, in1, op0, op1), reads=reads, writes=writes)

        def vts2(out_ap, in0, s1, s2, op0, op1, reads, writes, eng="V"):
            sch.op(eng, (lambda o_, a_, y1, y2, p0, p1:
                         (lambda E: E.tensor_scalar(o_, a_, y1, y2, p0, p1)))
                   (out_ap, in0, s1, s2, op0, op1), reads=reads, writes=writes)

        def vts1(out_ap, in0, s1, op, reads, writes, eng="V"):
            sch.op(eng, (lambda o_, a_, y1, p_:
                         (lambda E: E.tensor_single_scalar(o_, a_, y1, p_)))
                   (out_ap, in0, s1, op), reads=reads, writes=writes)

        def vcopy(out_ap, in_ap, reads, writes, eng="V"):
            sch.op(eng, (lambda o_, i_: (lambda E: E.tensor_copy(o_, i_)))(out_ap, in_ap),
                   reads=reads, writes=writes)

        def vscan(out_ap, a_ap, b_ap, init, reads, writes):
            sch.op("V", (lambda o_, a_, b_, i_:
                         (lambda E: E.tensor_tensor_scan(o_, a_, b_, i_, ALU.mult, ALU.add)))
                   (out_ap, a_ap, b_ap, init), reads=reads, writes=writes)

        def vmax(out_ap, in_ap, reads, writes, negate=False):
            sch.op("V", (lambda o_, i_, n_: (lambda E: E.reduce_max(o_, i_, AX.X, negate=n_)))
                   (out_ap, in_ap, negate), reads=reads, writes=writes)

        def vrecip(out_ap, in_ap, reads, writes):
            sch.op("V", (lambda o_, i_: (lambda E: E.reciprocal(o_, i_)))(out_ap, in_ap),
                   reads=reads, writes=writes)

        def vmemset(ap, v, writes, eng="V"):
            sch.op(eng, (lambda a_, v_: (lambda E: E.memset(a_, v_)))(ap, v), writes=writes)

        def trans(ps_ap, in_ap, id_ap, reads, writes):
            sch.op("T", (lambda o_, i_, d_: (lambda E: E.transpose(o_, i_, d_)))
                   (ps_ap, in_ap, id_ap), reads=reads, writes=writes)

        PB = [B(f"ps{i}") for i in range(8)]

        def psv(bank, n=512, off=0, dt=None, p0=0, pn=128):
            ap = PS[p0:p0 + pn, 512 * bank + off:512 * bank + off + n]
            return ap.bitcast(dt) if dt else ap

        def tapw(nm, src_ap, rbufs):
            if nm in taps:
                want = taps[nm][1]
                if src_ap.dtype != want:
                    src_ap = src_ap.bitcast(want)
                st(tap_out[nm][:, :], src_ap, rbufs, writes=[B("tap" + nm)])

        def arv(off, n, dt=None):
            assert off + n <= 19456, (off, n)
            ap = AR[:, off:off + n]
            return ap.bitcast(dt) if dt else ap

        def abv(off, n, dt=None):
            assert off + n <= 29824, (off, n)
            ap = AB[:, off:off + n]
            return ap.bitcast(dt) if dt else ap

        def axv(off, n):
            assert off + n <= 11328, (off, n)
            return AXS[:, off:off + n]

        def row0(off, n):
            return AR[0:1, off:off + n]

        for nm in ("ident_b", "oneskcol_b", "oneskcol_r", "headones_r", "rperm_r",
                   "mask_d0", "mask_d4", "maskT", "ident_r"):
            tgt = {"ident_b": identb, "oneskcol_b": oneskb, "oneskcol_r": oneskr,
                   "headones_r": headones, "rperm_r": rperm, "mask_d0": maskd0,
                   "mask_d4": maskd4, "maskT": maskTv, "ident_r": identr}[nm]
            ld(tgt, dram[nm][:, :], [B("constD")])
        ld(onesrow, dram["onesrow_r"][0:1, :], [B("constD")])
        ld(svv("convb"), convb2[:, :], [B("constD")])
        ld(svv("dtb"), dtb2[:, :], [B("constD")])
        for l in range(2):
            ld(svv("qs")[:, l:l + 1], qs2[l][:, :], [B("constD")])
            ld(svv("ks")[:, l:l + 1], ks2[l][:, :], [B("constD")])
            ld(svv("lng")[:, 8 * l:8 * l + 8], lng2[l][:, :], [B("constD")])
            ld(svv("lnb")[:, 8 * l:8 * l + 8], lnb2[l][:, :], [B("constD")])
            ld(svv("b2c")[:, 8 * l:8 * l + 8], b2t2[l][:, :], [B("constD")])
            ld(svv("b1c")[:, 2 * NPAIR * l:2 * NPAIR * (l + 1)], b1t2[l][:, :],
               [B("constD")])
        ld(svv("altmp"), alog2[:, :], [B("constD")])
        act(svv("negA"), svv("altmp"), AF.Exp, [B("constD")], [B("negA")])
        vts1(svv("negA"), svv("negA"), -1.0, ALU.mult, [B("negA")], [B("negA")])
        for k in range(DT_J):
            ld(XPT[:, 96 * k:96 * k + 96], xpwT[128 * k:128 * k + 128, :], [B("constD")])
        ld(DTT[:, :], dtwT[:, :], [B("constD")])
        vmemset(svv("hcarry"), 0.0, [B("hcarry")])
        vmemset(svv("epsc"), EPS, [B("epsc")])
        vmemset(svv("onesf"), 1.0, [B("constD")])

        RXb = [B(f"RX{i}") for i in range(16)]
        XIb = [B(f"XI{i}") for i in range(8)]
        rx = lambda i: arv(512 * i, 512)
        xi_t = [abv(520 * j, 520) for j in range(DT_J)]
        y3_t = [abv(512 * j, 512) for j in range(DT_J)]

        XIr = B("XIr")

        def xib(j):
            return XIr

        def y3b(j):
            return XIr

        dtlo = arv(8192, 512)
        dtlo_b = B("dtlo")
        bcT = arv(15360, 512)
        bc_b = B("bcT")
        ddiag_t = arv(18944, 128)
        ddiag_b = B("ddiagb")
        dt_t = [arv(14336 + 512 * i, 512) for i in range(2)]
        dt_b = [B(f"dt{i}") for i in range(2)]
        u_t = [abv(8320 + 512 * i, 512) for i in range(2)]
        u_b = [B(f"u{i}") for i in range(2)]
        a_t = [arv(12800 + 512 * i, 512) for i in range(3)]
        a_b = [B(f"a{i}") for i in range(3)]
        b_t = [axv(8704 + 512 * i, 512).bitcast(F32R) for i in range(3)]
        b_b = [B(f"b{i}") for i in range(3)]
        w_t = [abv(9344 + 512 * i, 512) for i in range(3)]
        w_b = [B(f"w{i}") for i in range(3)]
        cr_t = [abv(17024 + 512 * s, 512) for s in range(DST)]
        cr_b = [B("crD") for s in range(DST)]
        hb_t = [abv(10880 + 512 * i, 512) for i in range(4)]
        hb_b = [B(f"hb{i}") for i in range(4)]
        xlnB = [abv(12928 + 512 * j, 512) for j in range(DJ)]
        xln_b = [B(f"xln{j}") for j in range(DJ)]
        br_t = [axv(512 + 512 * s, 512) for s in range(DST)]
        br_b = [B("brD") for s in range(DST)]
        cdiag_t = abv(29312, 512)
        outm_t = [abv(27264, 2048)]
        outm_b = [[B("outm0")]]

        xch = [arv(8704 + 512 * j, 512) for j in range(DJ)]
        xch_b = [B("xchD") for j in range(DJ)]
        inm_t = [abv(25216 + 1024 * i, 1024) for i in range(2)]
        inm_b = [B(f"inm{i}") for i in range(2)]
        muB, rsB = arv(16896, 512), arv(17408, 512)
        g_t = [arv(17920, 512), arv(18432, 512)]
        sp1, sp2 = arv(15872, 512), arv(16384, 512)
        mrow, vrow, rrow = row0(512 * 9, 512), row0(512 * 10, 512), row0(512 * 11, 512)
        rowb = [RXb[9], RXb[10], RXb[11]]
        rrD = B("rrD")
        rowsBD, rowsCD = B("rowsBD"), B("rowsCD")

        def ln_rows(sumA, sumQ, bA, bQ, n, nrm, slot, dq="S"):
            act(mrow[:, 0:n], sumA, AF.Copy, [bA], [rowb[0]], scale=1.0 / nrm)
            act(vrow[:, 0:n], sumQ, AF.Copy, [bQ], [rowb[1]], scale=1.0 / nrm)
            vtt(rrow[:, 0:n], mrow[:, 0:n], mrow[:, 0:n], ALU.mult, [rowb[0]], [rowb[2]])
            vtt(vrow[:, 0:n], vrow[:, 0:n], rrow[:, 0:n], ALU.subtract,
                [rowb[1], rowb[2]], [rowb[1]])
            act(rrow[:, 0:n], vrow[:, 0:n], AF.Ln, [rowb[1], B("epsc")], [rowb[2]], bias=svv("epsc")[0:1, :])
            act(rrow[:, 0:n], rrow[:, 0:n], AF.Exp, [rowb[2]], [rowb[2]], scale=-0.5)
            st(rr[slot:slot + 1, 0:n], mrow[:, 0:n].bitcast(F32), [rowb[0]],
               writes=[B(f"rrM{slot}")], eng=dq)
            st(rr[slot + 1:slot + 2, 0:n], rrow[:, 0:n].bitcast(F32), [rowb[2]],
               writes=[B(f"rrM{slot}")], eng=dq)

        sch.mark("M")
        mchunks = NCH if not trunc else min(trunc.get("mc", NCH), NCH)
        mstage = 99 if not trunc else trunc.get("ms", 99)
        for c in range(mchunks):
            t0 = CH * c
            sch.curlab = "M1"
            for j in range(DJ):
                ld(xch[j], x_in[128 * j:128 * j + 128, t0:t0 + CH], [xch_b[j]])
            if mstage < 2:
                continue
            sA, sQ = psv(6, CH, p0=0, pn=1), psv(7, CH, p0=0, pn=1)
            for j in range(DJ):
                mm(sA, oneskr, xch[j], [xch_b[j], B("constD")], [PB[6]],
                   start=(j == 0), stop=(j == DJ - 1), inc=(1 if j == DJ - 1 else 0))
            sq1 = rx(8)
            for j in range(DJ):
                act(sq1, xch[j], AF.Square, [xch_b[j]], [RXb[8]])
                mm(sQ, oneskr, sq1, [RXb[8], B("constD")], [PB[7]],
                   start=(j == 0), stop=(j == DJ - 1), inc=(1 if j == DJ - 1 else 0))
            ln_rows(sA, sQ, PB[6], PB[7], CH, D, 0)
            if mstage < 3:
                continue
            ld(muB, rr.bitcast(F32R)[0:1, 0:CH].partition_broadcast(128), [B("muBD")],
               reads=[B("rrM0")])
            ld(rsB, rr.bitcast(F32R)[1:2, 0:CH].partition_broadcast(128), [B("rsBD")],
               reads=[B("rrM0")])
            for j in range(DJ):
                t1 = rx(j)
                vstt(t1, xch[j].bitcast(F32), 1.0, muB, ALU.mult,
                     ALU.subtract, [xch_b[j], B("muBD")], [RXb[j]])
                vtt(xlnB[j], t1.bitcast(F32), rsB, ALU.mult, [RXb[j], B("rsBD")],
                    [xln_b[j]])
            if c == 0:
                tapw("xln", xlnB[0], [xln_b[0]])
            if mstage < 4:
                continue
            sch.curlab = "M2"
            xitv = svv("xit").bitcast(BF16)
            for j in range(DT_J):
                if c == 0:
                    ld(xi_t[j][:, 0:3], dram["zeros_c"][:, 0:3], [xib(j)],
                       eng="P")
                else:
                    vcopy(xi_t[j][:, 0:3], xitv[:, 3 * j:3 * j + 3],
                          [B("xitB")], [xib(j)])
            for m in range(DT_J):
                lw, lwb = inm_t[m % 2], inm_b[m % 2]
                ldw(lw, inwT[:, 128 * m:128 * m + 128], 8, 128, [lwb], eng="P")
                bank = m % 6
                for k in range(DJ):
                    mm(psv(bank), lw[:, 128 * k:128 * k + 128], xlnB[k],
                       [lwb, xln_b[k]], [PB[bank]], start=(k == 0),
                       stop=(k == DJ - 1), inc=(1 if k == DJ - 1 else 0))
                act(xi_t[m][:, 3:515], psv(bank), AF.Copy, [PB[bank]], [xib(m)])
            if mstage < 5:
                continue
            sch.curlab = "M3"
            for j in range(DT_J):
                ld(cdiag_t.rearrange("p (k c) -> p k c", k=DCONV),
                   cdiagd[j].rearrange("k p c -> p k c"), [B("cdiagb")], eng="P")
                bank = j % 2
                for k in range(DCONV):
                    mm(psv(bank), cdiag_t[:, 128 * k:128 * k + 128],
                       xi_t[j][:, k:k + 512], [B("cdiagb"), xib(j)], [PB[bank]],
                       start=(k == 0), stop=(k == DCONV - 1),
                       inc=(1 if k == DCONV - 1 else 0))
                act(rx(j), psv(bank), AF.Silu, [PB[bank]], [RXb[j]],
                    bias=svv("convb")[:, j:j + 1])
                vcopy(xitv[:, 3 * j:3 * j + 3], xi_t[j][:, 512:515], [xib(j)],
                      [B("xitB")])
            if mstage < 6:
                continue
            sch.curlab = "M4"
            xdbl = PS[0:96, 1024:1024 + CH]
            for k in range(DT_J):
                mm(xdbl, XPT[:, 96 * k:96 * k + 96], rx(k), [RXb[k], B("constD")],
                   [PB[2]], start=(k == 0), stop=(k == DT_J - 1),
                   inc=(1 if k == DT_J - 1 else 0))
            act(dtlo[0:64, :], PS[0:64, 1024:1024 + CH], AF.Copy, [PB[2]], [dtlo_b])
            act(bcT[0:32, :], PS[64:96, 1024:1024 + CH], AF.Copy, [PB[2]], [bc_b])
            st(rowsB[c], bcT[0:16, :].bitcast(F32), [bc_b], writes=[rowsBD])
            st(rowsC[c], bcT[16:32, :], [bc_b], writes=[rowsCD], eng="P")
            if c == 0:
                tapw("xc", rx(0), [RXb[0]])
            for s_i in range(DST):
                ld(br_t[s_i], rowsB[c][s_i:s_i + 1, :].partition_broadcast(128),
                   [br_b[s_i]], reads=[rowsBD])
                ld(cr_t[s_i], rowsC[c][s_i:s_i + 1, :].partition_broadcast(128),
                   [cr_b[s_i]], reads=[rowsCD])
            if mstage < 7:
                continue
            sch.curlab = "M5"
            dtiles = DT_J if mstage >= 8 else 1
            for j in range(dtiles):
                dbank = 3 + (j % 2)
                mm(psv(dbank), DTT[:, 128 * j:128 * j + 128], dtlo[0:64, :],
                   [B("constD"), dtlo_b], [PB[dbank]])
                dtj, dtjb = dt_t[j % 2], dt_b[j % 2]
                uj, ujb = u_t[j % 2], u_b[j % 2]
                act(sp1, psv(dbank), AF.Abs, [PB[dbank]], [RXb[14]],
                    bias=svv("dtb")[:, j:j + 1])
                act(sp2, sp1, AF.Exp, [RXb[14]], [RXb[15]], scale=-1.0)
                act(sp1, sp2, AF.Ln, [RXb[15]], [RXb[14]], bias=1.0)
                vts2(sp2, psv(dbank), svv("dtb")[:, j:j + 1], 0.0, ALU.add, ALU.max,
                     [PB[dbank]], [RXb[15]])
                vtt(dtj, sp1, sp2, ALU.add, [RXb[14], RXb[15]], [dtjb])
                vtt(uj, dtj, rx(j).bitcast(F32), ALU.mult, [dtjb, RXb[j]], [ujb])
                if c == 0 and j == 0:
                    tapw("dt0", dtj.bitcast(F32R), [dtjb])
                mz = DT_J + j
                lw, lwb = inm_t[j % 2], inm_b[j % 2]
                ldw(lw, inwT[:, 128 * mz:128 * mz + 128], 8, 128, [lwb], eng="P")
                zbank = 7
                for k in range(DJ):
                    mm(psv(zbank), lw[:, 128 * k:128 * k + 128], xlnB[k],
                       [lwb, xln_b[k]], [PB[zbank]], start=(k == 0),
                       stop=(k == DJ - 1), inc=(1 if k == DJ - 1 else 0))
                gj = g_t[j % 2]
                gjb = B(f"gring{j % 2}")
                act(gj, psv(zbank), AF.Silu, [PB[zbank]], [gjb])
                sch.curlab = "M6scan"
                ybank = 5 + (j % 2)
                for s_i in range(DST):
                    sl = s_i % 3
                    hs = s_i % 4
                    act(a_t[sl], dtj, AF.Exp, [dtjb], [a_b[sl]],
                        scale=svv("negA")[:, DST * j + s_i:DST * j + s_i + 1])
                    vtt(b_t[sl], uj, br_t[s_i], ALU.mult, [ujb, br_b[s_i]],
                        [b_b[sl]], eng="P")
                    vscan(hb_t[hs], a_t[sl], b_t[sl],
                          svv("hcarry")[:, DST * j + s_i:DST * j + s_i + 1],
                          [a_b[sl], b_b[sl], B("hcarry")], [hb_b[hs]])
                    vtt(w_t[sl], hb_t[hs], cr_t[s_i], ALU.mult,
                        [hb_b[hs], cr_b[s_i]], [w_b[sl]])
                    mm(psv(ybank), identb, w_t[sl], [w_b[sl], B("constD")],
                       [PB[ybank]], start=(s_i == 0), stop=False, inc=0)
                    if s_i % 4 == 3:
                        q4 = s_i // 4
                        hc_src = AB[:, 10880:10880 + 2048] \
                            .rearrange("p (i c) -> p i c", c=512)[:, :, 511:512]
                        vcopy(svv("hcarry")[:, DST * j + 4 * q4:
                                            DST * j + 4 * q4 + 4],
                              hc_src, [hb_b[i] for i in range(4)], [B("hcarry")])
                sch.curlab = "M6y"
                ld(ddiag_t, ddiagd[j].bitcast(F32R)[:, :], [ddiag_b])
                mm(psv(ybank), ddiag_t, rx(j), [ddiag_b, RXb[j]], [PB[ybank]],
                   start=False, stop=True, inc=1)
                vtt(y3_t[j], psv(ybank), gj, ALU.mult, [PB[ybank], gjb], [y3b(j)])
            if c == 0:
                tapw("y30", y3_t[0], [y3b(0)])
            if mstage < 9:
                continue
            sch.curlab = "M7"
            for m in range(DJ):
                lw, lwbs = outm_t[0], outm_b[0]
                ldw(lw, outwT[:, 128 * m:128 * m + 128], 16, 128, lwbs, eng="P")
                bank = m % 4
                for k in range(DT_J):
                    mm(psv(bank), lw[:, 128 * k:128 * k + 128], y3_t[k],
                       lwbs + [y3b(k)], [PB[bank]], start=(k == 0),
                       stop=(k == DT_J - 1), inc=(1 if k == DT_J - 1 else 0))
                act(rx(m), psv(bank), AF.Copy, [PB[bank]], [RXb[m]])
            if c == 0:
                tapw("mo0", rx(0), [RXb[0]])
            if mstage < 10:
                continue
            sch.curlab = "M8"
            sA2, sQ2 = psv(6, CH, p0=0, pn=1), psv(7, CH, p0=0, pn=1)
            for j in range(DJ):
                mm(sA2, oneskr, rx(j), [RXb[j], B("constD")], [PB[6]],
                   start=(j == 0), stop=(j == DJ - 1), inc=(1 if j == DJ - 1 else 0))
            sq2 = rx(8)
            for j in range(DJ):
                act(sq2, rx(j), AF.Square, [RXb[j]], [RXb[8]])
                mm(sQ2, oneskr, sq2, [RXb[8], B("constD")], [PB[7]],
                   start=(j == 0), stop=(j == DJ - 1), inc=(1 if j == DJ - 1 else 0))
            ln_rows(sA2, sQ2, PB[6], PB[7], CH, D, 2)
            ld(muB, rr.bitcast(F32R)[2:3, 0:CH].partition_broadcast(128), [B("muBD")],
               reads=[B("rrM2")])
            ld(rsB, rr.bitcast(F32R)[3:4, 0:CH].partition_broadcast(128), [B("rsBD")],
               reads=[B("rrM2")])
            for j in range(DJ):
                ld(xch[j], x_in[128 * j:128 * j + 128, t0:t0 + CH], [xch_b[j]])
            for j in range(DJ):
                tmp = axv(0, 512)
                vstt(tmp, rx(j).bitcast(F32), 1.0, muB, ALU.mult, ALU.subtract,
                     [RXb[j], B("muBD")], [XIr])
                vtt(rx(j), tmp, rsB, ALU.mult, [XIr, B("rsBD")],
                    [RXb[j]])
                vtt(tmp, rx(j).bitcast(F32), xch[j].bitcast(F32), ALU.add,
                    [RXb[j], xch_b[j]], [XIr])
                st(x1s[128 * j:128 * j + 128, t0:t0 + CH], tmp.bitcast(F32),
                   [XIr], writes=[B("x1sD")])
                if c == 0 and j == 0:
                    tapw("x10", tmp.bitcast(F32R), [XIr])

        sch.barrier()

        AXF, AQT, AKT, ASC3 = 0, 4160, 6208, 10304
        ANR, ASQ, AAT = 11840, 12352, 12872
        ALW1, ALW2, ANG = 14920, 16968, 17992
        ATQ, AVB, AVT, AES = 0, 8320, 10368, 14464
        XAX1, XART, XAMU, XARW = 0, 4096, 6144, 7184
        XAN2, XAN2C, XARB2 = 8744, 9256, 9768
        XARC, XADN = 10280, 10792

        def ffn_attn(l, srcdram, dstdram):
            x1t = [axv(XAX1 + 512 * j, 512).bitcast(F32R) for j in range(DJ)]
            x1b = [B("ax1D") for j in range(DJ)]
            xft = [arv(AXF + 520 * j, 520) for j in range(DJ)]
            xfb = [B("axfD") for j in range(DJ)]
            tt_ = [abv(ATQ + 1040 * j, 1040, F32R) for j in range(DJ)]
            ttb = [B(f"a_t{j}") for j in range(DJ)]
            sqa = arv(ASQ, 520)
            sqab = B("a_sq")
            qt = [arv(AQT + 512 * m, 512) for m in range(4)]
            qtb = [B(f"a_q{m}") for m in range(4)]
            kt = [[arv(AKT + 2048 * sl + 512 * m, 512) for m in range(4)]
                  for sl in range(2)]
            ktb = [[B(f"a{l}k{sl}{m}") for m in range(4)] for sl in range(2)]
            scr = [arv(ASC3 + 512 * i, 512) for i in range(3)]
            scrb = [B(f"a_scr{i}") for i in range(3)]
            sqnt = [arv(ANR, 512), arv(ASQ, 512)]
            sqnbt = [B("a_sqn"), sqab]
            rstdBt = [axv(XARB2, 512).bitcast(F32R), axv(XAN2, 512).bitcast(F32R)]
            rstdBbt = [B("a_rstdB0"), B("a_rstdB1")]
            vb = [abv(AVB + 512 * m, 512) for m in range(4)]
            vbb = [B(f"a_v{m}") for m in range(4)]
            vT = [[abv(AVT + 2048 * sl + 64 * i, 64) for i in range(32)]
                  for sl in range(2)]
            vTb = [[B(f"a{l}vT{sl}{i}") for i in range(32)] for sl in range(2)]
            eSTt = [[abv(AES + 512 * (8 * par + m), 512) for m in range(8)]
                    for par in range(2)]
            eSTbt = [[B(f"a_e{8 * par + m}") for m in range(8)] for par in range(2)]
            negrowt = [arv(ANG, 512), arv(ANG + 512, 512)]
            negrowbt = [B("a_negrow0"), B("a_negrow1")]
            dnrowt = [AXS[0:1, XADN:XADN + 512].bitcast(F32R),
                      AXS[0:1, XARW:XARW + 512].bitcast(F32R)]
            dnrowbt = [B("a_n3"), B("a_mr")]
            recipBt = [axv(XARC, 512).bitcast(F32R)[0:64, :],
                       axv(XAN2C, 512).bitcast(F32R)[0:64, :]]
            recipBbt = [B("a_recipB0"), B("a_recipB1")]
            att = [arv(AAT + 512 * m, 512) for m in range(4)]
            attb = [B(f"a_att{m}") for m in range(4)]
            xmid = [axv(XAX1 + 512 * j, 512).bitcast(F32R) for j in range(DJ)]
            xmidb = [B("ax1D") for j in range(DJ)]
            rtab = {nm: axv(XART + 512 * i, 512).bitcast(F32R)
                    for i, nm in enumerate(("csq", "ssq", "csk", "ssk"))}
            rtabb = {nm: B("artD") for nm in rtab}
            qlw = [arv(ALW1 + 1024 * i, 1024) for i in range(2)]
            qlwb = [B(f"a_qlw{i}") for i in range(2)]
            wlw = [arv(ALW2 + 512 * i, 512) for i in range(2)]
            wlwb = [B(f"a_wlw{i}") for i in range(2)]
            muBa, rsBa = axv(XAMU, 520).bitcast(F32R), axv(XAMU + 520, 520).bitcast(F32R)
            muBab, rsBab = B("a_muB"), B("a_rsB")
            mrA = AXS[0:1, XARW:XARW + 520].bitcast(F32R)
            vrA = AXS[0:1, XARW + 520:XARW + 1040].bitcast(F32R)
            rrA = AXS[0:1, XARW + 1040:XARW + 1560].bitcast(F32R)
            rowAb = [B("a_mr"), B("a_vr"), B("a_rr")]
            nrm2t = [AXS[0:2, XARW + 520:XARW + 1032].bitcast(F32R),
                     AXS[0:2, XARW + 1040:XARW + 1552].bitcast(F32R)]
            nrm2bt = [B("a_vr"), B("a_rr")]
            nrm2ct = [AXS[0:2, XARW:XARW + 512].bitcast(F32R),
                      AXS[0:2, XADN:XADN + 512].bitcast(F32R)]
            nrm2cbt = [B("a_mr"), B("a_n3")]

            for c in range(NCH):
                sch.curlab = "Aload"
                t0 = CH * c
                sl_c, sl_p = c % 2, (c + 1) % 2
                n1 = 513
                if l == 0:
                    for j in range(DJ):
                        ld(x1t[j][:, 0:512],
                           srcdram.bitcast(F32R)[128 * j:128 * j + 128, t0:t0 + CH],
                           [x1b[j]], reads=[B("x1sD")])
                else:
                    for j in range(DJ):
                        if c == 0:
                            ld(xft[j][:, 0:1],
                               dram["zeros_c"].bitcast(F32R)[:, 0:1], [xfb[j]])
                            ld(xft[j][:, 1:513],
                               srcdram.bitcast(F32R)[128 * j:128 * j + 128,
                                                     t0:t0 + CH],
                               [xfb[j]], reads=[B("xl0D")])
                        else:
                            ld(xft[j][:, 0:513],
                               srcdram.bitcast(F32R)[128 * j:128 * j + 128,
                                                     t0 - 1:t0 + CH],
                               [xfb[j]], reads=[B("xl0D")])
                if l == 0:
                    sch.curlab = "Aln"
                    for j in range(DJ):
                        if c == 0:
                            ld(xft[j][:, 0:1],
                               dram["zeros_c"].bitcast(F32R)[:, 0:1], [xfb[j]])
                        else:
                            vcopy(xft[j][:, 0:1], xft[j][:, 512:513], [xfb[j]],
                                  [xfb[j]])
                    sA = psv(6, CH, p0=0, pn=1)
                    sQ = psv(7, CH, p0=0, pn=1)
                    for j in range(DJ):
                        mm(sA, svv("onesf"), x1t[j][:, 0:512].bitcast(F32),
                           [x1b[j], B("constD")],
                           [PB[6]], start=(j == 0), stop=(j == DJ - 1),
                           inc=(1 if j == DJ - 1 else 0))
                    for j in range(DJ):
                        act(sqa[:, 0:512], x1t[j][:, 0:512], AF.Square, [x1b[j]],
                            [sqab])
                        mm(sQ, oneskr, sqa[:, 0:512], [sqab, B("constD")],
                           [PB[7]], start=(j == 0), stop=(j == DJ - 1),
                           inc=(1 if j == DJ - 1 else 0))
                    act(mrA[:, 0:512], sA, AF.Copy, [PB[6]], [rowAb[0]], scale=1.0 / D)
                    act(vrA[:, 0:512], sQ, AF.Copy, [PB[7]], [rowAb[1]], scale=1.0 / D)
                    vtt(rrA[:, 0:512], mrA[:, 0:512], mrA[:, 0:512], ALU.mult,
                        [rowAb[0]], [rowAb[2]])
                    vtt(vrA[:, 0:512], vrA[:, 0:512], rrA[:, 0:512], ALU.subtract,
                        [rowAb[1], rowAb[2]], [rowAb[1]])
                    act(rrA[:, 0:512], vrA[:, 0:512], AF.Ln,
                        [rowAb[1], B("epsc")], [rowAb[2]],
                        bias=svv("epsc")[0:1, :])
                    act(rrA[:, 0:512], rrA[:, 0:512], AF.Exp, [rowAb[2]], [rowAb[2]],
                        scale=-0.5)
                    st(rr[4:5, 0:512], mrA[:, 0:512].bitcast(F32), [rowAb[0]],
                       writes=[B("rrA45")], eng="P")
                    st(rr[5:6, 0:512], rrA[:, 0:512].bitcast(F32), [rowAb[2]],
                       writes=[B("rrA45")], eng="P")
                    ld(muBa[:, 0:512], rr.bitcast(F32R)[4:5, 0:512].partition_broadcast(128),
                       [muBab], reads=[B("rrA45")], eng="P")
                    ld(rsBa[:, 0:512], rr.bitcast(F32R)[5:6, 0:512].partition_broadcast(128),
                       [rsBab], reads=[B("rrA45")], eng="P")
                    for j in range(DJ):
                        vstt(tt_[j][:, 0:512], x1t[j][:, 0:512].bitcast(F32), 1.0,
                             muBa[:, 0:512], ALU.mult, ALU.subtract,
                             [x1b[j], muBab], [ttb[j]])
                        vtt(xft[j][:, 1:513], tt_[j][:, 0:512],
                            rsBa[:, 0:512], ALU.mult, [ttb[j], rsBab], [xfb[j]])
                if c == 0 and l == 0:
                    tapw("xf0", xft[0][:, 1:513], [xfb[0]])
                sch.curlab = "Aqkv"
                for nm in rtab:
                    ld(rtab[nm], dram[nm].bitcast(F32R)[:, t0:t0 + CH], [rtabb[nm]],
                       eng="P")
                for m in range(12):
                    lw, lwb = qlw[m % 2], qlwb[m % 2]
                    ldw(lw, wqkv[l][:, 128 * m:128 * m + 128], 8, 128, [lwb])
                    bank = m % 5
                    for k in range(DJ):
                        mv_ap = xft[k][:, 1:513] if k < 4 else xft[k][:, 0:512]
                        mm(psv(bank), lw[:, 128 * k:128 * k + 128], mv_ap,
                           [lwb, xfb[k]], [PB[bank]], start=(k == 0),
                           stop=(k == DJ - 1), inc=(1 if k == DJ - 1 else 0))
                    if m < 8:
                        isq = m < 4
                        mt = m if isq else m - 4
                        dstt = qt[mt] if isq else kt[sl_c][mt]
                        dsttb = qtb[mt] if isq else ktb[sl_c][mt]
                        cs_t = rtab["csq" if isq else "csk"]
                        ss_t = rtab["ssq" if isq else "ssk"]
                        csb_ = rtabb["csq" if isq else "csk"]
                        ssb_ = rtabb["ssq" if isq else "ssk"]
                        sccol = svv("qs" if isq else "ks")[:, l:l + 1]
                        scbuf = B("c_qs" if isq else "c_ks")
                        par = m % 2
                        sqn, sqnb = sqnt[par], sqnbt[par]
                        nrm2, nrm2b = nrm2t[par], nrm2bt[par]
                        nrm2c, nrm2cb = nrm2ct[par], nrm2cbt[par]
                        rstdB, rstdBb = rstdBt[par], rstdBbt[par]
                        act(sqn, psv(bank), AF.Square, [PB[bank]], [sqnb])
                        nps = PS[0:2, 512 * (6 + par):512 * (6 + par) + 512]
                        mm(nps, headones, sqn, [sqnb, B("constD")], [PB[6 + par]])
                        act(nrm2, nps, AF.Sqrt, [PB[6 + par]], [nrm2b])
                        vts1(nrm2, nrm2, 1e-12, ALU.max, [nrm2b], [nrm2b])
                        vrecip(nrm2c, nrm2, [nrm2b], [nrm2cb])
                        slot = (6 + 2 * m) if isq else (14 + 2 * mt)
                        st(rr[slot:slot + 2, 0:512], nrm2c.bitcast(F32), [nrm2cb],
                           writes=[B(f"rrn{m}")], eng="P")
                        ld(rstdB[0:64, :], rr.bitcast(F32R)[slot:slot + 1, 0:512]
                           .partition_broadcast(64), [rstdBb], reads=[B(f"rrn{m}")],
                           eng="P")
                        ld(rstdB[64:128, :],
                           rr.bitcast(F32R)[slot + 1:slot + 2, 0:512]
                           .partition_broadcast(64), [rstdBb],
                           reads=[rstdBb, B(f"rrn{m}")], eng="P")
                        vstt(scr[0], psv(bank), sccol, rstdB, ALU.mult, ALU.mult,
                             [PB[bank], rstdBb, scbuf], [scrb[0]])
                        mm(psv(7), rperm, scr[0], [scrb[0], B("constD")], [PB[7]])
                        vstt(scr[1], psv(7), 1.0, ss_t, ALU.mult, ALU.mult,
                             [PB[7], ssb_], [scrb[1]])
                        vtt(scr[2], scr[0].bitcast(F32), cs_t, ALU.mult,
                            [scrb[0], csb_], [scrb[2]])
                        vtt(dstt, scr[2].bitcast(F32),
                            scr[1].bitcast(F32), ALU.add, [scrb[1], scrb[2]], [dsttb])
                    else:
                        mt = m - 8
                        act(vb[mt], psv(bank), AF.Copy, [PB[bank]], [vbb[mt]])
                        for hh in range(2):
                            h_ = 2 * mt + hh
                            for ts in range(4):
                                pv = psv(7, 32, 0, BF16)
                                trans(pv, vb[mt][64 * hh:64 * hh + 64,
                                                 128 * ts:128 * ts + 128],
                                      identb[64 * hh:64 * hh + 64,
                                             64 * hh:64 * hh + 64],
                                      [vbb[mt], B("constD")], [PB[7]])
                                act(vT[sl_c][4 * h_ + ts], pv, AF.Copy, [PB[7]],
                                    [vTb[sl_c][4 * h_ + ts]])
                if c == 0 and l == 0:
                    tapw("qt0", qt[0], [qtb[0]])
                    tapw("kt0", kt[sl_c][0], [ktb[sl_c][0]])
                for h_ in range(NH):
                    sch.curlab = "ArowS"
                    mt, hh = h_ // 2, h_ % 2
                    hp = h_ % 2
                    eST, eSTb = eSTt[hp], eSTbt[hp]
                    negrow, negrowb = negrowt[hp], negrowbt[hp]
                    dnrow, dnrowb = dnrowt[hp], dnrowbt[hp]
                    recipB, recipBb = recipBt[hp], recipBbt[hp]
                    qsl = qt[mt][64 * hh:64 * hh + 64, :]

                    def kloc(i):
                        sl = sl_p if i < 4 else sl_c
                        ii = i % 4
                        return (kt[sl][mt][64 * hh:64 * hh + 64,
                                           128 * ii:128 * ii + 128], ktb[sl][mt])

                    def vloc(i):
                        if i < 4:
                            return vT[sl_p][4 * h_ + i], vTb[sl_p][4 * h_ + i]
                        return vT[sl_c][4 * h_ + i - 4], vTb[sl_c][4 * h_ + i - 4]

                    for a_i in range(4):
                        dlist = list(range(0, 5)) if c > 0 else list(range(4 - a_i, 5))
                        for d_ in dlist:
                            kap, kbf = kloc(a_i + d_)
                            needmask = (d_ == 0 and c > 0) or (d_ == 4)
                            mm(PS[:, 128 * d_:128 * d_ + 128],
                               qsl[:, 128 * a_i:128 * a_i + 128], kap,
                               [qtb[mt], kbf], [PB[0], PB[1]], start=True,
                               stop=not needmask, inc=(0 if needmask else 0))
                            if d_ == 0 and c > 0:
                                mm(PS[:, 0:128], identr, maskd0,
                                   [B("constD"), B("constD")], [PB[0], PB[1]],
                                   start=False, stop=True, inc=0)
                            if d_ == 4:
                                mm(PS[:, 512:640], identr, maskd4,
                                   [B("constD"), B("constD")], [PB[0], PB[1]],
                                   start=False, stop=True,
                                   inc=1)
                        dmin = dlist[0]
                        negm = svv("sp1")[:, a_i:a_i + 1]
                        negm2 = svv("sp1")[:, 8 + a_i:9 + a_i]
                        if dmin < 4:
                            vmax(negm, PS[:, 128 * dmin:512], [PB[0], PB[1]],
                                 [B(f"a_ng{a_i}")], negate=True)
                            vmax(negm2, PS[:, 512:640], [PB[0], PB[1]],
                                 [B(f"a_ng2{a_i}")], negate=True)
                            vtt(negm, negm, negm2, ALU.min,
                                [B(f"a_ng{a_i}"), B(f"a_ng2{a_i}")],
                                [B(f"a_ng{a_i}")])
                        else:
                            vmax(negm, PS[:, 512:640], [PB[0], PB[1]],
                                 [B(f"a_ng{a_i}")], negate=True)
                        st(rr[30 + h_:31 + h_, 128 * a_i:128 * a_i + 128], negm,
                           [B(f"a_ng{a_i}")], writes=[B(f"rrg{h_}")], eng="P")
                    sch.curlab = "AsT"
                    ld(negrow[0:1, :], rr.bitcast(F32R)[30 + h_:31 + h_, 0:512],
                       [negrowb], reads=[B(f"rrg{h_}")], eng="P")
                    if c == 0 and l == 0 and h_ == 0:
                        tapw("ng0", negrow[0:1, :], [negrowb])
                    m2list = list(range(8)) if c > 0 else list(range(4, 8))
                    for gi in range(0, len(m2list), 4):
                        for m2 in m2list[gi:gi + 4]:
                            bank = 2 + (m2 % 4)
                            kap, kbf = kloc(m2)
                            mm(psv(bank), kap, qsl, [kbf, qtb[mt]],
                               [PB[bank]], start=True, stop=False, inc=0)
                            off = 896 - 128 * m2
                            mm(psv(bank), identr, maskTv[:, off:off + 512],
                               [B("constD"), B("constD")], [PB[bank]],
                               start=False, stop=False, inc=0)
                            mm(psv(bank), onesrow, negrow[0:1, :],
                               [negrowb, B("constD")], [PB[bank]],
                               start=False, stop=True, inc=1)
                            act(eST[m2], psv(bank), AF.Exp, [PB[bank]], [eSTb[m2]])
                            if c == 0 and l == 0 and h_ == 0 and m2 == 4:
                                tapw("es40", eST[4], [eSTb[4]])
                    sch.curlab = "Adn"
                    dn = PS[0:1, 512 * 6:512 * 6 + 512]
                    for i, m2 in enumerate(m2list):
                        mm(dn, oneskb, eST[m2], [eSTb[m2], B("constD")],
                           [PB[6]], start=(i == 0), stop=(i == len(m2list) - 1),
                           inc=(1 if i == len(m2list) - 1 else 0))
                    vrecip(dnrow, dn, [PB[6]], [dnrowb])
                    st(rr[22 + h_:23 + h_, 0:512], dnrow.bitcast(F32), [dnrowb],
                       writes=[B(f"rrd{h_}")], eng="P")
                    ld(recipB, rr.bitcast(F32R)[22 + h_:23 + h_, 0:512].partition_broadcast(64),
                       [recipBb], reads=[B(f"rrd{h_}")], eng="P")
                    sch.curlab = "Aav"
                    ao = PS[0:64, 512 * 7:512 * 7 + 512]
                    for i, m2 in enumerate(m2list):
                        vap, vbf = vloc(m2)
                        mm(ao, vap, eST[m2], [vbf, eSTb[m2]], [PB[7]],
                           start=(i == 0), stop=(i == len(m2list) - 1),
                           inc=(1 if i == len(m2list) - 1 else 0))
                    vtt(att[mt][64 * hh:64 * hh + 64, :], ao, recipB,
                        ALU.mult, [PB[7], recipBb], [attb[mt]])
                    if c == 0 and l == 0 and h_ == 1:
                        tapw("att00", att[0], [attb[0]])
                sch.curlab = "Awo"
                for m in range(DJ):
                    lw, lwb = wlw[m % 2], wlwb[m % 2]
                    ldw(lw, wod[l][:, 128 * m:128 * m + 128], 4, 128, [lwb])
                    bank = m % 4
                    for k in range(4):
                        mm(psv(bank), lw[:, 128 * k:128 * k + 128], att[k],
                           [lwb, attb[k]], [PB[bank]], start=(k == 0), stop=(k == 3),
                           inc=(1 if k == 3 else 0))
                    vstt(xmid[m], psv(bank), 1.0, xft[m][:, 1:513].bitcast(F32),
                         ALU.mult, ALU.add, [PB[bank], xfb[m]], [xmidb[m]])
                    st(dstdram[128 * m:128 * m + 128, t0:t0 + CH],
                       xmid[m].bitcast(F32), [xmidb[m]], writes=[B(f"xm{l}D")])
                if c == 0 and l == 0:
                    tapw("xm00", xmid[0], [xmidb[0]])

        BTT, BW1, BHN, BMU2 = 0, 8192, 9216, 17408
        BPR, BAG, BW2 = 0, 22528, 24576
        XBXM, XBRW = 0, 8256

        def ffn_geglu(l, srcdram, dstdram, final=False):
            xmt = [axv(XBXM + 1032 * j, 1032).bitcast(F32R) for j in range(DJ)]
            xmb = [B("bxmD") for j in range(DJ)]
            hnt = [arv(BHN + 1024 * j, 1024) for j in range(DJ)]
            hnb = [B(f"b_hn{j}") for j in range(DJ)]
            ttt = [arv(BTT + 1024 * j, 1024) for j in range(DJ)]
            ttb2 = [B(f"b_tt{j}") for j in range(DJ)]
            sqbb2 = ttb2[0]
            sqb = arv(BTT, 1024)
            sqbb = sqbb2
            prod = [abv(BPR + 1024 * p, 1024) for p in range(NPAIR)]
            prodb = [B(f"b_pr{p}") for p in range(NPAIR)]
            aev = abv(BAG, 1024)
            aevb = B("b_aev")
            gev = abv(BAG + 1024, 1024)
            gevb = B("b_gev")
            muBc = arv(BMU2, 1024)
            rsBc = arv(BMU2 + 1024, 1024)
            muBcb, rsBcb = B("b_muB"), B("b_rsB")
            w1lw = arv(BW1, 1024)
            w1lwb = B("b_w1lw")
            w2lw = abv(BW2, 2816)
            w2lwb = B("b_w2lw")
            mrB = AXS[0:1, XBRW:XBRW + 1024].bitcast(F32R)
            vrB = AXS[0:1, XBRW + 1024:XBRW + 2048].bitcast(F32R)
            rrB = AXS[0:1, XBRW + 2048:XBRW + 3072].bitcast(F32R)
            rowBb = [B("b_mr"), B("b_vr"), B("b_rr")]
            xl_t = hnt
            xl_b = hnb
            srcD = B(f"xm{l}D")
            dstD = B("xl0D")

            for s_ in range(NSC):
                sch.curlab = "Bload"
                t0 = SC * s_
                for j in range(DJ):
                    if s_ == 0:
                        ld(xmt[j][:, 0:1],
                           dram["zeros_c"].bitcast(F32R)[:, 0:1], [xmb[j]])
                        ld(xmt[j][:, 1:1025],
                           srcdram.bitcast(F32R)[128 * j:128 * j + 128, t0:t0 + SC],
                           [xmb[j]], reads=[srcD])
                    else:
                        ld(xmt[j][:, 0:1025],
                           srcdram.bitcast(F32R)[128 * j:128 * j + 128,
                                                 t0 - 1:t0 + SC],
                           [xmb[j]], reads=[srcD])

                def hap(j):
                    return xmt[j][:, 1:1025] if j < 4 else xmt[j][:, 0:1024]

                sch.curlab = "Bstat"
                for half in range(2):
                    cS = slice(512 * half, 512 * half + 512)
                    sA = PS[0:1, 512 * (4 + half):512 * (4 + half) + 512]
                    sQ = PS[0:1, 512 * (6 + half):512 * (6 + half) + 512]
                    for j in range(DJ):
                        mm(sA, svv("onesf"), hap(j)[:, cS].bitcast(F32),
                           [xmb[j], B("constD")],
                           [PB[4 + half]], start=(j == 0), stop=(j == DJ - 1),
                           inc=(1 if j == DJ - 1 else 0))
                    for j in range(DJ):
                        act(sqb[:, cS], hap(j)[:, cS], AF.Square, [xmb[j]], [sqbb])
                        mm(sQ, oneskr, sqb[:, cS], [sqbb, B("constD")],
                           [PB[6 + half]], start=(j == 0), stop=(j == DJ - 1),
                           inc=(1 if j == DJ - 1 else 0))
                    act(mrB[:, cS], sA, AF.Copy, [PB[4 + half]], [rowBb[0]],
                        scale=1.0 / D)
                    act(vrB[:, cS], sQ, AF.Copy, [PB[6 + half]], [rowBb[1]],
                        scale=1.0 / D)
                vtt(rrB, mrB, mrB, ALU.mult, [rowBb[0]], [rowBb[2]])
                vtt(vrB, vrB, rrB, ALU.subtract, [rowBb[1], rowBb[2]], [rowBb[1]])
                act(rrB, vrB, AF.Ln, [rowBb[1], B("epsc")], [rowBb[2]], bias=svv("epsc")[0:1, :])
                act(rrB, rrB, AF.Exp, [rowBb[2]], [rowBb[2]], scale=-0.5)
                st(rr[38:39, 0:SC], mrB.bitcast(F32), [rowBb[0]],
                   writes=[B("rrB38")], eng="P")
                st(rr[39:40, 0:SC], rrB.bitcast(F32), [rowBb[2]],
                   writes=[B("rrB38")], eng="P")
                ld(muBc, rr.bitcast(F32R)[38:39, 0:SC].partition_broadcast(128), [muBcb],
                   reads=[B("rrB38")], eng="P")
                ld(rsBc, rr.bitcast(F32R)[39:40, 0:SC].partition_broadcast(128), [rsBcb],
                   reads=[B("rrB38")], eng="P")
                sch.curlab = "Bln"
                for j in range(DJ):
                    vstt(ttt[j], hap(j).bitcast(F32), 1.0, muBc,
                         ALU.mult, ALU.subtract, [xmb[j], muBcb], [ttb2[j]])
                    vtt(hnt[j], ttt[j].bitcast(F32), rsBc, ALU.mult,
                        [ttb2[j], rsBcb], [hnb[j]])
                    vts2(ttt[j], hnt[j].bitcast(F32),
                         svv("lng")[:, 8 * l + j:8 * l + j + 1],
                         svv("lnb")[:, 8 * l + j:8 * l + j + 1], ALU.mult, ALU.add,
                         [hnb[j], B("constD"), B("constD")], [ttb2[j]])
                if s_ == 0 and l == 0:
                    tapw("hn0", ttt[0][:, 0:512], [ttb2[0]])
                sch.curlab = "Bw1"
                coffs = []
                off_ = 0
                for p in range(NPAIR):
                    wdt = min(128, HID - 128 * p)
                    coffs.append((off_, wdt))
                    off_ += 2 * wdt
                for p in range(NPAIR):
                    coff, wdt = coffs[p]
                    for ag in range(2):
                        ldw(w1lw[:, 0:8 * wdt],
                            w1p[l][:, coff + ag * wdt:coff + ag * wdt + wdt],
                            8, wdt, [w1lwb])
                        for half in range(2):
                            bank = 2 * ag + half
                            cS = slice(512 * half, 512 * half + 512)
                            for k in range(DJ):
                                mm(PS[0:wdt, 512 * bank:512 * bank + 512],
                                   w1lw[:, wdt * k:wdt * k + wdt], ttt[k][:, cS],
                                   [w1lwb, ttb2[k]], [PB[bank]], start=(k == 0),
                                   stop=(k == DJ - 1), inc=(1 if k == DJ - 1 else 0))
                    bca = svv("b1c")[0:wdt, 2 * NPAIR * l + 2 * p:
                                     2 * NPAIR * l + 2 * p + 1]
                    bcg = svv("b1c")[0:wdt, 2 * NPAIR * l + 2 * p + 1:
                                     2 * NPAIR * l + 2 * p + 2]
                    for half in range(2):
                        cS = slice(512 * half, 512 * half + 512)
                        act(aev[0:wdt, cS], PS[0:wdt, 512 * half:512 * half + 512],
                            AF.Copy, [PB[half]], [aevb])
                        act(gev[0:wdt, cS],
                            PS[0:wdt, 1024 + 512 * half:1024 + 512 * half + 512],
                            AF.Gelu, [PB[2 + half]], [gevb], bias=bcg)
                    vstt(prod[p][0:wdt, :], aev[0:wdt, :], bca, gev[0:wdt, :],
                         ALU.add, ALU.mult, [aevb, gevb, B("constD")], [prodb[p]])
                sch.curlab = "Bw2"
                for m in range(DJ):
                    ldw(w2lw[:, 0:21 * 128], w2d[l][0:2688, 128 * m:128 * m + 128],
                        21, 128, [w2lwb], eng="P")
                    ld(w2lw[0:RAG, 21 * 128:22 * 128],
                       w2d[l][2688:2730, 128 * m:128 * m + 128], [w2lwb],
                       reads=[w2lwb], eng="P")
                    for half in range(2):
                        bank = 4 + 2 * (m % 2) + half
                        cS = slice(512 * half, 512 * half + 512)
                        for kp in range(NPAIR):
                            wdt = min(128, HID - 128 * kp)
                            mm(psv(bank), w2lw[0:wdt, 128 * kp:128 * kp + 128],
                               prod[kp][0:wdt, cS], [w2lwb, prodb[kp]], [PB[bank]],
                               start=(kp == 0), stop=(kp == NPAIR - 1),
                               inc=(1 if kp == NPAIR - 1 else 0))
                        vstt(xl_t[m][:, cS], psv(bank),
                             svv("b2c")[:, 8 * l + m:8 * l + m + 1],
                             xmt[m][:, 1:1025].bitcast(F32)[:, cS], ALU.add, ALU.add,
                             [PB[bank], xmb[m], B("constD")], [xl_b[m]])
                    if not final:
                        st(dstdram[128 * m:128 * m + 128, t0:t0 + SC],
                           xl_t[m].bitcast(F32), [xl_b[m]], writes=[dstD])
                if s_ == 0 and l == 0:
                    tapw("xl00", xl_t[0].bitcast(F32)[:, 0:512], [xl_b[0]])
                if final:
                    sch.curlab = "Bfin"
                    x1f = [axv(XBXM + 1024 * j, 1024).bitcast(F32R) for j in range(DJ)]
                    x1fb = [B("bxmD") for j in range(DJ)]
                    for j in range(DJ):
                        ld(x1f[j], x1s.bitcast(F32R)[128 * j:128 * j + 128,
                                                         t0:t0 + SC],
                           [B("bxmD")], reads=[B("x1sD")])
                    for half in range(2):
                        cS = slice(512 * half, 512 * half + 512)
                        sA = PS[0:1, 512 * half:512 * half + 512]
                        sQ = PS[0:1, 1024 + 512 * half:1024 + 512 * half + 512]
                        for j in range(DJ):
                            mm(sA, oneskr, xl_t[j][:, cS], [xl_b[j], B("constD")],
                               [PB[half]], start=(j == 0), stop=(j == DJ - 1),
                               inc=(1 if j == DJ - 1 else 0))
                        for j in range(DJ):
                            act(sqb[:, cS], xl_t[j][:, cS], AF.Square, [xl_b[j]],
                                [sqbb])
                            mm(sQ, oneskr, sqb[:, cS], [sqbb, B("constD")],
                               [PB[2 + half]], start=(j == 0), stop=(j == DJ - 1),
                               inc=(1 if j == DJ - 1 else 0))
                        act(mrB[:, cS], sA, AF.Copy, [PB[half]], [rowBb[0]],
                            scale=1.0 / D)
                        act(vrB[:, cS], sQ, AF.Copy, [PB[2 + half]], [rowBb[1]],
                            scale=1.0 / D)
                    vtt(rrB, mrB, mrB, ALU.mult, [rowBb[0]], [rowBb[2]])
                    vtt(vrB, vrB, rrB, ALU.subtract, [rowBb[1], rowBb[2]], [rowBb[1]])
                    act(rrB, vrB, AF.Ln, [rowBb[1], B("epsc")], [rowBb[2]], bias=svv("epsc")[0:1, :])
                    act(rrB, rrB, AF.Exp, [rowBb[2]], [rowBb[2]], scale=-0.5)
                    st(rr[40:41, 0:SC], mrB.bitcast(F32), [rowBb[0]],
                       writes=[B("rrB40")], eng="P")
                    st(rr[41:42, 0:SC], rrB.bitcast(F32), [rowBb[2]],
                       writes=[B("rrB40")], eng="P")
                    ld(muBc, rr.bitcast(F32R)[40:41, 0:SC].partition_broadcast(128), [muBcb],
                       reads=[B("rrB40")], eng="P")
                    ld(rsBc, rr.bitcast(F32R)[41:42, 0:SC].partition_broadcast(128), [rsBcb],
                       reads=[B("rrB40")], eng="P")
                    tf = abv(BAG, 2048, F32R)
                    tfb = aevb
                    o1 = arv(BW1, 1024)
                    o1b = w1lwb
                    ot = axv(XBRW, 1024).bitcast(F32R)
                    otb = rowBb[0]
                    for j in range(DJ):
                        vstt(tf, xl_t[j].bitcast(F32), 1.0, muBc, ALU.mult,
                             ALU.subtract, [xl_b[j], muBcb], [tfb, gevb])
                        vtt(o1, tf, rsBc, ALU.mult, [tfb, rsBcb], [o1b])
                        vtt(ot, o1, x1f[j].bitcast(F32), ALU.add,
                            [o1b, x1fb[j]], [otb])
                        st(out[128 * j:128 * j + 128, t0:t0 + SC],
                           ot.bitcast(F32), [otb], writes=[B("outD")])

        ph = trunc.get("ph", 5) if trunc else 5
        if ph >= 1:
            sch.mark("A1")
            ffn_attn(0, x1s, xms[0])
            sch.barrier()
        if ph >= 2:
            sch.mark("B1")
            ffn_geglu(0, xms[0], xl0)
            sch.barrier()
        if ph >= 3:
            sch.mark("A2")
            ffn_attn(1, xl0, xms[1])
            sch.barrier()
        if ph >= 4:
            sch.mark("B2")
            ffn_geglu(1, xms[1], None, final=True)
            sch.barrier()
        for sem, v in sch.cnt.items():
            if v > 0:
                sch.items["S"].append(("w", sem, v))
        global _LAST_COUNTS, _LAST_LABS
        _LAST_COUNTS = dict(sch.cnt)
        _LAST_LABS = {e: list(v) for e, v in sch.labs.items()}

        for semname, v in sch.cnt.items():
            if v > 0 or semname in Sched.ENGS:
                sems[semname] = es.enter_context(nc.semaphore(
                    semname.replace(":", "_")))
        assert len(sems) <= 200, f"too many semaphores: {len(sems)}"
        block = es.enter_context(nc.Block())

        def replay(E, ename):
            with nc.allow_non_contiguous_dma(
                    reason="1-column lead/zero loads are tiny and rare"), \
                 nc.allow_low_precision(
                    reason="float32r views are fp32-storage; PSUM accumulation fp32"):
                cur = None
                for it in sch.items[ename]:
                    if it[0] == "m":
                        if cur is not None:
                            nc.leave_named_scope(cur[0], cur[1], False)
                        sid, _ = nc.enter_named_scope(it[1], False)
                        cur = (it[1], sid)
                    elif it[0] == "w":
                        E.wait_ge(sems[it[1]], it[2])
                    else:
                        _, fn, sem, inc = it
                        ins = fn(E)
                        if inc > 0:
                            ins.then_inc(sems[sem], inc)
                if cur is not None:
                    nc.leave_named_scope(cur[0], cur[1], False)

        @block.tensor
        def _(eng):
            replay(eng, "T")

        @block.vector
        def _(eng):
            replay(eng, "V")

        @block.scalar
        def _(eng):
            replay(eng, "A")

        @block.gpsimd
        def _(eng):
            replay(eng, "P")

        @block.sync
        def _(eng):
            replay(eng, "S")

    return nc


def kernel(**inputs):
    taps = dict(_TAPS)
    consts = host_constants()
    hw = host_layout(inputs)
    nc = build(taps)
    X = np.asarray(inputs["L0_attn_x"])
    ncores = 8
    shared = {}
    shared.update({k: np.ascontiguousarray(v) for k, v in consts.items()})
    shared.update(hw)
    in_maps = []
    for b in range(ncores):
        m = dict(shared)
        m["x"] = np.ascontiguousarray(X[b].T)
        in_maps.append(m)
    res = run_bass_kernel_spmd(nc, in_maps, core_ids=list(range(ncores)))
    outs = np.stack([np.asarray(r["out"]).T for r in res.results], 0)
    if taps:
        kernel.last_taps = [{nm: np.asarray(r["tap_" + nm]) for nm in taps}
                            for r in res.results]
    return outs.astype(np.float32)


# version 03
# speedup vs baseline: 1.1302x; 1.1302x over previous
import numpy as np
import ml_dtypes
import contextlib
import concourse.bass as bass
import concourse.mybir as mybir
from concourse.bass_utils import run_bass_kernel_spmd

F32 = mybir.dt.float32
F32R = mybir.dt.float32r
BF16 = mybir.dt.bfloat16
AF = mybir.ActivationFunctionType
ALU = mybir.AluOpType
AX = mybir.AxisListType

T, D, DIN, DST, DCONV, DTR = 4096, 1024, 2048, 16, 4, 64
NH, DH, WIN, HID = 8, 64, 512, 2730
CH, SC = 512, 1024
NCH, NSC = T // CH, T // SC
DT_J, DJ = DIN // 128, D // 128
NPAIR = (HID + 127) // 128
RAG = HID - 128 * (NPAIR - 1)
NEG = -50000.0
EPS = 1e-5

_TAPS = {}
_LAST_LABS = {}
_LAST_COUNTS = {}


def host_constants():
    c = {}
    c["ident_b"] = np.eye(128, dtype=ml_dtypes.bfloat16)
    c["ident_r"] = np.eye(128, dtype=np.float32)
    P = np.zeros((128, 128), np.float32)
    for h0 in (0, 64):
        for i in range(32):
            P[h0 + i, h0 + i + 32] = -1.0
            P[h0 + i + 32, h0 + i] = 1.0
    c["rperm_r"] = np.ascontiguousarray(P.T)
    c["onesrow_r"] = np.ones((1, 128), np.float32)
    c["oneskcol_r"] = np.ones((128, 1), np.float32)
    c["oneskcol_b"] = np.ones((128, 1), dtype=ml_dtypes.bfloat16)
    ho = np.zeros((128, 2), np.float32)
    ho[0:64, 0] = 1.0
    ho[64:128, 1] = 1.0
    c["headones_r"] = ho
    qq = np.arange(128)[:, None]
    cc = np.arange(128)[None, :]
    c["mask_d0"] = np.where(qq <= cc, 0.0, NEG).astype(np.float32)
    c["mask_d4"] = np.where(qq >= cc, 0.0, NEG).astype(np.float32)
    s = np.arange(1408)[None, :]
    c["maskT"] = np.where((s - qq >= 384) & (s - qq <= 896), 0.0, NEG).astype(np.float32)
    hi = np.arange(0, DH, 2, dtype=np.float64)
    inv = 1.0 / 10000.0 ** (hi / DH)
    t = np.arange(T, dtype=np.float64)
    fr = np.concatenate([t[:, None] * inv[None, :]] * 2, 1)
    base = (hi + 0.4 * DH) / (1.4 * DH)
    power = (t - T // 2) / (WIN // 2)
    scl = np.concatenate([base[None, :] ** power[:, None]] * 2, 1)
    cos, sin = np.cos(fr), np.sin(fr)
    for nm, tab in (("csq", 8.0 * cos * scl), ("ssq", 8.0 * sin * scl),
                    ("csk", cos / scl), ("ssk", sin / scl)):
        tt = tab.T.astype(np.float32)
        c[nm] = np.ascontiguousarray(np.concatenate([tt, tt], 0))
    c["zeros_c"] = np.zeros((128, 4), np.float32)
    return c


def host_layout(inp):
    f = np.ascontiguousarray
    h = {}
    h["inwT"] = f(inp["L0_attn_in_proj_w"].T)
    h["outwT"] = f(inp["L0_attn_out_proj_w"].T)
    h["xpwT"] = f(inp["L0_attn_x_proj_w"].T)
    h["dtwT"] = f(inp["L0_attn_dt_proj_w"].T)
    cw = inp["L0_attn_conv_w"]
    cd = np.zeros((DT_J, DCONV, 128, 128), np.float32)
    for j in range(DT_J):
        for k in range(DCONV):
            np.fill_diagonal(cd[j, k], cw[128 * j:128 * (j + 1), k])
    h["cdiag"] = cd
    dd = np.zeros((DT_J, 128, 128), np.float32)
    for j in range(DT_J):
        np.fill_diagonal(dd[j], inp["L0_attn_Dp"][128 * j:128 * (j + 1)])
    h["ddiag"] = dd
    h["convb2"] = f(inp["L0_attn_conv_b"].reshape(DT_J, 128).T)
    h["dtb2"] = f(inp["L0_attn_dt_proj_b"].reshape(DT_J, 128).T)
    h["alog2"] = f(inp["L0_attn_A_log"].reshape(DT_J, 128, DST)
                   .transpose(1, 0, 2).reshape(128, DT_J * DST))
    for l in range(2):
        h[f"wqkv{l}"] = f(inp["L0_ffn_wqkv"][l])
        h[f"wo{l}"] = f(inp["L0_ffn_wo"][l])
        h[f"qs{l}"] = f(np.tile(inp["L0_ffn_q_scale"][l], 2)[:, None])
        h[f"ks{l}"] = f(np.tile(inp["L0_ffn_k_scale"][l], 2)[:, None])
        h[f"lng{l}"] = f(inp["L0_ffn_ln_g"][l].reshape(DJ, 128).T)
        h[f"lnb{l}"] = f(inp["L0_ffn_ln_b"][l].reshape(DJ, 128).T)
        h[f"b2t{l}"] = f(inp["L0_ffn_b2"][l].reshape(DJ, 128).T)
        w1 = inp["L0_ffn_w1"][l]
        b1 = inp["L0_ffn_b1"][l]
        perm = []
        bt = np.zeros((128, 2 * NPAIR), np.float32)
        for p in range(NPAIR):
            wdt = min(128, HID - 128 * p)
            ia = np.arange(128 * p, 128 * p + wdt)
            perm.extend(ia)
            perm.extend(ia + HID)
            bt[:wdt, 2 * p] = b1[ia]
            bt[:wdt, 2 * p + 1] = b1[ia + HID]
        h[f"w1p{l}"] = f(w1[:, np.array(perm)])
        h[f"b1t{l}"] = bt
        h[f"w2_{l}"] = f(inp["L0_ffn_w2"][l])
    return h


class Buf:
    __slots__ = ("w", "r", "name")

    def __init__(self, name=""):
        self.w, self.r, self.name = None, {}, name


class Sched:
    ENGS = ("T", "V", "A", "P", "S")
    SELF_WAIT = ("A", "P", "V")

    def __init__(self):
        self.cnt = {e: 0 for e in self.ENGS}
        self.seen = {e: {} for e in self.ENGS}
        self.items = {e: [] for e in self.ENGS}
        self.curlab = ""
        self.labs = {e: [] for e in self.ENGS}

    def mark(self, name):
        for e in self.ENGS:
            self.items[e].append(("m", name))
            self.labs[e].append("mark")

    def _wait(self, e, sem, v):
        if v > self.seen[e].get(sem, 0):
            self.items[e].append(("w", sem, v))
            self.labs[e].append("w." + self.curlab + ":" + sem)
            self.seen[e][sem] = v

    def op(self, e, fn, reads=(), writes=(), inc=1, dma=False):
        if dma:
            assert writes, "DMA needs a write target buf"
            sem = "D:" + writes[0].name
            self.cnt.setdefault(sem, 0)
        else:
            sem = e
        deps = {}
        for b in reads:
            if b.w is not None:
                deps[b.w[0]] = max(deps.get(b.w[0], 0), b.w[1])
        for b in writes:
            if b.w is not None:
                deps[b.w[0]] = max(deps.get(b.w[0], 0), b.w[1])
            for s2, v in b.r.items():
                deps[s2] = max(deps.get(s2, 0), v)
        for s2, v in deps.items():
            if s2 == e:
                if e in self.SELF_WAIT:
                    self._wait(e, s2, v)
                continue
            self._wait(e, s2, v)
        self.cnt[sem] += inc
        self.items[e].append(("i", fn, sem, inc))
        self.labs[e].append("i." + self.curlab)
        rmark = self.cnt[sem] + (1 if inc == 0 and not dma else 0)
        for b in reads:
            b.r[sem] = max(b.r.get(sem, 0), rmark)
        if inc > 0:
            for b in writes:
                b.w = (sem, self.cnt[sem])
                b.r = {}

    def barrier(self):
        tgt = dict(self.cnt)
        for e in self.ENGS:
            for sem, v in tgt.items():
                if sem != e and v > 0:
                    self._wait(e, sem, v)


def build(taps=None, trunc=None):
    taps = taps or {}
    nc = bass.Bass("TRN2", target_bir_lowering=False, debug=False)
    dram = {}

    def din(name, shape, dt):
        dram[name] = nc.dram_tensor(name, shape, dt, kind="ExternalInput").ap()
        return dram[name]

    def dscr(name, shape, dt):
        dram[name] = nc.dram_tensor(name, shape, dt, kind="Internal").ap()
        return dram[name]

    x_in = din("x", [D, T], F32R)
    inwT = din("inwT", [D, 2 * DIN], F32)
    outwT = din("outwT", [DIN, D], F32)
    xpwT = din("xpwT", [DIN, DTR + 2 * DST], F32R)
    dtwT = din("dtwT", [DTR, DIN], F32R)
    cdiagd = din("cdiag", [DT_J, DCONV, 128, 128], F32)
    ddiagd = din("ddiag", [DT_J, 128, 128], F32)
    convb2 = din("convb2", [128, DT_J], F32)
    dtb2 = din("dtb2", [128, DT_J], F32)
    alog2 = din("alog2", [128, DT_J * DST], F32)
    wqkv = [din(f"wqkv{l}", [D, 3 * 512], F32R) for l in range(2)]
    wod = [din(f"wo{l}", [512, D], F32R) for l in range(2)]
    qs2 = [din(f"qs{l}", [128, 1], F32) for l in range(2)]
    ks2 = [din(f"ks{l}", [128, 1], F32) for l in range(2)]
    lng2 = [din(f"lng{l}", [128, DJ], F32) for l in range(2)]
    lnb2 = [din(f"lnb{l}", [128, DJ], F32) for l in range(2)]
    b2t2 = [din(f"b2t{l}", [128, DJ], F32) for l in range(2)]
    w1p = [din(f"w1p{l}", [D, 2 * HID], F32R) for l in range(2)]
    b1t2 = [din(f"b1t{l}", [128, 2 * NPAIR], F32) for l in range(2)]
    w2d = [din(f"w2_{l}", [HID, D], F32) for l in range(2)]
    cmeta = {"ident_b": ([128, 128], BF16), "ident_r": ([128, 128], F32R),
             "rperm_r": ([128, 128], F32R), "onesrow_r": ([1, 128], F32R),
             "oneskcol_r": ([128, 1], F32R), "oneskcol_b": ([128, 1], BF16),
             "headones_r": ([128, 2], F32R), "mask_d0": ([128, 128], F32R),
             "mask_d4": ([128, 128], F32R), "maskT": ([128, 1408], F32R),
             "csq": ([128, T], F32), "ssq": ([128, T], F32),
             "csk": ([128, T], F32), "ssk": ([128, T], F32),
             "zeros_c": ([128, 4], F32)}
    for nm, (shape, dt) in cmeta.items():
        din(nm, shape, dt)

    rowsB = dscr("rowsB", [NCH, DST, CH], F32)
    rowsC = dscr("rowsC", [NCH, DST, CH], BF16)
    rr = dscr("rr", [48, SC], F32)
    x1s = dscr("x1s", [D, T], F32)
    xms = [dscr(f"xm{l}", [D, T], F32) for l in range(2)]
    xl0 = dscr("xl0", [D, T], F32)
    out = nc.dram_tensor("out", [D, T], F32, kind="ExternalOutput").ap()
    dram["out"] = out
    tap_out = {nm: nc.dram_tensor("tap_" + nm, sh, dt, kind="ExternalOutput").ap()
               for nm, (sh, dt) in taps.items()}

    sch = Sched()
    es = contextlib.ExitStack()
    with es:
        AR = es.enter_context(nc.sbuf_tensor("AR", [128, 19456], F32R))
        AB = es.enter_context(nc.sbuf_tensor("AB", [128, 29824], BF16))
        AXS = es.enter_context(nc.sbuf_tensor("AX", [128, 11328], F32))
        SM = es.enter_context(nc.sbuf_tensor("SM", [128, 1024], F32))
        CST = es.enter_context(nc.sbuf_tensor("CST", [128, 3712], F32R))
        CSB = es.enter_context(nc.sbuf_tensor("CSB", [128, 256], BF16))
        DTT = es.enter_context(nc.sbuf_tensor("DTT", [64, 2048], F32R))
        PS = es.enter_context(nc.psum_tensor("PS", [128, 4096], F32))
        sems = {}
        block = None

        bufs = {}

        def B(name):
            if name not in bufs:
                bufs[name] = Buf(name)
            return bufs[name]

        CVo, _co = {}, [0]

        def _cal(nm, n):
            CVo[nm] = _co[0]
            _co[0] += n

        for nm_, n_ in [("maskd0", 128), ("maskd4", 128), ("maskT", 1408),
                        ("identr", 128), ("rperm", 128), ("onesrow", 128),
                        ("oneskr", 1), ("headones", 2), ("xpt", 96 * DT_J)]:
            _cal(nm_, n_)
        assert _co[0] <= 3712, _co[0]

        def cvv(nm, n):
            return CST[:, CVo[nm]:CVo[nm] + n]

        maskd0, maskd4, maskTv = cvv("maskd0", 128), cvv("maskd4", 128), cvv("maskT", 1408)
        identr, rperm = cvv("identr", 128), cvv("rperm", 128)
        onesrow = cvv("onesrow", 128)[0:1, 0:128]
        oneskr, headones = cvv("oneskr", 1), cvv("headones", 2)
        XPT = cvv("xpt", 96 * DT_J)
        identb, oneskb = CSB[:, 0:128], CSB[:, 128:129]

        SVdef = [("negA", 256), ("convb", 16), ("dtb", 16), ("hcarry", 256),
                 ("qs", 2), ("ks", 2), ("lng", 16), ("lnb", 16), ("b2c", 16),
                 ("b1c", 2 * NPAIR * 2), ("altmp", 256), ("sp1", 16), ("xit", 24), ("epsc", 1), ("onesf", 1)]
        so = {nm: sum(n for _, n in SVdef[:i]) for i, (nm, _) in enumerate(SVdef)}
        assert sum(n for _, n in SVdef) <= 1024

        def svv(nm):
            n = dict(SVdef)[nm]
            return SM[:, so[nm]:so[nm] + n]

        def ld(dst, src, writes, reads=(), eng="S"):
            sch.op(eng, (lambda d, s: (lambda E: E.dma_start(out=d, in_=s)))(dst, src),
                   reads=reads, writes=writes, inc=16, dma=True)

        def st(dst, src, reads, writes=(), eng="S"):
            wr = list(writes) if writes else [B("stsink")]
            sch.op(eng, (lambda d, s: (lambda E: E.dma_start(out=d, in_=s)))(dst, src),
                   reads=reads, writes=wr, inc=16, dma=True)

        def ldw(lw2d, src2d, nk, ncc, writes, reads=(), eng="S"):
            ld(lw2d.rearrange("p (k c) -> p k c", k=nk),
               src2d.rearrange("(k p) c -> p k c", p=128), writes, reads, eng)

        def mm(ps_ap, lhsT, rhs, reads, writes, start=True, stop=True, inc=1):
            sch.op("T", (lambda o_, a_, b_, s_, p_:
                         (lambda E: E.matmul(o_, a_, b_, start=s_, stop=p_)))
                   (ps_ap, lhsT, rhs, start, stop), reads=reads, writes=writes, inc=1)

        def act(out_ap, in_ap, fn, reads, writes, bias=0.0, scale=1.0, accum=None):
            sch.op("A", (lambda o_, i_, f_, bi, s_, ac:
                         (lambda E: E.activation(o_, i_, f_, bias=bi, scale=s_,
                                                 accum_out=ac)))
                   (out_ap, in_ap, fn, bias, scale, accum), reads=reads, writes=writes)

        def vtt(out_ap, a_ap, b_ap, op, reads, writes, eng="V"):
            sch.op(eng, (lambda o_, a_, b_, p_: (lambda E: E.tensor_tensor(o_, a_, b_, p_)))
                   (out_ap, a_ap, b_ap, op), reads=reads, writes=writes)

        def vstt(out_ap, in0, scal, in1, op0, op1, reads, writes, eng="V"):
            sch.op(eng, (lambda o_, a_, s_, b_, p0, p1:
                         (lambda E: E.scalar_tensor_tensor(o_, a_, s_, b_, p0, p1)))
                   (out_ap, in0, scal, in1, op0, op1), reads=reads, writes=writes)

        def vts2(out_ap, in0, s1, s2, op0, op1, reads, writes, eng="V"):
            sch.op(eng, (lambda o_, a_, y1, y2, p0, p1:
                         (lambda E: E.tensor_scalar(o_, a_, y1, y2, p0, p1)))
                   (out_ap, in0, s1, s2, op0, op1), reads=reads, writes=writes)

        def vts1(out_ap, in0, s1, op, reads, writes, eng="V"):
            sch.op(eng, (lambda o_, a_, y1, p_:
                         (lambda E: E.tensor_single_scalar(o_, a_, y1, p_)))
                   (out_ap, in0, s1, op), reads=reads, writes=writes)

        def vcopy(out_ap, in_ap, reads, writes, eng="V"):
            sch.op(eng, (lambda o_, i_: (lambda E: E.tensor_copy(o_, i_)))(out_ap, in_ap),
                   reads=reads, writes=writes)

        def vscan(out_ap, a_ap, b_ap, init, reads, writes):
            sch.op("V", (lambda o_, a_, b_, i_:
                         (lambda E: E.tensor_tensor_scan(o_, a_, b_, i_, ALU.mult, ALU.add)))
                   (out_ap, a_ap, b_ap, init), reads=reads, writes=writes)

        def vmax(out_ap, in_ap, reads, writes, negate=False):
            sch.op("V", (lambda o_, i_, n_: (lambda E: E.reduce_max(o_, i_, AX.X, negate=n_)))
                   (out_ap, in_ap, negate), reads=reads, writes=writes)

        def vrecip(out_ap, in_ap, reads, writes):
            sch.op("V", (lambda o_, i_: (lambda E: E.reciprocal(o_, i_)))(out_ap, in_ap),
                   reads=reads, writes=writes)

        def vmemset(ap, v, writes, eng="V"):
            sch.op(eng, (lambda a_, v_: (lambda E: E.memset(a_, v_)))(ap, v), writes=writes)

        def trans(ps_ap, in_ap, id_ap, reads, writes):
            sch.op("T", (lambda o_, i_, d_: (lambda E: E.transpose(o_, i_, d_)))
                   (ps_ap, in_ap, id_ap), reads=reads, writes=writes)

        PB = [B(f"ps{i}") for i in range(8)]

        def psv(bank, n=512, off=0, dt=None, p0=0, pn=128):
            ap = PS[p0:p0 + pn, 512 * bank + off:512 * bank + off + n]
            return ap.bitcast(dt) if dt else ap

        def tapw(nm, src_ap, rbufs):
            if nm in taps:
                want = taps[nm][1]
                if src_ap.dtype != want:
                    src_ap = src_ap.bitcast(want)
                st(tap_out[nm][:, :], src_ap, rbufs, writes=[B("tap" + nm)])

        def arv(off, n, dt=None):
            assert off + n <= 19456, (off, n)
            ap = AR[:, off:off + n]
            return ap.bitcast(dt) if dt else ap

        def abv(off, n, dt=None):
            assert off + n <= 29824, (off, n)
            ap = AB[:, off:off + n]
            return ap.bitcast(dt) if dt else ap

        def axv(off, n):
            assert off + n <= 11328, (off, n)
            return AXS[:, off:off + n]

        def row0(off, n):
            return AR[0:1, off:off + n]

        for nm in ("ident_b", "oneskcol_b", "oneskcol_r", "headones_r", "rperm_r",
                   "mask_d0", "mask_d4", "maskT", "ident_r"):
            tgt = {"ident_b": identb, "oneskcol_b": oneskb, "oneskcol_r": oneskr,
                   "headones_r": headones, "rperm_r": rperm, "mask_d0": maskd0,
                   "mask_d4": maskd4, "maskT": maskTv, "ident_r": identr}[nm]
            ld(tgt, dram[nm][:, :], [B("constD")])
        ld(onesrow, dram["onesrow_r"][0:1, :], [B("constD")])
        ld(svv("convb"), convb2[:, :], [B("constD")])
        ld(svv("dtb"), dtb2[:, :], [B("constD")])
        for l in range(2):
            ld(svv("qs")[:, l:l + 1], qs2[l][:, :], [B("constD")])
            ld(svv("ks")[:, l:l + 1], ks2[l][:, :], [B("constD")])
            ld(svv("lng")[:, 8 * l:8 * l + 8], lng2[l][:, :], [B("constD")])
            ld(svv("lnb")[:, 8 * l:8 * l + 8], lnb2[l][:, :], [B("constD")])
            ld(svv("b2c")[:, 8 * l:8 * l + 8], b2t2[l][:, :], [B("constD")])
            ld(svv("b1c")[:, 2 * NPAIR * l:2 * NPAIR * (l + 1)], b1t2[l][:, :],
               [B("constD")])
        ld(svv("altmp"), alog2[:, :], [B("constD")])
        act(svv("negA"), svv("altmp"), AF.Exp, [B("constD")], [B("negA")])
        vts1(svv("negA"), svv("negA"), -1.0, ALU.mult, [B("negA")], [B("negA")])
        for k in range(DT_J):
            ld(XPT[:, 96 * k:96 * k + 96], xpwT[128 * k:128 * k + 128, :], [B("constD")])
        ld(DTT[:, :], dtwT[:, :], [B("constD")])
        vmemset(svv("hcarry"), 0.0, [B("hcarry")])
        vmemset(svv("epsc"), EPS, [B("epsc")])
        vmemset(svv("onesf"), 1.0, [B("constD")])

        RXb = [B(f"RX{i}") for i in range(16)]
        XIb = [B(f"XI{i}") for i in range(8)]
        rx = lambda i: arv(512 * i, 512)
        xi_t = [abv(520 * j, 520) for j in range(DT_J)]
        y3_t = [abv(512 * j, 512) for j in range(DT_J)]

        XIr = B("XIr")

        def xib(j):
            return XIr

        def y3b(j):
            return XIr

        dtlo = arv(8192, 512)
        dtlo_b = B("dtlo")
        bcT = arv(15360, 512)
        bc_b = B("bcT")
        ddiag_t = arv(18944, 128)
        ddiag_b = B("ddiagb")
        dt_t = [arv(14336 + 512 * i, 512) for i in range(2)]
        dt_b = [B(f"dt{i}") for i in range(2)]
        u_t = [abv(8320 + 512 * i, 512) for i in range(2)]
        u_b = [B(f"u{i}") for i in range(2)]
        a_t = [arv(12800 + 512 * i, 512) for i in range(3)]
        a_b = [B(f"a{i}") for i in range(3)]
        b_t = [axv(8704 + 512 * i, 512).bitcast(F32R) for i in range(3)]
        b_b = [B(f"b{i}") for i in range(3)]
        w_t = [abv(9344 + 512 * i, 512) for i in range(3)]
        w_b = [B(f"w{i}") for i in range(3)]
        cr_t = [abv(17024 + 512 * s, 512) for s in range(DST)]
        cr_b = [B("crD") for s in range(DST)]
        hb_t = [abv(10880 + 512 * i, 512) for i in range(4)]
        hb_b = [B(f"hb{i}") for i in range(4)]
        xlnB = [abv(12928 + 512 * j, 512) for j in range(DJ)]
        xln_b = [B(f"xln{j}") for j in range(DJ)]
        br_t = [axv(512 + 512 * s, 512) for s in range(DST)]
        br_b = [B("brD") for s in range(DST)]
        cdiag_t = abv(29312, 512)
        outm_t = [abv(27264, 2048)]
        outm_b = [[B("outm0")]]

        xch = [arv(8704 + 512 * j, 512) for j in range(DJ)]
        xch_b = [B("xchD") for j in range(DJ)]
        inm_t = [abv(25216 + 1024 * i, 1024) for i in range(2)]
        inm_b = [B(f"inm{i}") for i in range(2)]
        muB, rsB = arv(16896, 512), arv(17408, 512)
        g_t = [arv(17920, 512), arv(18432, 512)]
        sp1, sp2 = arv(15872, 512), arv(16384, 512)
        mrow, vrow, rrow = row0(512 * 9, 512), row0(512 * 10, 512), row0(512 * 11, 512)
        rowb = [RXb[9], RXb[10], RXb[11]]
        rrD = B("rrD")
        rowsBD, rowsCD = B("rowsBD"), B("rowsCD")

        def ln_rows(sumA, sumQ, bA, bQ, n, nrm, slot, dq="S"):
            act(mrow[:, 0:n], sumA, AF.Copy, [bA], [rowb[0]], scale=1.0 / nrm)
            act(vrow[:, 0:n], sumQ, AF.Copy, [bQ], [rowb[1]], scale=1.0 / nrm)
            vtt(rrow[:, 0:n], mrow[:, 0:n], mrow[:, 0:n], ALU.mult, [rowb[0]], [rowb[2]])
            vtt(vrow[:, 0:n], vrow[:, 0:n], rrow[:, 0:n], ALU.subtract,
                [rowb[1], rowb[2]], [rowb[1]])
            act(rrow[:, 0:n], vrow[:, 0:n], AF.Ln, [rowb[1], B("epsc")], [rowb[2]], bias=svv("epsc")[0:1, :])
            act(rrow[:, 0:n], rrow[:, 0:n], AF.Exp, [rowb[2]], [rowb[2]], scale=-0.5)
            st(rr[slot:slot + 1, 0:n], mrow[:, 0:n].bitcast(F32), [rowb[0]],
               writes=[B(f"rrM{slot}")], eng=dq)
            st(rr[slot + 1:slot + 2, 0:n], rrow[:, 0:n].bitcast(F32), [rowb[2]],
               writes=[B(f"rrM{slot}")], eng=dq)

        sch.mark("M")
        mchunks = NCH if not trunc else min(trunc.get("mc", NCH), NCH)
        mstage = 99 if not trunc else trunc.get("ms", 99)
        for c in range(mchunks):
            t0 = CH * c
            sch.curlab = "M1"
            for j in range(DJ):
                ld(xch[j], x_in[128 * j:128 * j + 128, t0:t0 + CH], [xch_b[j]])
            if mstage < 2:
                continue
            sA, sQ = psv(6, CH, p0=0, pn=1), psv(7, CH, p0=0, pn=1)
            for j in range(DJ):
                mm(sA, oneskr, xch[j], [xch_b[j], B("constD")], [PB[6]],
                   start=(j == 0), stop=(j == DJ - 1), inc=(1 if j == DJ - 1 else 0))
            sq1 = rx(8)
            for j in range(DJ):
                act(sq1, xch[j], AF.Square, [xch_b[j]], [RXb[8]])
                mm(sQ, oneskr, sq1, [RXb[8], B("constD")], [PB[7]],
                   start=(j == 0), stop=(j == DJ - 1), inc=(1 if j == DJ - 1 else 0))
            ln_rows(sA, sQ, PB[6], PB[7], CH, D, 0)
            if mstage < 3:
                continue
            ld(muB, rr.bitcast(F32R)[0:1, 0:CH].partition_broadcast(128), [B("muBD")],
               reads=[B("rrM0")])
            ld(rsB, rr.bitcast(F32R)[1:2, 0:CH].partition_broadcast(128), [B("rsBD")],
               reads=[B("rrM0")])
            for j in range(DJ):
                t1 = rx(j)
                vstt(t1, xch[j].bitcast(F32), 1.0, muB, ALU.mult,
                     ALU.subtract, [xch_b[j], B("muBD")], [RXb[j]])
                vtt(xlnB[j], t1.bitcast(F32), rsB, ALU.mult, [RXb[j], B("rsBD")],
                    [xln_b[j]])
            if c == 0:
                tapw("xln", xlnB[0], [xln_b[0]])
            if mstage < 4:
                continue
            sch.curlab = "M2"
            xitv = svv("xit").bitcast(BF16)
            for j in range(DT_J):
                if c == 0:
                    ld(xi_t[j][:, 0:3], dram["zeros_c"][:, 0:3], [xib(j)],
                       eng="P")
                else:
                    vcopy(xi_t[j][:, 0:3], xitv[:, 3 * j:3 * j + 3],
                          [B("xitB")], [xib(j)])
            for m in range(DT_J):
                lw, lwb = inm_t[m % 2], inm_b[m % 2]
                ldw(lw, inwT[:, 128 * m:128 * m + 128], 8, 128, [lwb], eng="P")
                bank = m % 6
                for k in range(DJ):
                    mm(psv(bank), lw[:, 128 * k:128 * k + 128], xlnB[k],
                       [lwb, xln_b[k]], [PB[bank]], start=(k == 0),
                       stop=(k == DJ - 1), inc=(1 if k == DJ - 1 else 0))
                act(xi_t[m][:, 3:515], psv(bank), AF.Copy, [PB[bank]], [xib(m)])
            if mstage < 5:
                continue
            sch.curlab = "M3"
            for j in range(DT_J):
                ld(cdiag_t.rearrange("p (k c) -> p k c", k=DCONV),
                   cdiagd[j].rearrange("k p c -> p k c"), [B("cdiagb")], eng="P")
                bank = j % 2
                for k in range(DCONV):
                    mm(psv(bank), cdiag_t[:, 128 * k:128 * k + 128],
                       xi_t[j][:, k:k + 512], [B("cdiagb"), xib(j)], [PB[bank]],
                       start=(k == 0), stop=(k == DCONV - 1),
                       inc=(1 if k == DCONV - 1 else 0))
                act(rx(j), psv(bank), AF.Silu, [PB[bank]], [RXb[j]],
                    bias=svv("convb")[:, j:j + 1])
                vcopy(xitv[:, 3 * j:3 * j + 3], xi_t[j][:, 512:515], [xib(j)],
                      [B("xitB")])
            if mstage < 6:
                continue
            sch.curlab = "M4"
            xdbl = PS[0:96, 1024:1024 + CH]
            for k in range(DT_J):
                mm(xdbl, XPT[:, 96 * k:96 * k + 96], rx(k), [RXb[k], B("constD")],
                   [PB[2]], start=(k == 0), stop=(k == DT_J - 1),
                   inc=(1 if k == DT_J - 1 else 0))
            act(dtlo[0:64, :], PS[0:64, 1024:1024 + CH], AF.Copy, [PB[2]], [dtlo_b])
            act(bcT[0:32, :], PS[64:96, 1024:1024 + CH], AF.Copy, [PB[2]], [bc_b])
            st(rowsB[c], bcT[0:16, :].bitcast(F32), [bc_b], writes=[rowsBD])
            st(rowsC[c], bcT[16:32, :], [bc_b], writes=[rowsCD], eng="P")
            if c == 0:
                tapw("xc", rx(0), [RXb[0]])
            for s_i in range(DST):
                ld(br_t[s_i], rowsB[c][s_i:s_i + 1, :].partition_broadcast(128),
                   [br_b[s_i]], reads=[rowsBD])
                ld(cr_t[s_i], rowsC[c][s_i:s_i + 1, :].partition_broadcast(128),
                   [cr_b[s_i]], reads=[rowsCD])
            if mstage < 7:
                continue
            sch.curlab = "M5"
            dtiles = DT_J if mstage >= 8 else 1
            for j in range(dtiles):
                dbank = 3 + (j % 2)
                mm(psv(dbank), DTT[:, 128 * j:128 * j + 128], dtlo[0:64, :],
                   [B("constD"), dtlo_b], [PB[dbank]])
                dtj, dtjb = dt_t[j % 2], dt_b[j % 2]
                uj, ujb = u_t[j % 2], u_b[j % 2]
                act(sp1, psv(dbank), AF.Abs, [PB[dbank]], [RXb[14]],
                    bias=svv("dtb")[:, j:j + 1])
                act(sp2, sp1, AF.Exp, [RXb[14]], [RXb[15]], scale=-1.0)
                act(sp1, sp2, AF.Ln, [RXb[15]], [RXb[14]], bias=1.0)
                vts2(sp2, psv(dbank), svv("dtb")[:, j:j + 1], 0.0, ALU.add, ALU.max,
                     [PB[dbank]], [RXb[15]])
                vtt(dtj, sp1, sp2, ALU.add, [RXb[14], RXb[15]], [dtjb])
                vtt(uj, dtj, rx(j).bitcast(F32), ALU.mult, [dtjb, RXb[j]], [ujb])
                if c == 0 and j == 0:
                    tapw("dt0", dtj.bitcast(F32R), [dtjb])
                mz = DT_J + j
                lw, lwb = inm_t[j % 2], inm_b[j % 2]
                ldw(lw, inwT[:, 128 * mz:128 * mz + 128], 8, 128, [lwb], eng="P")
                zbank = 7
                for k in range(DJ):
                    mm(psv(zbank), lw[:, 128 * k:128 * k + 128], xlnB[k],
                       [lwb, xln_b[k]], [PB[zbank]], start=(k == 0),
                       stop=(k == DJ - 1), inc=(1 if k == DJ - 1 else 0))
                gj = g_t[j % 2]
                gjb = B(f"gring{j % 2}")
                act(gj, psv(zbank), AF.Silu, [PB[zbank]], [gjb])
                sch.curlab = "M6scan"
                ybank = 5 + (j % 2)
                for s_i in range(DST):
                    sl = s_i % 3
                    hs = s_i % 4
                    act(a_t[sl], dtj, AF.Exp, [dtjb], [a_b[sl]],
                        scale=svv("negA")[:, DST * j + s_i:DST * j + s_i + 1])
                    vtt(b_t[sl], uj, br_t[s_i], ALU.mult, [ujb, br_b[s_i]],
                        [b_b[sl]], eng="P")
                    vscan(hb_t[hs], a_t[sl], b_t[sl],
                          svv("hcarry")[:, DST * j + s_i:DST * j + s_i + 1],
                          [a_b[sl], b_b[sl], B("hcarry")], [hb_b[hs]])
                    vtt(w_t[sl], hb_t[hs], cr_t[s_i], ALU.mult,
                        [hb_b[hs], cr_b[s_i]], [w_b[sl]])
                    mm(psv(ybank), identb, w_t[sl], [w_b[sl], B("constD")],
                       [PB[ybank]], start=(s_i == 0), stop=False, inc=0)
                    if s_i % 4 == 3:
                        q4 = s_i // 4
                        hc_src = AB[:, 10880:10880 + 2048] \
                            .rearrange("p (i c) -> p i c", c=512)[:, :, 511:512]
                        vcopy(svv("hcarry")[:, DST * j + 4 * q4:
                                            DST * j + 4 * q4 + 4],
                              hc_src, [hb_b[i] for i in range(4)], [B("hcarry")])
                sch.curlab = "M6y"
                ld(ddiag_t, ddiagd[j].bitcast(F32R)[:, :], [ddiag_b])
                mm(psv(ybank), ddiag_t, rx(j), [ddiag_b, RXb[j]], [PB[ybank]],
                   start=False, stop=True, inc=1)
                vtt(y3_t[j], psv(ybank), gj, ALU.mult, [PB[ybank], gjb], [y3b(j)])
            if c == 0:
                tapw("y30", y3_t[0], [y3b(0)])
            if mstage < 9:
                continue
            sch.curlab = "M7"
            for m in range(DJ):
                lw, lwbs = outm_t[0], outm_b[0]
                ldw(lw, outwT[:, 128 * m:128 * m + 128], 16, 128, lwbs, eng="P")
                bank = m % 4
                for k in range(DT_J):
                    mm(psv(bank), lw[:, 128 * k:128 * k + 128], y3_t[k],
                       lwbs + [y3b(k)], [PB[bank]], start=(k == 0),
                       stop=(k == DT_J - 1), inc=(1 if k == DT_J - 1 else 0))
                act(rx(m), psv(bank), AF.Copy, [PB[bank]], [RXb[m]])
            if c == 0:
                tapw("mo0", rx(0), [RXb[0]])
            if mstage < 10:
                continue
            sch.curlab = "M8"
            sA2, sQ2 = psv(6, CH, p0=0, pn=1), psv(7, CH, p0=0, pn=1)
            for j in range(DJ):
                mm(sA2, oneskr, rx(j), [RXb[j], B("constD")], [PB[6]],
                   start=(j == 0), stop=(j == DJ - 1), inc=(1 if j == DJ - 1 else 0))
            sq2 = rx(8)
            for j in range(DJ):
                act(sq2, rx(j), AF.Square, [RXb[j]], [RXb[8]])
                mm(sQ2, oneskr, sq2, [RXb[8], B("constD")], [PB[7]],
                   start=(j == 0), stop=(j == DJ - 1), inc=(1 if j == DJ - 1 else 0))
            ln_rows(sA2, sQ2, PB[6], PB[7], CH, D, 2)
            ld(muB, rr.bitcast(F32R)[2:3, 0:CH].partition_broadcast(128), [B("muBD")],
               reads=[B("rrM2")])
            ld(rsB, rr.bitcast(F32R)[3:4, 0:CH].partition_broadcast(128), [B("rsBD")],
               reads=[B("rrM2")])
            for j in range(DJ):
                ld(xch[j], x_in[128 * j:128 * j + 128, t0:t0 + CH], [xch_b[j]])
            for j in range(DJ):
                tmp = axv(0, 512)
                vstt(tmp, rx(j).bitcast(F32), 1.0, muB, ALU.mult, ALU.subtract,
                     [RXb[j], B("muBD")], [XIr])
                vtt(rx(j), tmp, rsB, ALU.mult, [XIr, B("rsBD")],
                    [RXb[j]])
                vtt(tmp, rx(j).bitcast(F32), xch[j].bitcast(F32), ALU.add,
                    [RXb[j], xch_b[j]], [XIr])
                st(x1s[128 * j:128 * j + 128, t0:t0 + CH], tmp.bitcast(F32),
                   [XIr], writes=[B("x1sD")])
                if c == 0 and j == 0:
                    tapw("x10", tmp.bitcast(F32R), [XIr])

        sch.barrier()

        AXF, AQT, AKT, ASC3 = 0, 4160, 6208, 10304
        ANR, ASQ, AAT = 11840, 12352, 12872
        ALW1, ALW2, ANG = 14920, 16968, 17992
        ATQ, AVB, AVT, AES = 0, 8320, 10368, 14464
        XAX1, XART, XAMU, XARW = 0, 4096, 6144, 7184
        XAN2, XAN2C, XARB2 = 8744, 9256, 9768
        XARC, XADN = 10280, 10792

        def ffn_attn(l, srcdram, dstdram):
            x1t = [axv(XAX1 + 512 * j, 512).bitcast(F32R) for j in range(DJ)]
            x1b = [B("ax1D") for j in range(DJ)]
            xft = [arv(AXF + 520 * j, 520) for j in range(DJ)]
            xfb = [B("axfD") for j in range(DJ)]
            tt_ = [abv(ATQ + 1040 * j, 1040, F32R) for j in range(DJ)]
            ttb = [B(f"a_t{j}") for j in range(DJ)]
            sqa = arv(ASQ, 520)
            sqab = B("a_sq")
            qt = [arv(AQT + 512 * m, 512) for m in range(4)]
            qtb = [B(f"a_q{m}") for m in range(4)]
            kt = [[arv(AKT + 2048 * sl + 512 * m, 512) for m in range(4)]
                  for sl in range(2)]
            ktb = [[B(f"a{l}k{sl}{m}") for m in range(4)] for sl in range(2)]
            scr = [arv(ASC3 + 512 * i, 512) for i in range(3)]
            scrb = [B(f"a_scr{i}") for i in range(3)]
            sqnt = [arv(ANR, 512), arv(ASQ, 512)]
            sqnbt = [B("a_sqn"), sqab]
            rstdBt = [axv(XARB2, 512).bitcast(F32R), axv(XAN2, 512).bitcast(F32R)]
            rstdBbt = [B("a_rstdB0"), B("a_rstdB1")]
            vb = [abv(AVB + 512 * m, 512) for m in range(4)]
            vbb = [B(f"a_v{m}") for m in range(4)]
            vT = [[abv(AVT + 2048 * sl + 64 * i, 64) for i in range(32)]
                  for sl in range(2)]
            vTb = [[B(f"a{l}vT{sl}{i}") for i in range(32)] for sl in range(2)]
            eSTt = [[abv(AES + 512 * (8 * par + m), 512) for m in range(8)]
                    for par in range(3)]
            eSTbt = [[B(f"a_e{8 * par + m}") for m in range(8)] for par in range(3)]
            negrowt = [arv(ANG, 512), arv(ANG + 512, 512)]
            negrowbt = [B("a_negrow0"), B("a_negrow1")]
            dnrowt = [AXS[0:1, XADN:XADN + 512].bitcast(F32R),
                      AXS[0:1, XARW:XARW + 512].bitcast(F32R)]
            dnrowbt = [B("a_n3"), B("a_mr")]
            recipBt = [axv(XARC, 512).bitcast(F32R)[0:64, :],
                       axv(XAN2C, 512).bitcast(F32R)[0:64, :]]
            recipBbt = [B("a_recipB0"), B("a_recipB1")]
            att = [arv(AAT + 512 * m, 512) for m in range(4)]
            attb = [B(f"a_att{m}") for m in range(4)]
            xmid = [axv(XAX1 + 512 * j, 512).bitcast(F32R) for j in range(DJ)]
            xmidb = [B("ax1D") for j in range(DJ)]
            rtab = {nm: axv(XART + 512 * i, 512).bitcast(F32R)
                    for i, nm in enumerate(("csq", "ssq", "csk", "ssk"))}
            rtabb = {nm: B("artD") for nm in rtab}
            qlw = [arv(ALW1 + 1024 * i, 1024) for i in range(2)]
            qlwb = [B(f"a_qlw{i}") for i in range(2)]
            wlw = [arv(ALW2 + 512 * i, 512) for i in range(2)]
            wlwb = [B(f"a_wlw{i}") for i in range(2)]
            muBa, rsBa = axv(XAMU, 520).bitcast(F32R), axv(XAMU + 520, 520).bitcast(F32R)
            muBab, rsBab = B("a_muB"), B("a_rsB")
            mrA = AXS[0:1, XARW:XARW + 520].bitcast(F32R)
            vrA = AXS[0:1, XARW + 520:XARW + 1040].bitcast(F32R)
            rrA = AXS[0:1, XARW + 1040:XARW + 1560].bitcast(F32R)
            rowAb = [B("a_mr"), B("a_vr"), B("a_rr")]
            nrm2t = [AXS[0:2, XARW + 520:XARW + 1032].bitcast(F32R),
                     AXS[0:2, XARW + 1040:XARW + 1552].bitcast(F32R)]
            nrm2bt = [B("a_vr"), B("a_rr")]
            nrm2ct = [AXS[0:2, XARW:XARW + 512].bitcast(F32R),
                      AXS[0:2, XADN:XADN + 512].bitcast(F32R)]
            nrm2cbt = [B("a_mr"), B("a_n3")]

            for c in range(NCH):
                sch.curlab = "Aload"
                t0 = CH * c
                sl_c, sl_p = c % 2, (c + 1) % 2
                for j in range(DJ):
                    if c == 0:
                        ld(xft[j][:, 0:1], dram["zeros_c"].bitcast(F32R)[:, 0:1],
                           [xfb[j]])
                    elif l == 0:
                        vcopy(xft[j][:, 0:1], xft[j][:, 512:513], [xfb[j]],
                              [xfb[j]])
                if l == 0:
                    for j in range(DJ):
                        ld(xft[j][:, 1:513],
                           srcdram.bitcast(F32R)[128 * j:128 * j + 128, t0:t0 + CH],
                           [xfb[j]], reads=[B("x1sD")])
                else:
                    for j in range(DJ):
                        if c == 0:
                            ld(xft[j][:, 1:513],
                               srcdram.bitcast(F32R)[128 * j:128 * j + 128,
                                                     t0:t0 + CH],
                               [xfb[j]], reads=[B("xl0D")])
                        else:
                            ld(xft[j][:, 0:513],
                               srcdram.bitcast(F32R)[128 * j:128 * j + 128,
                                                     t0 - 1:t0 + CH],
                               [xfb[j]], reads=[B("xl0D")])
                if l == 0:
                    sch.curlab = "Aln"
                    sA = psv(6, CH, p0=0, pn=1)
                    sQ = psv(7, CH, p0=0, pn=1)
                    for j in range(DJ):
                        mm(sA, svv("onesf"), xft[j][:, 1:513].bitcast(F32),
                           [xfb[j], B("constD")],
                           [PB[6]], start=(j == 0), stop=(j == DJ - 1),
                           inc=(1 if j == DJ - 1 else 0))
                    for j in range(DJ):
                        act(sqa[:, 0:512], xft[j][:, 1:513], AF.Square, [xfb[j]],
                            [sqab])
                        mm(sQ, oneskr, sqa[:, 0:512], [sqab, B("constD")],
                           [PB[7]], start=(j == 0), stop=(j == DJ - 1),
                           inc=(1 if j == DJ - 1 else 0))
                    act(mrA[:, 0:512], sA, AF.Copy, [PB[6]], [rowAb[0]], scale=1.0 / D)
                    act(vrA[:, 0:512], sQ, AF.Copy, [PB[7]], [rowAb[1]], scale=1.0 / D)
                    vtt(rrA[:, 0:512], mrA[:, 0:512], mrA[:, 0:512], ALU.mult,
                        [rowAb[0]], [rowAb[2]])
                    vtt(vrA[:, 0:512], vrA[:, 0:512], rrA[:, 0:512], ALU.subtract,
                        [rowAb[1], rowAb[2]], [rowAb[1]])
                    act(rrA[:, 0:512], vrA[:, 0:512], AF.Ln,
                        [rowAb[1], B("epsc")], [rowAb[2]],
                        bias=svv("epsc")[0:1, :])
                    act(rrA[:, 0:512], rrA[:, 0:512], AF.Exp, [rowAb[2]], [rowAb[2]],
                        scale=-0.5)
                    st(rr[4:5, 0:512], mrA[:, 0:512].bitcast(F32), [rowAb[0]],
                       writes=[B("rrA45")], eng="P")
                    st(rr[5:6, 0:512], rrA[:, 0:512].bitcast(F32), [rowAb[2]],
                       writes=[B("rrA45")], eng="P")
                    ld(muBa[:, 0:512], rr.bitcast(F32R)[4:5, 0:512].partition_broadcast(128),
                       [muBab], reads=[B("rrA45")], eng="P")
                    ld(rsBa[:, 0:512], rr.bitcast(F32R)[5:6, 0:512].partition_broadcast(128),
                       [rsBab], reads=[B("rrA45")], eng="P")
                    for j in range(DJ):
                        vstt(tt_[j][:, 0:512], xft[j][:, 1:513].bitcast(F32), 1.0,
                             muBa[:, 0:512], ALU.mult, ALU.subtract,
                             [xfb[j], muBab], [ttb[j]])
                        vtt(xft[j][:, 1:513], tt_[j][:, 0:512],
                            rsBa[:, 0:512], ALU.mult, [ttb[j], rsBab], [xfb[j]])
                if c == 0 and l == 0:
                    tapw("xf0", xft[0][:, 1:513], [xfb[0]])
                sch.curlab = "Aqkv"
                for nm in rtab:
                    ld(rtab[nm], dram[nm].bitcast(F32R)[:, t0:t0 + CH], [rtabb[nm]],
                       eng="P")

                def qkvmm(m):
                    lw, lwb = qlw[m % 2], qlwb[m % 2]
                    bank = m % 5
                    for k in range(DJ):
                        mv_ap = xft[k][:, 1:513] if k < 4 else xft[k][:, 0:512]
                        mm(psv(bank), lw[:, 128 * k:128 * k + 128], mv_ap,
                           [lwb, xfb[k]], [PB[bank]], start=(k == 0),
                           stop=(k == DJ - 1), inc=(1 if k == DJ - 1 else 0))

                def qstats(m):
                    bank = m % 5
                    par = m % 2
                    sqn, sqnb = sqnt[par], sqnbt[par]
                    nrm2, nrm2b = nrm2t[par], nrm2bt[par]
                    nrm2c, nrm2cb = nrm2ct[par], nrm2cbt[par]
                    rstdB, rstdBb = rstdBt[par], rstdBbt[par]
                    isq = m < 4
                    mt = m if isq else m - 4
                    act(sqn, psv(bank), AF.Square, [PB[bank]], [sqnb])
                    nps = PS[0:2, 512 * (5 + par):512 * (5 + par) + 512]
                    mm(nps, headones, sqn, [sqnb, B("constD")], [PB[5 + par]])
                    act(nrm2, nps, AF.Sqrt, [PB[5 + par]], [nrm2b])
                    vts1(nrm2, nrm2, 1e-12, ALU.max, [nrm2b], [nrm2b])
                    vrecip(nrm2c, nrm2, [nrm2b], [nrm2cb])
                    slot = (6 + 2 * m) if isq else (14 + 2 * mt)
                    st(rr[slot:slot + 2, 0:512], nrm2c.bitcast(F32), [nrm2cb],
                       writes=[B(f"rrn{m}")], eng="P")
                    ld(rstdB[0:64, :], rr.bitcast(F32R)[slot:slot + 1, 0:512]
                       .partition_broadcast(64), [rstdBb], reads=[B(f"rrn{m}")],
                       eng="P")
                    ld(rstdB[64:128, :],
                       rr.bitcast(F32R)[slot + 1:slot + 2, 0:512]
                       .partition_broadcast(64), [rstdBb],
                       reads=[rstdBb, B(f"rrn{m}")], eng="P")

                def qapply(m):
                    bank = m % 5
                    if m < 8:
                        isq = m < 4
                        mt = m if isq else m - 4
                        dstt = qt[mt] if isq else kt[sl_c][mt]
                        dsttb = qtb[mt] if isq else ktb[sl_c][mt]
                        cs_t = rtab["csq" if isq else "csk"]
                        ss_t = rtab["ssq" if isq else "ssk"]
                        csb_ = rtabb["csq" if isq else "csk"]
                        ssb_ = rtabb["ssq" if isq else "ssk"]
                        sccol = svv("qs" if isq else "ks")[:, l:l + 1]
                        scbuf = B("c_qs" if isq else "c_ks")
                        par = m % 2
                        rstdB, rstdBb = rstdBt[par], rstdBbt[par]
                        vstt(scr[0], psv(bank), sccol, rstdB, ALU.mult, ALU.mult,
                             [PB[bank], rstdBb, scbuf], [scrb[0]])
                        mm(psv(7), rperm, scr[0], [scrb[0], B("constD")], [PB[7]])
                        vstt(scr[1], psv(7), 1.0, ss_t, ALU.mult, ALU.mult,
                             [PB[7], ssb_], [scrb[1]])
                        vtt(scr[2], scr[0].bitcast(F32), cs_t, ALU.mult,
                            [scrb[0], csb_], [scrb[2]])
                        vtt(dstt, scr[2].bitcast(F32),
                            scr[1].bitcast(F32), ALU.add, [scrb[1], scrb[2]], [dsttb])
                    else:
                        mt = m - 8
                        act(vb[mt], psv(bank), AF.Copy, [PB[bank]], [vbb[mt]])
                        for hh in range(2):
                            h_ = 2 * mt + hh
                            for ts in range(4):
                                pv = psv(7, 32, 0, BF16)
                                trans(pv, vb[mt][64 * hh:64 * hh + 64,
                                                 128 * ts:128 * ts + 128],
                                      identb[64 * hh:64 * hh + 64,
                                             64 * hh:64 * hh + 64],
                                      [vbb[mt], B("constD")], [PB[7]])
                                act(vT[sl_c][4 * h_ + ts], pv, AF.Copy, [PB[7]],
                                    [vTb[sl_c][4 * h_ + ts]])

                ldw(qlw[0], wqkv[l][:, 0:128], 8, 128, [qlwb[0]])
                ldw(qlw[1], wqkv[l][:, 128:256], 8, 128, [qlwb[1]])
                for m in range(14):
                    if m < 12:
                        qkvmm(m)
                    if m + 2 < 12:
                        ldw(qlw[(m + 2) % 2], wqkv[l][:, 128 * (m + 2):128 * (m + 2) + 128],
                            8, 128, [qlwb[(m + 2) % 2]])
                    if 1 <= m <= 8:
                        qstats(m - 1)
                    if m >= 2:
                        qapply(m - 2)
                if c == 0 and l == 0:
                    tapw("qt0", qt[0], [qtb[0]])
                    tapw("kt0", kt[sl_c][0], [ktb[sl_c][0]])

                def kloc(h_, i):
                    mt, hh = h_ // 2, h_ % 2
                    sl = sl_p if i < 4 else sl_c
                    ii = i % 4
                    return (kt[sl][mt][64 * hh:64 * hh + 64,
                                       128 * ii:128 * ii + 128], ktb[sl][mt])

                def vloc(h_, i):
                    if i < 4:
                        return vT[sl_p][4 * h_ + i], vTb[sl_p][4 * h_ + i]
                    return vT[sl_c][4 * h_ + i - 4], vTb[sl_c][4 * h_ + i - 4]

                m2list = list(range(8)) if c > 0 else list(range(4, 8))
                sch.curlab = "ArowS"
                ldw(wlw[0], wod[l][:, 0:128], 4, 128, [wlwb[0]])
                ldw(wlw[1], wod[l][:, 128:256], 4, 128, [wlwb[1]])
                for h_ in range(NH):
                    mt, hh = h_ // 2, h_ % 2
                    qsl = qt[mt][64 * hh:64 * hh + 64, :]
                    for a_i in range(4):
                        dlist = list(range(0, 5)) if c > 0 else list(range(4 - a_i, 5))
                        for d_ in dlist:
                            kap, kbf = kloc(h_, a_i + d_)
                            needmask = (d_ == 0 and c > 0) or (d_ == 4)
                            mm(PS[:, 128 * d_:128 * d_ + 128],
                               qsl[:, 128 * a_i:128 * a_i + 128], kap,
                               [qtb[mt], kbf], [PB[0], PB[1]], start=True,
                               stop=not needmask, inc=(0 if needmask else 0))
                            if d_ == 0 and c > 0:
                                mm(PS[:, 0:128], identr, maskd0,
                                   [B("constD"), B("constD")], [PB[0], PB[1]],
                                   start=False, stop=True, inc=0)
                            if d_ == 4:
                                mm(PS[:, 512:640], identr, maskd4,
                                   [B("constD"), B("constD")], [PB[0], PB[1]],
                                   start=False, stop=True,
                                   inc=1)
                        dmin = dlist[0]
                        ngb = B(f"a_ng{h_}_{a_i}")
                        negm = svv("altmp")[:, 4 * h_ + a_i:4 * h_ + a_i + 1]
                        negm2 = svv("altmp")[:, 32 + a_i:33 + a_i]
                        if dmin < 4:
                            vmax(negm, PS[:, 128 * dmin:512], [PB[0], PB[1]],
                                 [ngb], negate=True)
                            vmax(negm2, PS[:, 512:640], [PB[0], PB[1]],
                                 [B(f"a_ng2{a_i}")], negate=True)
                            vtt(negm, negm, negm2, ALU.min,
                                [ngb, B(f"a_ng2{a_i}")], [ngb])
                        else:
                            vmax(negm, PS[:, 512:640], [PB[0], PB[1]],
                                 [ngb], negate=True)
                        st(rr[30 + h_:31 + h_, 128 * a_i:128 * a_i + 128], negm,
                           [ngb], writes=[B(f"rrg{h_}")], eng="P")
                for h_ in range(2):
                    ld(negrowt[h_][0:1, :], rr.bitcast(F32R)[30 + h_:31 + h_, 0:512],
                       [negrowbt[h_]], reads=[B(f"rrg{h_}")], eng="P")
                for v in range(NH + 2):
                    if v < NH:
                        sch.curlab = "AsT"
                        h_ = v
                        mt, hh = h_ // 2, h_ % 2
                        eST, eSTb = eSTt[h_ % 3], eSTbt[h_ % 3]
                        negrow, negrowb = negrowt[h_ % 2], negrowbt[h_ % 2]
                        qsl = qt[mt][64 * hh:64 * hh + 64, :]
                        for m2 in m2list:
                            bank = 2 + (m2 % 4)
                            kap, kbf = kloc(h_, m2)
                            mm(psv(bank), kap, qsl, [kbf, qtb[mt]],
                               [PB[bank]], start=True, stop=False, inc=0)
                            off = 896 - 128 * m2
                            mm(psv(bank), identr, maskTv[:, off:off + 512],
                               [B("constD"), B("constD")], [PB[bank]],
                               start=False, stop=False, inc=0)
                            mm(psv(bank), onesrow, negrow[0:1, :],
                               [negrowb, B("constD")], [PB[bank]],
                               start=False, stop=True, inc=1)
                            act(eST[m2], psv(bank), AF.Exp, [PB[bank]], [eSTb[m2]])
                            if c == 0 and l == 0 and h_ == 0 and m2 == 4:
                                tapw("es40", eST[4], [eSTb[4]])
                        if c == 0 and l == 0 and h_ == 0:
                            tapw("ng0", negrow[0:1, :], [negrowb])
                        if v + 2 < NH:
                            ld(negrowt[v % 2][0:1, :],
                               rr.bitcast(F32R)[32 + v:33 + v, 0:512],
                               [negrowbt[v % 2]], reads=[B(f"rrg{v + 2}")], eng="P")
                    if 1 <= v <= NH:
                        sch.curlab = "Adn"
                        hd = v - 1
                        eST, eSTb = eSTt[hd % 3], eSTbt[hd % 3]
                        dnrow, dnrowb = dnrowt[hd % 2], dnrowbt[hd % 2]
                        recipB, recipBb = recipBt[hd % 2], recipBbt[hd % 2]
                        dn = PS[0:1, 512 * 6:512 * 6 + 512]
                        for i, m2 in enumerate(m2list):
                            mm(dn, oneskb, eST[m2], [eSTb[m2], B("constD")],
                               [PB[6]], start=(i == 0), stop=(i == len(m2list) - 1),
                               inc=(1 if i == len(m2list) - 1 else 0))
                        vrecip(dnrow, dn, [PB[6]], [dnrowb])
                        st(rr[22 + hd:23 + hd, 0:512], dnrow.bitcast(F32), [dnrowb],
                           writes=[B(f"rrd{hd}")], eng="P")
                        ld(recipB, rr.bitcast(F32R)[22 + hd:23 + hd, 0:512]
                           .partition_broadcast(64),
                           [recipBb], reads=[B(f"rrd{hd}")], eng="P")
                    if v >= 2:
                        sch.curlab = "Aav"
                        ha = v - 2
                        mt, hh = ha // 2, ha % 2
                        eST, eSTb = eSTt[ha % 3], eSTbt[ha % 3]
                        recipB, recipBb = recipBt[ha % 2], recipBbt[ha % 2]
                        ao = PS[0:64, 512 * 7:512 * 7 + 512]
                        for i, m2 in enumerate(m2list):
                            vap, vbf = vloc(ha, m2)
                            mm(ao, vap, eST[m2], [vbf, eSTb[m2]], [PB[7]],
                               start=(i == 0), stop=(i == len(m2list) - 1),
                               inc=(1 if i == len(m2list) - 1 else 0))
                        vtt(att[mt][64 * hh:64 * hh + 64, :], ao, recipB,
                            ALU.mult, [PB[7], recipBb], [attb[mt]])
                        if c == 0 and l == 0 and ha == 1:
                            tapw("att00", att[0], [attb[0]])
                sch.curlab = "Awo"
                for m in range(DJ):
                    lw, lwb = wlw[m % 2], wlwb[m % 2]
                    bank = m % 4
                    for k in range(4):
                        mm(psv(bank), lw[:, 128 * k:128 * k + 128], att[k],
                           [lwb, attb[k]], [PB[bank]], start=(k == 0), stop=(k == 3),
                           inc=(1 if k == 3 else 0))
                    if m + 2 < DJ:
                        ldw(wlw[m % 2], wod[l][:, 128 * (m + 2):128 * (m + 2) + 128],
                            4, 128, [wlwb[m % 2]])
                    vstt(xmid[m], psv(bank), 1.0, xft[m][:, 1:513].bitcast(F32),
                         ALU.mult, ALU.add, [PB[bank], xfb[m]], [xmidb[m]])
                    st(dstdram[128 * m:128 * m + 128, t0:t0 + CH],
                       xmid[m].bitcast(F32), [xmidb[m]], writes=[B(f"xm{l}D")])
                if c == 0 and l == 0:
                    tapw("xm00", xmid[0], [xmidb[0]])

        BTT, BW1, BHN, BMU2 = 0, 8192, 9216, 17408
        BPR, BAG, BW2 = 0, 22528, 24576
        XBXM, XBRW = 0, 8256

        def ffn_geglu(l, srcdram, dstdram, final=False):
            xmt = [axv(XBXM + 1032 * j, 1032).bitcast(F32R) for j in range(DJ)]
            xmb = [B("bxmD") for j in range(DJ)]
            hnt = [arv(BHN + 1024 * j, 1024) for j in range(DJ)]
            hnb = [B(f"b_hn{j}") for j in range(DJ)]
            ttt = [arv(BTT + 1024 * j, 1024) for j in range(DJ)]
            ttb2 = [B(f"b_tt{j}") for j in range(DJ)]
            sqbb2 = ttb2[0]
            sqb = arv(BTT, 1024)
            sqbb = sqbb2
            prod = [abv(BPR + 1024 * p, 1024) for p in range(NPAIR)]
            prodb = [B(f"b_pr{p}") for p in range(NPAIR)]
            aev = abv(BAG, 1024)
            aevb = B("b_aev")
            gev = abv(BAG + 1024, 1024)
            gevb = B("b_gev")
            muBc = arv(BMU2, 1024)
            rsBc = arv(BMU2 + 1024, 1024)
            muBcb, rsBcb = B("b_muB"), B("b_rsB")
            w1lw = arv(BW1, 1024)
            w1lwb = B("b_w1lw")
            w2lw = abv(BW2, 2816)
            w2lwb = B("b_w2lw")
            mrB = AXS[0:1, XBRW:XBRW + 1024].bitcast(F32R)
            vrB = AXS[0:1, XBRW + 1024:XBRW + 2048].bitcast(F32R)
            rrB = AXS[0:1, XBRW + 2048:XBRW + 3072].bitcast(F32R)
            rowBb = [B("b_mr"), B("b_vr"), B("b_rr")]
            xl_t = hnt
            xl_b = hnb
            srcD = B(f"xm{l}D")
            dstD = B("xl0D")

            for s_ in range(NSC):
                sch.curlab = "Bload"
                t0 = SC * s_
                for j in range(DJ):
                    if s_ == 0:
                        ld(xmt[j][:, 0:1],
                           dram["zeros_c"].bitcast(F32R)[:, 0:1], [xmb[j]])
                        ld(xmt[j][:, 1:1025],
                           srcdram.bitcast(F32R)[128 * j:128 * j + 128, t0:t0 + SC],
                           [xmb[j]], reads=[srcD])
                    else:
                        ld(xmt[j][:, 0:1025],
                           srcdram.bitcast(F32R)[128 * j:128 * j + 128,
                                                 t0 - 1:t0 + SC],
                           [xmb[j]], reads=[srcD])

                def hap(j):
                    return xmt[j][:, 1:1025] if j < 4 else xmt[j][:, 0:1024]

                sch.curlab = "Bstat"
                for half in range(2):
                    cS = slice(512 * half, 512 * half + 512)
                    sA = PS[0:1, 512 * (4 + half):512 * (4 + half) + 512]
                    sQ = PS[0:1, 512 * (6 + half):512 * (6 + half) + 512]
                    for j in range(DJ):
                        mm(sA, svv("onesf"), hap(j)[:, cS].bitcast(F32),
                           [xmb[j], B("constD")],
                           [PB[4 + half]], start=(j == 0), stop=(j == DJ - 1),
                           inc=(1 if j == DJ - 1 else 0))
                    for j in range(DJ):
                        act(sqb[:, cS], hap(j)[:, cS], AF.Square, [xmb[j]], [sqbb])
                        mm(sQ, oneskr, sqb[:, cS], [sqbb, B("constD")],
                           [PB[6 + half]], start=(j == 0), stop=(j == DJ - 1),
                           inc=(1 if j == DJ - 1 else 0))
                    act(mrB[:, cS], sA, AF.Copy, [PB[4 + half]], [rowBb[0]],
                        scale=1.0 / D)
                    act(vrB[:, cS], sQ, AF.Copy, [PB[6 + half]], [rowBb[1]],
                        scale=1.0 / D)
                vtt(rrB, mrB, mrB, ALU.mult, [rowBb[0]], [rowBb[2]])
                vtt(vrB, vrB, rrB, ALU.subtract, [rowBb[1], rowBb[2]], [rowBb[1]])
                act(rrB, vrB, AF.Ln, [rowBb[1], B("epsc")], [rowBb[2]], bias=svv("epsc")[0:1, :])
                act(rrB, rrB, AF.Exp, [rowBb[2]], [rowBb[2]], scale=-0.5)
                st(rr[38:39, 0:SC], mrB.bitcast(F32), [rowBb[0]],
                   writes=[B("rrB38")], eng="P")
                st(rr[39:40, 0:SC], rrB.bitcast(F32), [rowBb[2]],
                   writes=[B("rrB38")], eng="P")
                ld(muBc, rr.bitcast(F32R)[38:39, 0:SC].partition_broadcast(128), [muBcb],
                   reads=[B("rrB38")], eng="P")
                ld(rsBc, rr.bitcast(F32R)[39:40, 0:SC].partition_broadcast(128), [rsBcb],
                   reads=[B("rrB38")], eng="P")
                sch.curlab = "Bln"
                for j in range(DJ):
                    vstt(ttt[j], hap(j).bitcast(F32), 1.0, muBc,
                         ALU.mult, ALU.subtract, [xmb[j], muBcb], [ttb2[j]])
                    vtt(hnt[j], ttt[j].bitcast(F32), rsBc, ALU.mult,
                        [ttb2[j], rsBcb], [hnb[j]])
                    vts2(ttt[j], hnt[j].bitcast(F32),
                         svv("lng")[:, 8 * l + j:8 * l + j + 1],
                         svv("lnb")[:, 8 * l + j:8 * l + j + 1], ALU.mult, ALU.add,
                         [hnb[j], B("constD"), B("constD")], [ttb2[j]])
                if s_ == 0 and l == 0:
                    tapw("hn0", ttt[0][:, 0:512], [ttb2[0]])
                sch.curlab = "Bw1"
                coffs = []
                off_ = 0
                for p in range(NPAIR):
                    wdt = min(128, HID - 128 * p)
                    coffs.append((off_, wdt))
                    off_ += 2 * wdt
                for p in range(NPAIR):
                    coff, wdt = coffs[p]
                    for ag in range(2):
                        ldw(w1lw[:, 0:8 * wdt],
                            w1p[l][:, coff + ag * wdt:coff + ag * wdt + wdt],
                            8, wdt, [w1lwb])
                        for half in range(2):
                            bank = 2 * ag + half
                            cS = slice(512 * half, 512 * half + 512)
                            for k in range(DJ):
                                mm(PS[0:wdt, 512 * bank:512 * bank + 512],
                                   w1lw[:, wdt * k:wdt * k + wdt], ttt[k][:, cS],
                                   [w1lwb, ttb2[k]], [PB[bank]], start=(k == 0),
                                   stop=(k == DJ - 1), inc=(1 if k == DJ - 1 else 0))
                    bca = svv("b1c")[0:wdt, 2 * NPAIR * l + 2 * p:
                                     2 * NPAIR * l + 2 * p + 1]
                    bcg = svv("b1c")[0:wdt, 2 * NPAIR * l + 2 * p + 1:
                                     2 * NPAIR * l + 2 * p + 2]
                    for half in range(2):
                        cS = slice(512 * half, 512 * half + 512)
                        act(aev[0:wdt, cS], PS[0:wdt, 512 * half:512 * half + 512],
                            AF.Copy, [PB[half]], [aevb])
                        act(gev[0:wdt, cS],
                            PS[0:wdt, 1024 + 512 * half:1024 + 512 * half + 512],
                            AF.Gelu, [PB[2 + half]], [gevb], bias=bcg)
                    vstt(prod[p][0:wdt, :], aev[0:wdt, :], bca, gev[0:wdt, :],
                         ALU.add, ALU.mult, [aevb, gevb, B("constD")], [prodb[p]])
                sch.curlab = "Bw2"
                for m in range(DJ):
                    ldw(w2lw[:, 0:21 * 128], w2d[l][0:2688, 128 * m:128 * m + 128],
                        21, 128, [w2lwb], eng="P")
                    ld(w2lw[0:RAG, 21 * 128:22 * 128],
                       w2d[l][2688:2730, 128 * m:128 * m + 128], [w2lwb],
                       reads=[w2lwb], eng="P")
                    for half in range(2):
                        bank = 4 + 2 * (m % 2) + half
                        cS = slice(512 * half, 512 * half + 512)
                        for kp in range(NPAIR):
                            wdt = min(128, HID - 128 * kp)
                            mm(psv(bank), w2lw[0:wdt, 128 * kp:128 * kp + 128],
                               prod[kp][0:wdt, cS], [w2lwb, prodb[kp]], [PB[bank]],
                               start=(kp == 0), stop=(kp == NPAIR - 1),
                               inc=(1 if kp == NPAIR - 1 else 0))
                        vstt(xl_t[m][:, cS], psv(bank),
                             svv("b2c")[:, 8 * l + m:8 * l + m + 1],
                             xmt[m][:, 1:1025].bitcast(F32)[:, cS], ALU.add, ALU.add,
                             [PB[bank], xmb[m], B("constD")], [xl_b[m]])
                    if not final:
                        st(dstdram[128 * m:128 * m + 128, t0:t0 + SC],
                           xl_t[m].bitcast(F32), [xl_b[m]], writes=[dstD])
                if s_ == 0 and l == 0:
                    tapw("xl00", xl_t[0].bitcast(F32)[:, 0:512], [xl_b[0]])
                if final:
                    sch.curlab = "Bfin"
                    x1f = [axv(XBXM + 1024 * j, 1024).bitcast(F32R) for j in range(DJ)]
                    x1fb = [B("bxmD") for j in range(DJ)]
                    for j in range(DJ):
                        ld(x1f[j], x1s.bitcast(F32R)[128 * j:128 * j + 128,
                                                         t0:t0 + SC],
                           [B("bxmD")], reads=[B("x1sD")])
                    for half in range(2):
                        cS = slice(512 * half, 512 * half + 512)
                        sA = PS[0:1, 512 * half:512 * half + 512]
                        sQ = PS[0:1, 1024 + 512 * half:1024 + 512 * half + 512]
                        for j in range(DJ):
                            mm(sA, oneskr, xl_t[j][:, cS], [xl_b[j], B("constD")],
                               [PB[half]], start=(j == 0), stop=(j == DJ - 1),
                               inc=(1 if j == DJ - 1 else 0))
                        for j in range(DJ):
                            act(sqb[:, cS], xl_t[j][:, cS], AF.Square, [xl_b[j]],
                                [sqbb])
                            mm(sQ, oneskr, sqb[:, cS], [sqbb, B("constD")],
                               [PB[2 + half]], start=(j == 0), stop=(j == DJ - 1),
                               inc=(1 if j == DJ - 1 else 0))
                        act(mrB[:, cS], sA, AF.Copy, [PB[half]], [rowBb[0]],
                            scale=1.0 / D)
                        act(vrB[:, cS], sQ, AF.Copy, [PB[2 + half]], [rowBb[1]],
                            scale=1.0 / D)
                    vtt(rrB, mrB, mrB, ALU.mult, [rowBb[0]], [rowBb[2]])
                    vtt(vrB, vrB, rrB, ALU.subtract, [rowBb[1], rowBb[2]], [rowBb[1]])
                    act(rrB, vrB, AF.Ln, [rowBb[1], B("epsc")], [rowBb[2]], bias=svv("epsc")[0:1, :])
                    act(rrB, rrB, AF.Exp, [rowBb[2]], [rowBb[2]], scale=-0.5)
                    st(rr[40:41, 0:SC], mrB.bitcast(F32), [rowBb[0]],
                       writes=[B("rrB40")], eng="P")
                    st(rr[41:42, 0:SC], rrB.bitcast(F32), [rowBb[2]],
                       writes=[B("rrB40")], eng="P")
                    ld(muBc, rr.bitcast(F32R)[40:41, 0:SC].partition_broadcast(128), [muBcb],
                       reads=[B("rrB40")], eng="P")
                    ld(rsBc, rr.bitcast(F32R)[41:42, 0:SC].partition_broadcast(128), [rsBcb],
                       reads=[B("rrB40")], eng="P")
                    tf = abv(BAG, 2048, F32R)
                    tfb = aevb
                    o1 = arv(BW1, 1024)
                    o1b = w1lwb
                    ot = axv(XBRW, 1024).bitcast(F32R)
                    otb = rowBb[0]
                    for j in range(DJ):
                        vstt(tf, xl_t[j].bitcast(F32), 1.0, muBc, ALU.mult,
                             ALU.subtract, [xl_b[j], muBcb], [tfb, gevb])
                        vtt(o1, tf, rsBc, ALU.mult, [tfb, rsBcb], [o1b])
                        vtt(ot, o1, x1f[j].bitcast(F32), ALU.add,
                            [o1b, x1fb[j]], [otb])
                        st(out[128 * j:128 * j + 128, t0:t0 + SC],
                           ot.bitcast(F32), [otb], writes=[B("outD")])

        ph = trunc.get("ph", 5) if trunc else 5
        if ph >= 1:
            sch.mark("A1")
            ffn_attn(0, x1s, xms[0])
            sch.barrier()
        if ph >= 2:
            sch.mark("B1")
            ffn_geglu(0, xms[0], xl0)
            sch.barrier()
        if ph >= 3:
            sch.mark("A2")
            ffn_attn(1, xl0, xms[1])
            sch.barrier()
        if ph >= 4:
            sch.mark("B2")
            ffn_geglu(1, xms[1], None, final=True)
            sch.barrier()
        for sem, v in sch.cnt.items():
            if v > 0:
                sch.items["S"].append(("w", sem, v))
        global _LAST_COUNTS, _LAST_LABS
        _LAST_COUNTS = dict(sch.cnt)
        _LAST_LABS = {e: list(v) for e, v in sch.labs.items()}

        for semname, v in sch.cnt.items():
            if v > 0 or semname in Sched.ENGS:
                sems[semname] = es.enter_context(nc.semaphore(
                    semname.replace(":", "_")))
        assert len(sems) <= 200, f"too many semaphores: {len(sems)}"
        block = es.enter_context(nc.Block())

        def replay(E, ename):
            with nc.allow_non_contiguous_dma(
                    reason="1-column lead/zero loads are tiny and rare"), \
                 nc.allow_low_precision(
                    reason="float32r views are fp32-storage; PSUM accumulation fp32"):
                cur = None
                for it in sch.items[ename]:
                    if it[0] == "m":
                        if cur is not None:
                            nc.leave_named_scope(cur[0], cur[1], False)
                        sid, _ = nc.enter_named_scope(it[1], False)
                        cur = (it[1], sid)
                    elif it[0] == "w":
                        E.wait_ge(sems[it[1]], it[2])
                    else:
                        _, fn, sem, inc = it
                        ins = fn(E)
                        if inc > 0:
                            ins.then_inc(sems[sem], inc)
                if cur is not None:
                    nc.leave_named_scope(cur[0], cur[1], False)

        @block.tensor
        def _(eng):
            replay(eng, "T")

        @block.vector
        def _(eng):
            replay(eng, "V")

        @block.scalar
        def _(eng):
            replay(eng, "A")

        @block.gpsimd
        def _(eng):
            replay(eng, "P")

        @block.sync
        def _(eng):
            replay(eng, "S")

    return nc


def kernel(**inputs):
    taps = dict(_TAPS)
    consts = host_constants()
    hw = host_layout(inputs)
    nc = build(taps)
    X = np.asarray(inputs["L0_attn_x"])
    ncores = 8
    shared = {}
    shared.update({k: np.ascontiguousarray(v) for k, v in consts.items()})
    shared.update(hw)
    in_maps = []
    for b in range(ncores):
        m = dict(shared)
        m["x"] = np.ascontiguousarray(X[b].T)
        in_maps.append(m)
    res = run_bass_kernel_spmd(nc, in_maps, core_ids=list(range(ncores)))
    outs = np.stack([np.asarray(r["out"]).T for r in res.results], 0)
    if taps:
        kernel.last_taps = [{nm: np.asarray(r["tap_" + nm]) for nm in taps}
                            for r in res.results]
    return outs.astype(np.float32)


# version 04
# speedup vs baseline: 1.2820x; 1.1343x over previous
import numpy as np
import ml_dtypes
import contextlib
import concourse.bass as bass
import concourse.mybir as mybir
from concourse.bass_utils import run_bass_kernel_spmd

F32 = mybir.dt.float32
F32R = mybir.dt.float32r
BF16 = mybir.dt.bfloat16
AF = mybir.ActivationFunctionType
ALU = mybir.AluOpType
AX = mybir.AxisListType

T, D, DIN, DST, DCONV, DTR = 4096, 1024, 2048, 16, 4, 64
NH, DH, WIN, HID = 8, 64, 512, 2730
CH, SC = 512, 1024
NCH, NSC = T // CH, T // SC
DT_J, DJ = DIN // 128, D // 128
NPAIR = (HID + 127) // 128
RAG = HID - 128 * (NPAIR - 1)
NEG = -50000.0
EPS = 1e-5

_TAPS = {}
_LAST_LABS = {}
_LAST_COUNTS = {}


def host_constants():
    c = {}
    c["ident_b"] = np.eye(128, dtype=ml_dtypes.bfloat16)
    c["ident_r"] = np.eye(128, dtype=np.float32)
    P = np.zeros((128, 128), np.float32)
    for h0 in (0, 64):
        for i in range(32):
            P[h0 + i, h0 + i + 32] = -1.0
            P[h0 + i + 32, h0 + i] = 1.0
    c["rperm_r"] = np.ascontiguousarray(P.T)
    c["onesrow_r"] = np.ones((1, 128), np.float32)
    c["oneskcol_r"] = np.ones((128, 1), np.float32)
    c["oneskcol_b"] = np.ones((128, 1), dtype=ml_dtypes.bfloat16)
    ho = np.zeros((128, 2), np.float32)
    ho[0:64, 0] = 1.0
    ho[64:128, 1] = 1.0
    c["headones_r"] = ho
    qq = np.arange(128)[:, None]
    cc = np.arange(128)[None, :]
    c["mask_d0"] = np.where(qq <= cc, 0.0, NEG).astype(np.float32)
    c["mask_d4"] = np.where(qq >= cc, 0.0, NEG).astype(np.float32)
    s = np.arange(1408)[None, :]
    c["maskT"] = np.where((s - qq >= 384) & (s - qq <= 896), 0.0, NEG).astype(np.float32)
    hi = np.arange(0, DH, 2, dtype=np.float64)
    inv = 1.0 / 10000.0 ** (hi / DH)
    t = np.arange(T, dtype=np.float64)
    fr = np.concatenate([t[:, None] * inv[None, :]] * 2, 1)
    base = (hi + 0.4 * DH) / (1.4 * DH)
    power = (t - T // 2) / (WIN // 2)
    scl = np.concatenate([base[None, :] ** power[:, None]] * 2, 1)
    cos, sin = np.cos(fr), np.sin(fr)
    for nm, tab in (("csq", 8.0 * cos * scl), ("ssq", 8.0 * sin * scl),
                    ("csk", cos / scl), ("ssk", sin / scl)):
        tt = tab.T.astype(np.float32)
        c[nm] = np.ascontiguousarray(np.concatenate([tt, tt], 0))
    c["zeros_c"] = np.zeros((128, 4), np.float32)
    return c


def host_layout(inp):
    f = np.ascontiguousarray
    h = {}
    h["inwT"] = f(inp["L0_attn_in_proj_w"].T)
    h["outwT"] = f(inp["L0_attn_out_proj_w"].T)
    h["xpwT"] = f(inp["L0_attn_x_proj_w"].T)
    h["dtwT"] = f(inp["L0_attn_dt_proj_w"].T)
    cw = inp["L0_attn_conv_w"]
    cd = np.zeros((DT_J, DCONV, 128, 128), np.float32)
    for j in range(DT_J):
        for k in range(DCONV):
            np.fill_diagonal(cd[j, k], cw[128 * j:128 * (j + 1), k])
    h["cdiag"] = cd
    dd = np.zeros((DT_J, 128, 128), np.float32)
    for j in range(DT_J):
        np.fill_diagonal(dd[j], inp["L0_attn_Dp"][128 * j:128 * (j + 1)])
    h["ddiag"] = dd
    h["convb2"] = f(inp["L0_attn_conv_b"].reshape(DT_J, 128).T)
    h["dtb2"] = f(inp["L0_attn_dt_proj_b"].reshape(DT_J, 128).T)
    h["alog2"] = f(inp["L0_attn_A_log"].reshape(DT_J, 128, DST)
                   .transpose(1, 0, 2).reshape(128, DT_J * DST))
    for l in range(2):
        h[f"wqkv{l}"] = f(inp["L0_ffn_wqkv"][l])
        h[f"wo{l}"] = f(inp["L0_ffn_wo"][l])
        h[f"qs{l}"] = f(np.tile(inp["L0_ffn_q_scale"][l], 2)[:, None])
        h[f"ks{l}"] = f(np.tile(inp["L0_ffn_k_scale"][l], 2)[:, None])
        h[f"lng{l}"] = f(inp["L0_ffn_ln_g"][l].reshape(DJ, 128).T)
        h[f"lnb{l}"] = f(inp["L0_ffn_ln_b"][l].reshape(DJ, 128).T)
        h[f"b2t{l}"] = f(inp["L0_ffn_b2"][l].reshape(DJ, 128).T)
        w1 = inp["L0_ffn_w1"][l]
        b1 = inp["L0_ffn_b1"][l]
        perm = []
        bt = np.zeros((128, 2 * NPAIR), np.float32)
        for p in range(NPAIR):
            wdt = min(128, HID - 128 * p)
            ia = np.arange(128 * p, 128 * p + wdt)
            perm.extend(ia)
            perm.extend(ia + HID)
            bt[:wdt, 2 * p] = b1[ia]
            bt[:wdt, 2 * p + 1] = b1[ia + HID]
        h[f"w1p{l}"] = f(w1[:, np.array(perm)])
        h[f"b1t{l}"] = bt
        h[f"w2_{l}"] = f(inp["L0_ffn_w2"][l])
    return h


class Buf:
    __slots__ = ("w", "r", "name")

    def __init__(self, name=""):
        self.w, self.r, self.name = None, {}, name


class Sched:
    ENGS = ("T", "V", "A", "P", "S")
    SELF_WAIT = ("A", "P", "V")

    def __init__(self):
        self.cnt = {e: 0 for e in self.ENGS}
        self.seen = {e: {} for e in self.ENGS}
        self.items = {e: [] for e in self.ENGS}
        self.curlab = ""
        self.labs = {e: [] for e in self.ENGS}

    def mark(self, name):
        for e in self.ENGS:
            self.items[e].append(("m", name))
            self.labs[e].append("mark")

    def _wait(self, e, sem, v):
        if v > self.seen[e].get(sem, 0):
            self.items[e].append(("w", sem, v))
            self.labs[e].append("w." + self.curlab + ":" + sem)
            self.seen[e][sem] = v

    def op(self, e, fn, reads=(), writes=(), inc=1, dma=False):
        if dma:
            assert writes, "DMA needs a write target buf"
            sem = "D:" + writes[0].name
            self.cnt.setdefault(sem, 0)
        else:
            sem = e
        deps = {}
        for b in reads:
            if b.w is not None:
                deps[b.w[0]] = max(deps.get(b.w[0], 0), b.w[1])
        for b in writes:
            if b.w is not None:
                deps[b.w[0]] = max(deps.get(b.w[0], 0), b.w[1])
            for s2, v in b.r.items():
                deps[s2] = max(deps.get(s2, 0), v)
        for s2, v in deps.items():
            if s2 == e:
                if e in self.SELF_WAIT:
                    self._wait(e, s2, v)
                continue
            self._wait(e, s2, v)
        self.cnt[sem] += inc
        self.items[e].append(("i", fn, sem, inc))
        self.labs[e].append("i." + self.curlab)
        rmark = self.cnt[sem] + (1 if inc == 0 and not dma else 0)
        for b in reads:
            b.r[sem] = max(b.r.get(sem, 0), rmark)
        if inc > 0:
            for b in writes:
                b.w = (sem, self.cnt[sem])
                b.r = {}

    def barrier(self):
        tgt = dict(self.cnt)
        for e in self.ENGS:
            for sem, v in tgt.items():
                if sem != e and v > 0:
                    self._wait(e, sem, v)


def build(taps=None, trunc=None):
    taps = taps or {}
    nc = bass.Bass("TRN2", target_bir_lowering=False, debug=False)
    dram = {}

    def din(name, shape, dt):
        dram[name] = nc.dram_tensor(name, shape, dt, kind="ExternalInput").ap()
        return dram[name]

    def dscr(name, shape, dt):
        dram[name] = nc.dram_tensor(name, shape, dt, kind="Internal").ap()
        return dram[name]

    x_in = din("x", [D, T], F32R)
    inwT = din("inwT", [D, 2 * DIN], F32)
    outwT = din("outwT", [DIN, D], F32)
    xpwT = din("xpwT", [DIN, DTR + 2 * DST], F32R)
    dtwT = din("dtwT", [DTR, DIN], F32R)
    cdiagd = din("cdiag", [DT_J, DCONV, 128, 128], F32)
    ddiagd = din("ddiag", [DT_J, 128, 128], F32)
    convb2 = din("convb2", [128, DT_J], F32)
    dtb2 = din("dtb2", [128, DT_J], F32)
    alog2 = din("alog2", [128, DT_J * DST], F32)
    wqkv = [din(f"wqkv{l}", [D, 3 * 512], F32R) for l in range(2)]
    wod = [din(f"wo{l}", [512, D], F32R) for l in range(2)]
    qs2 = [din(f"qs{l}", [128, 1], F32) for l in range(2)]
    ks2 = [din(f"ks{l}", [128, 1], F32) for l in range(2)]
    lng2 = [din(f"lng{l}", [128, DJ], F32) for l in range(2)]
    lnb2 = [din(f"lnb{l}", [128, DJ], F32) for l in range(2)]
    b2t2 = [din(f"b2t{l}", [128, DJ], F32) for l in range(2)]
    w1p = [din(f"w1p{l}", [D, 2 * HID], F32R) for l in range(2)]
    b1t2 = [din(f"b1t{l}", [128, 2 * NPAIR], F32) for l in range(2)]
    w2d = [din(f"w2_{l}", [HID, D], F32) for l in range(2)]
    cmeta = {"ident_b": ([128, 128], BF16), "ident_r": ([128, 128], F32R),
             "rperm_r": ([128, 128], F32R), "onesrow_r": ([1, 128], F32R),
             "oneskcol_r": ([128, 1], F32R), "oneskcol_b": ([128, 1], BF16),
             "headones_r": ([128, 2], F32R), "mask_d0": ([128, 128], F32R),
             "mask_d4": ([128, 128], F32R), "maskT": ([128, 1408], F32R),
             "csq": ([128, T], F32), "ssq": ([128, T], F32),
             "csk": ([128, T], F32), "ssk": ([128, T], F32),
             "zeros_c": ([128, 4], F32)}
    for nm, (shape, dt) in cmeta.items():
        din(nm, shape, dt)

    rowsB = dscr("rowsB", [NCH, DST, CH], F32)
    rowsC = dscr("rowsC", [NCH, DST, CH], BF16)
    rr = dscr("rr", [48, SC], F32)
    x1s = dscr("x1s", [D, T], F32)
    xms = [dscr(f"xm{l}", [D, T], F32) for l in range(2)]
    xl0 = dscr("xl0", [D, T], F32)
    out = nc.dram_tensor("out", [D, T], F32, kind="ExternalOutput").ap()
    dram["out"] = out
    tap_out = {nm: nc.dram_tensor("tap_" + nm, sh, dt, kind="ExternalOutput").ap()
               for nm, (sh, dt) in taps.items()}

    sch = Sched()
    es = contextlib.ExitStack()
    with es:
        AR = es.enter_context(nc.sbuf_tensor("AR", [128, 19456], F32R))
        AB = es.enter_context(nc.sbuf_tensor("AB", [128, 29824], BF16))
        AXS = es.enter_context(nc.sbuf_tensor("AX", [128, 11328], F32))
        SM = es.enter_context(nc.sbuf_tensor("SM", [128, 1024], F32))
        CST = es.enter_context(nc.sbuf_tensor("CST", [128, 3712], F32R))
        CSB = es.enter_context(nc.sbuf_tensor("CSB", [128, 256], BF16))
        DTT = es.enter_context(nc.sbuf_tensor("DTT", [64, 2048], F32R))
        PS = es.enter_context(nc.psum_tensor("PS", [128, 4096], F32))
        sems = {}
        block = None

        bufs = {}

        def B(name):
            if name not in bufs:
                bufs[name] = Buf(name)
            return bufs[name]

        CVo, _co = {}, [0]

        def _cal(nm, n):
            CVo[nm] = _co[0]
            _co[0] += n

        for nm_, n_ in [("maskd0", 128), ("maskd4", 128), ("maskT", 1408),
                        ("identr", 128), ("rperm", 128), ("onesrow", 128),
                        ("oneskr", 1), ("headones", 2), ("xpt", 96 * DT_J)]:
            _cal(nm_, n_)
        assert _co[0] <= 3712, _co[0]

        def cvv(nm, n):
            return CST[:, CVo[nm]:CVo[nm] + n]

        maskd0, maskd4, maskTv = cvv("maskd0", 128), cvv("maskd4", 128), cvv("maskT", 1408)
        identr, rperm = cvv("identr", 128), cvv("rperm", 128)
        onesrow = cvv("onesrow", 128)[0:1, 0:128]
        oneskr, headones = cvv("oneskr", 1), cvv("headones", 2)
        XPT = cvv("xpt", 96 * DT_J)
        identb, oneskb = CSB[:, 0:128], CSB[:, 128:129]

        SVdef = [("negA", 256), ("convb", 16), ("dtb", 16), ("hcarry", 256),
                 ("qs", 2), ("ks", 2), ("lng", 16), ("lnb", 16), ("b2c", 16),
                 ("b1c", 2 * NPAIR * 2), ("altmp", 256), ("sp1", 16), ("xit", 24), ("epsc", 1), ("onesf", 1)]
        so = {nm: sum(n for _, n in SVdef[:i]) for i, (nm, _) in enumerate(SVdef)}
        assert sum(n for _, n in SVdef) <= 1024

        def svv(nm):
            n = dict(SVdef)[nm]
            return SM[:, so[nm]:so[nm] + n]

        def ld(dst, src, writes, reads=(), eng="S"):
            sch.op(eng, (lambda d, s: (lambda E: E.dma_start(out=d, in_=s)))(dst, src),
                   reads=reads, writes=writes, inc=16, dma=True)

        def st(dst, src, reads, writes=(), eng="S"):
            wr = list(writes) if writes else [B("stsink")]
            sch.op(eng, (lambda d, s: (lambda E: E.dma_start(out=d, in_=s)))(dst, src),
                   reads=reads, writes=wr, inc=16, dma=True)

        def ldw(lw2d, src2d, nk, ncc, writes, reads=(), eng="S"):
            ld(lw2d.rearrange("p (k c) -> p k c", k=nk),
               src2d.rearrange("(k p) c -> p k c", p=128), writes, reads, eng)

        def mm(ps_ap, lhsT, rhs, reads, writes, start=True, stop=True, inc=1):
            sch.op("T", (lambda o_, a_, b_, s_, p_:
                         (lambda E: E.matmul(o_, a_, b_, start=s_, stop=p_)))
                   (ps_ap, lhsT, rhs, start, stop), reads=reads, writes=writes, inc=1)

        def act(out_ap, in_ap, fn, reads, writes, bias=0.0, scale=1.0, accum=None):
            sch.op("A", (lambda o_, i_, f_, bi, s_, ac:
                         (lambda E: E.activation(o_, i_, f_, bias=bi, scale=s_,
                                                 accum_out=ac)))
                   (out_ap, in_ap, fn, bias, scale, accum), reads=reads, writes=writes)

        def vtt(out_ap, a_ap, b_ap, op, reads, writes, eng="V"):
            sch.op(eng, (lambda o_, a_, b_, p_: (lambda E: E.tensor_tensor(o_, a_, b_, p_)))
                   (out_ap, a_ap, b_ap, op), reads=reads, writes=writes)

        def vstt(out_ap, in0, scal, in1, op0, op1, reads, writes, eng="V"):
            sch.op(eng, (lambda o_, a_, s_, b_, p0, p1:
                         (lambda E: E.scalar_tensor_tensor(o_, a_, s_, b_, p0, p1)))
                   (out_ap, in0, scal, in1, op0, op1), reads=reads, writes=writes)

        def vts2(out_ap, in0, s1, s2, op0, op1, reads, writes, eng="V"):
            sch.op(eng, (lambda o_, a_, y1, y2, p0, p1:
                         (lambda E: E.tensor_scalar(o_, a_, y1, y2, p0, p1)))
                   (out_ap, in0, s1, s2, op0, op1), reads=reads, writes=writes)

        def vts1(out_ap, in0, s1, op, reads, writes, eng="V"):
            sch.op(eng, (lambda o_, a_, y1, p_:
                         (lambda E: E.tensor_single_scalar(o_, a_, y1, p_)))
                   (out_ap, in0, s1, op), reads=reads, writes=writes)

        def vcopy(out_ap, in_ap, reads, writes, eng="V"):
            sch.op(eng, (lambda o_, i_: (lambda E: E.tensor_copy(o_, i_)))(out_ap, in_ap),
                   reads=reads, writes=writes)

        def vscan(out_ap, a_ap, b_ap, init, reads, writes):
            sch.op("V", (lambda o_, a_, b_, i_:
                         (lambda E: E.tensor_tensor_scan(o_, a_, b_, i_, ALU.mult, ALU.add)))
                   (out_ap, a_ap, b_ap, init), reads=reads, writes=writes)

        def vmax(out_ap, in_ap, reads, writes, negate=False):
            sch.op("V", (lambda o_, i_, n_: (lambda E: E.reduce_max(o_, i_, AX.X, negate=n_)))
                   (out_ap, in_ap, negate), reads=reads, writes=writes)

        def vrecip(out_ap, in_ap, reads, writes):
            sch.op("V", (lambda o_, i_: (lambda E: E.reciprocal(o_, i_)))(out_ap, in_ap),
                   reads=reads, writes=writes)

        def vmemset(ap, v, writes, eng="V"):
            sch.op(eng, (lambda a_, v_: (lambda E: E.memset(a_, v_)))(ap, v), writes=writes)

        def trans(ps_ap, in_ap, id_ap, reads, writes):
            sch.op("T", (lambda o_, i_, d_: (lambda E: E.transpose(o_, i_, d_)))
                   (ps_ap, in_ap, id_ap), reads=reads, writes=writes)

        PB = [B(f"ps{i}") for i in range(8)]

        def psv(bank, n=512, off=0, dt=None, p0=0, pn=128):
            ap = PS[p0:p0 + pn, 512 * bank + off:512 * bank + off + n]
            return ap.bitcast(dt) if dt else ap

        def tapw(nm, src_ap, rbufs):
            if nm in taps:
                want = taps[nm][1]
                if src_ap.dtype != want:
                    src_ap = src_ap.bitcast(want)
                st(tap_out[nm][:, :], src_ap, rbufs, writes=[B("tap" + nm)])

        def arv(off, n, dt=None):
            assert off + n <= 19456, (off, n)
            ap = AR[:, off:off + n]
            return ap.bitcast(dt) if dt else ap

        def abv(off, n, dt=None):
            assert off + n <= 29824, (off, n)
            ap = AB[:, off:off + n]
            return ap.bitcast(dt) if dt else ap

        def axv(off, n):
            assert off + n <= 11328, (off, n)
            return AXS[:, off:off + n]

        def row0(off, n):
            return AR[0:1, off:off + n]

        for nm in ("ident_b", "oneskcol_b", "oneskcol_r", "headones_r", "rperm_r",
                   "mask_d0", "mask_d4", "maskT", "ident_r"):
            tgt = {"ident_b": identb, "oneskcol_b": oneskb, "oneskcol_r": oneskr,
                   "headones_r": headones, "rperm_r": rperm, "mask_d0": maskd0,
                   "mask_d4": maskd4, "maskT": maskTv, "ident_r": identr}[nm]
            ld(tgt, dram[nm][:, :], [B("constD")])
        ld(onesrow, dram["onesrow_r"][0:1, :], [B("constD")])
        ld(svv("convb"), convb2[:, :], [B("constD")])
        ld(svv("dtb"), dtb2[:, :], [B("constD")])
        for l in range(2):
            ld(svv("qs")[:, l:l + 1], qs2[l][:, :], [B("constD")])
            ld(svv("ks")[:, l:l + 1], ks2[l][:, :], [B("constD")])
            ld(svv("lng")[:, 8 * l:8 * l + 8], lng2[l][:, :], [B("constD")])
            ld(svv("lnb")[:, 8 * l:8 * l + 8], lnb2[l][:, :], [B("constD")])
            ld(svv("b2c")[:, 8 * l:8 * l + 8], b2t2[l][:, :], [B("constD")])
            ld(svv("b1c")[:, 2 * NPAIR * l:2 * NPAIR * (l + 1)], b1t2[l][:, :],
               [B("constD")])
        ld(svv("altmp"), alog2[:, :], [B("constD")])
        act(svv("negA"), svv("altmp"), AF.Exp, [B("constD")], [B("negA")])
        vts1(svv("negA"), svv("negA"), -1.0, ALU.mult, [B("negA")], [B("negA")])
        for k in range(DT_J):
            ld(XPT[:, 96 * k:96 * k + 96], xpwT[128 * k:128 * k + 128, :], [B("constD")])
        ld(DTT[:, :], dtwT[:, :], [B("constD")])
        vmemset(svv("hcarry"), 0.0, [B("hcarry")])
        vmemset(svv("epsc"), EPS, [B("epsc")])
        vmemset(svv("onesf"), 1.0, [B("constD")])

        RXb = [B(f"RX{i}") for i in range(16)]
        XIb = [B(f"XI{i}") for i in range(8)]
        rx = lambda i: arv(512 * i, 512)
        xi_t = [abv(520 * j, 520) for j in range(DT_J)]
        y3_t = [abv(512 * j, 512) for j in range(DT_J)]

        XIr = B("XIr")

        def xib(j):
            return XIr

        def y3b(j):
            return XIr

        dtlo = arv(8192, 512)
        dtlo_b = B("dtlo")
        bcT = arv(15360, 512)
        bc_b = B("bcT")
        ddiag_t = arv(18944, 128)
        ddiag_b = B("ddiagb")
        dt_t = [arv(14336 + 512 * i, 512) for i in range(2)]
        dt_b = [B(f"dt{i}") for i in range(2)]
        u_t = [abv(8320 + 512 * i, 512) for i in range(2)]
        u_b = [B(f"u{i}") for i in range(2)]
        a_t = [arv(12800 + 512 * i, 512) for i in range(3)]
        a_b = [B(f"a{i}") for i in range(3)]
        b_t = [axv(8704 + 512 * i, 512).bitcast(F32R) for i in range(3)]
        b_b = [B(f"b{i}") for i in range(3)]
        w_t = [abv(9344 + 512 * i, 512) for i in range(3)]
        w_b = [B(f"w{i}") for i in range(3)]
        cr_t = [abv(17024 + 512 * s, 512) for s in range(DST)]
        cr_b = [B("crD") for s in range(DST)]
        hb_t = [abv(10880 + 512 * i, 512) for i in range(4)]
        hb_b = [B(f"hb{i}") for i in range(4)]
        xlnB = [abv(12928 + 512 * j, 512) for j in range(DJ)]
        xln_b = [B(f"xln{j}") for j in range(DJ)]
        br_t = [axv(512 + 512 * s, 512) for s in range(DST)]
        br_b = [B("brD") for s in range(DST)]
        cdiag_t = abv(29312, 512)
        outm_t = [abv(27264, 2048)]
        outm_b = [[B("outm0")]]

        xch = [arv(8704 + 512 * j, 512) for j in range(DJ)]
        xch_b = [B("xchD") for j in range(DJ)]
        inm_t = [abv(25216 + 1024 * i, 1024) for i in range(2)]
        inm_b = [B(f"inm{i}") for i in range(2)]
        muB, rsB = arv(16896, 512), arv(17408, 512)
        g_t = [arv(17920, 512), arv(18432, 512)]
        sp1, sp2 = arv(15872, 512), arv(16384, 512)
        mrow, vrow, rrow = row0(512 * 9, 512), row0(512 * 10, 512), row0(512 * 11, 512)
        rowb = [RXb[9], RXb[10], RXb[11]]
        rrD = B("rrD")
        rowsBD, rowsCD = B("rowsBD"), B("rowsCD")

        def ln_rows(sumA, sumQ, bA, bQ, n, nrm, slot, dq="S"):
            act(mrow[:, 0:n], sumA, AF.Copy, [bA], [rowb[0]], scale=1.0 / nrm)
            act(vrow[:, 0:n], sumQ, AF.Copy, [bQ], [rowb[1]], scale=1.0 / nrm)
            vtt(rrow[:, 0:n], mrow[:, 0:n], mrow[:, 0:n], ALU.mult, [rowb[0]], [rowb[2]])
            vtt(vrow[:, 0:n], vrow[:, 0:n], rrow[:, 0:n], ALU.subtract,
                [rowb[1], rowb[2]], [rowb[1]])
            act(rrow[:, 0:n], vrow[:, 0:n], AF.Ln, [rowb[1], B("epsc")], [rowb[2]], bias=svv("epsc")[0:1, :])
            act(rrow[:, 0:n], rrow[:, 0:n], AF.Exp, [rowb[2]], [rowb[2]], scale=-0.5)
            st(rr[slot:slot + 1, 0:n], mrow[:, 0:n].bitcast(F32), [rowb[0]],
               writes=[B(f"rrM{slot}")], eng=dq)
            st(rr[slot + 1:slot + 2, 0:n], rrow[:, 0:n].bitcast(F32), [rowb[2]],
               writes=[B(f"rrM{slot}")], eng=dq)

        sch.mark("M")
        mchunks = NCH if not trunc else min(trunc.get("mc", NCH), NCH)
        mstage = 99 if not trunc else trunc.get("ms", 99)
        for c in range(mchunks):
            t0 = CH * c
            sch.curlab = "M1"
            for j in range(DJ):
                ld(xch[j], x_in[128 * j:128 * j + 128, t0:t0 + CH], [xch_b[j]])
            if mstage < 2:
                continue
            sA, sQ = psv(6, CH, p0=0, pn=1), psv(7, CH, p0=0, pn=1)
            for j in range(DJ):
                mm(sA, oneskr, xch[j], [xch_b[j], B("constD")], [PB[6]],
                   start=(j == 0), stop=(j == DJ - 1), inc=(1 if j == DJ - 1 else 0))
            sq1 = rx(8)
            for j in range(DJ):
                act(sq1, xch[j], AF.Square, [xch_b[j]], [RXb[8]])
                mm(sQ, oneskr, sq1, [RXb[8], B("constD")], [PB[7]],
                   start=(j == 0), stop=(j == DJ - 1), inc=(1 if j == DJ - 1 else 0))
            ln_rows(sA, sQ, PB[6], PB[7], CH, D, 0)
            if mstage < 3:
                continue
            ld(muB, rr.bitcast(F32R)[0:1, 0:CH].partition_broadcast(128), [B("muBD")],
               reads=[B("rrM0")])
            ld(rsB, rr.bitcast(F32R)[1:2, 0:CH].partition_broadcast(128), [B("rsBD")],
               reads=[B("rrM0")])
            for j in range(DJ):
                t1 = rx(j)
                vstt(t1, xch[j].bitcast(F32), 1.0, muB, ALU.mult,
                     ALU.subtract, [xch_b[j], B("muBD")], [RXb[j]])
                vtt(xlnB[j], t1.bitcast(F32), rsB, ALU.mult, [RXb[j], B("rsBD")],
                    [xln_b[j]])
            if c == 0:
                tapw("xln", xlnB[0], [xln_b[0]])
            if mstage < 4:
                continue
            sch.curlab = "M2"
            xitv = svv("xit").bitcast(BF16)
            for j in range(DT_J):
                if c == 0:
                    ld(xi_t[j][:, 0:3], dram["zeros_c"][:, 0:3], [xib(j)],
                       eng="P")
                else:
                    vcopy(xi_t[j][:, 0:3], xitv[:, 3 * j:3 * j + 3],
                          [B("xitB")], [xib(j)])
            ldw(inm_t[0], inwT[:, 0:128], 8, 128, [inm_b[0]], eng="P")
            ldw(inm_t[1], inwT[:, 128:256], 8, 128, [inm_b[1]], eng="P")
            for m in range(DT_J):
                lw, lwb = inm_t[m % 2], inm_b[m % 2]
                bank = m % 6
                for k in range(DJ):
                    mm(psv(bank), lw[:, 128 * k:128 * k + 128], xlnB[k],
                       [lwb, xln_b[k]], [PB[bank]], start=(k == 0),
                       stop=(k == DJ - 1), inc=(1 if k == DJ - 1 else 0))
                if m + 2 < DT_J:
                    ldw(inm_t[m % 2], inwT[:, 128 * (m + 2):128 * (m + 2) + 128],
                        8, 128, [inm_b[m % 2]], eng="P")
                act(xi_t[m][:, 3:515], psv(bank), AF.Copy, [PB[bank]], [xib(m)])
            if mstage < 5:
                continue
            sch.curlab = "M3"
            for j in range(DT_J):
                ld(cdiag_t.rearrange("p (k c) -> p k c", k=DCONV),
                   cdiagd[j].rearrange("k p c -> p k c"), [B("cdiagb")], eng="P")
                bank = j % 2
                for k in range(DCONV):
                    mm(psv(bank), cdiag_t[:, 128 * k:128 * k + 128],
                       xi_t[j][:, k:k + 512], [B("cdiagb"), xib(j)], [PB[bank]],
                       start=(k == 0), stop=(k == DCONV - 1),
                       inc=(1 if k == DCONV - 1 else 0))
                act(rx(j), psv(bank), AF.Silu, [PB[bank]], [RXb[j]],
                    bias=svv("convb")[:, j:j + 1])
                vcopy(xitv[:, 3 * j:3 * j + 3], xi_t[j][:, 512:515], [xib(j)],
                      [B("xitB")])
            if mstage < 6:
                continue
            sch.curlab = "M4"
            xdbl = PS[0:96, 1024:1024 + CH]
            for k in range(DT_J):
                mm(xdbl, XPT[:, 96 * k:96 * k + 96], rx(k), [RXb[k], B("constD")],
                   [PB[2]], start=(k == 0), stop=(k == DT_J - 1),
                   inc=(1 if k == DT_J - 1 else 0))
            act(dtlo[0:64, :], PS[0:64, 1024:1024 + CH], AF.Copy, [PB[2]], [dtlo_b])
            act(bcT[0:32, :], PS[64:96, 1024:1024 + CH], AF.Copy, [PB[2]], [bc_b])
            st(rowsB[c], bcT[0:16, :].bitcast(F32), [bc_b], writes=[rowsBD])
            st(rowsC[c], bcT[16:32, :], [bc_b], writes=[rowsCD], eng="P")
            if c == 0:
                tapw("xc", rx(0), [RXb[0]])
            for s_i in range(DST):
                ld(br_t[s_i], rowsB[c][s_i:s_i + 1, :].partition_broadcast(128),
                   [br_b[s_i]], reads=[rowsBD])
                ld(cr_t[s_i], rowsC[c][s_i:s_i + 1, :].partition_broadcast(128),
                   [cr_b[s_i]], reads=[rowsCD])
            if mstage < 7:
                continue
            sch.curlab = "M5"
            dtiles = DT_J if mstage >= 8 else 1
            ldw(inm_t[0], inwT[:, 128 * DT_J:128 * DT_J + 128], 8, 128,
                [inm_b[0]], eng="P")

            def m5pro(j):
                dbank = 3 + (j % 2)
                mm(psv(dbank), DTT[:, 128 * j:128 * j + 128], dtlo[0:64, :],
                   [B("constD"), dtlo_b], [PB[dbank]])
                dtj, dtjb = dt_t[j % 2], dt_b[j % 2]
                uj, ujb = u_t[j % 2], u_b[j % 2]
                act(sp1, psv(dbank), AF.Abs, [PB[dbank]], [RXb[14]],
                    bias=svv("dtb")[:, j:j + 1])
                act(sp2, sp1, AF.Exp, [RXb[14]], [RXb[15]], scale=-1.0)
                act(sp1, sp2, AF.Ln, [RXb[15]], [RXb[14]], bias=1.0)
                vts2(sp2, psv(dbank), svv("dtb")[:, j:j + 1], 0.0, ALU.add, ALU.max,
                     [PB[dbank]], [RXb[15]])
                vtt(dtj, sp1, sp2, ALU.add, [RXb[14], RXb[15]], [dtjb])
                vtt(uj, dtj, rx(j).bitcast(F32), ALU.mult, [dtjb, RXb[j]], [ujb])
                if c == 0 and j == 0:
                    tapw("dt0", dtj.bitcast(F32R), [dtjb])
                lw, lwb = inm_t[j % 2], inm_b[j % 2]
                zbank = 7
                for k in range(DJ):
                    mm(psv(zbank), lw[:, 128 * k:128 * k + 128], xlnB[k],
                       [lwb, xln_b[k]], [PB[zbank]], start=(k == 0),
                       stop=(k == DJ - 1), inc=(1 if k == DJ - 1 else 0))
                gj = g_t[j % 2]
                gjb = B(f"gring{j % 2}")
                act(gj, psv(zbank), AF.Silu, [PB[zbank]], [gjb])
                if j + 1 < DT_J:
                    mz1 = DT_J + j + 1
                    ldw(inm_t[(j + 1) % 2], inwT[:, 128 * mz1:128 * mz1 + 128],
                        8, 128, [inm_b[(j + 1) % 2]], eng="P")

            m5pro(0)
            for j in range(dtiles):
                dtj, dtjb = dt_t[j % 2], dt_b[j % 2]
                uj, ujb = u_t[j % 2], u_b[j % 2]
                gj = g_t[j % 2]
                gjb = B(f"gring{j % 2}")
                sch.curlab = "M6scan"
                ybank = 5 + (j % 2)
                for s_i in range(DST):
                    sl = s_i % 3
                    hs = s_i % 4
                    act(a_t[sl], dtj, AF.Exp, [dtjb], [a_b[sl]],
                        scale=svv("negA")[:, DST * j + s_i:DST * j + s_i + 1])
                    vtt(b_t[sl], uj, br_t[s_i], ALU.mult, [ujb, br_b[s_i]],
                        [b_b[sl]], eng="P")
                    vscan(hb_t[hs], a_t[sl], b_t[sl],
                          svv("hcarry")[:, DST * j + s_i:DST * j + s_i + 1],
                          [a_b[sl], b_b[sl], B("hcarry")], [hb_b[hs]])
                    vtt(w_t[sl], hb_t[hs], cr_t[s_i], ALU.mult,
                        [hb_b[hs], cr_b[s_i]], [w_b[sl]])
                    mm(psv(ybank), identb, w_t[sl], [w_b[sl], B("constD")],
                       [PB[ybank]], start=(s_i == 0), stop=False, inc=0)
                    if s_i % 4 == 3:
                        q4 = s_i // 4
                        hc_src = AB[:, 10880:10880 + 2048] \
                            .rearrange("p (i c) -> p i c", c=512)[:, :, 511:512]
                        vcopy(svv("hcarry")[:, DST * j + 4 * q4:
                                            DST * j + 4 * q4 + 4],
                              hc_src, [hb_b[i] for i in range(4)], [B("hcarry")])
                    if s_i == 2 and j + 1 < dtiles:
                        sch.curlab = "M5"
                        m5pro(j + 1)
                        sch.curlab = "M6scan"
                sch.curlab = "M6y"
                ld(ddiag_t, ddiagd[j].bitcast(F32R)[:, :], [ddiag_b])
                mm(psv(ybank), ddiag_t, rx(j), [ddiag_b, RXb[j]], [PB[ybank]],
                   start=False, stop=True, inc=1)
                vtt(y3_t[j], psv(ybank), gj, ALU.mult, [PB[ybank], gjb], [y3b(j)])
            if c == 0:
                tapw("y30", y3_t[0], [y3b(0)])
            if mstage < 9:
                continue
            sch.curlab = "M7"
            for m in range(DJ):
                lw, lwbs = outm_t[0], outm_b[0]
                ldw(lw, outwT[:, 128 * m:128 * m + 128], 16, 128, lwbs, eng="P")
                bank = m % 4
                for k in range(DT_J):
                    mm(psv(bank), lw[:, 128 * k:128 * k + 128], y3_t[k],
                       lwbs + [y3b(k)], [PB[bank]], start=(k == 0),
                       stop=(k == DT_J - 1), inc=(1 if k == DT_J - 1 else 0))
                act(rx(m), psv(bank), AF.Copy, [PB[bank]], [RXb[m]])
            if c == 0:
                tapw("mo0", rx(0), [RXb[0]])
            if mstage < 10:
                continue
            sch.curlab = "M8"
            sA2, sQ2 = psv(6, CH, p0=0, pn=1), psv(7, CH, p0=0, pn=1)
            for j in range(DJ):
                mm(sA2, oneskr, rx(j), [RXb[j], B("constD")], [PB[6]],
                   start=(j == 0), stop=(j == DJ - 1), inc=(1 if j == DJ - 1 else 0))
            sq2 = rx(8)
            for j in range(DJ):
                act(sq2, rx(j), AF.Square, [RXb[j]], [RXb[8]])
                mm(sQ2, oneskr, sq2, [RXb[8], B("constD")], [PB[7]],
                   start=(j == 0), stop=(j == DJ - 1), inc=(1 if j == DJ - 1 else 0))
            ln_rows(sA2, sQ2, PB[6], PB[7], CH, D, 2)
            ld(muB, rr.bitcast(F32R)[2:3, 0:CH].partition_broadcast(128), [B("muBD")],
               reads=[B("rrM2")])
            ld(rsB, rr.bitcast(F32R)[3:4, 0:CH].partition_broadcast(128), [B("rsBD")],
               reads=[B("rrM2")])
            for j in range(DJ):
                ld(xch[j], x_in[128 * j:128 * j + 128, t0:t0 + CH], [xch_b[j]])
            for j in range(DJ):
                tmp = axv(0, 512)
                vstt(tmp, rx(j).bitcast(F32), 1.0, muB, ALU.mult, ALU.subtract,
                     [RXb[j], B("muBD")], [XIr])
                vtt(rx(j), tmp, rsB, ALU.mult, [XIr, B("rsBD")],
                    [RXb[j]])
                vtt(tmp, rx(j).bitcast(F32), xch[j].bitcast(F32), ALU.add,
                    [RXb[j], xch_b[j]], [XIr])
                st(x1s[128 * j:128 * j + 128, t0:t0 + CH], tmp.bitcast(F32),
                   [XIr], writes=[B("x1sD")])
                if c == 0 and j == 0:
                    tapw("x10", tmp.bitcast(F32R), [XIr])

        sch.barrier()

        AXF, AQT, AKT, ASC3 = 0, 4160, 6208, 10304
        ANR, ASQ, AAT = 11840, 12352, 12872
        ALW1, ALW2, ANG = 14920, 16968, 17992
        ATQ, AVB, AVT, AES = 0, 8320, 10368, 14464
        XAX1, XART, XAMU, XARW = 0, 4096, 6144, 7184
        XAN2, XAN2C, XARB2 = 8744, 9256, 9768
        XARC, XADN = 10280, 10792

        def ffn_attn(l, srcdram, dstdram):
            x1t = [axv(XAX1 + 512 * j, 512).bitcast(F32R) for j in range(DJ)]
            x1b = [B("ax1D") for j in range(DJ)]
            xft = [arv(AXF + 520 * j, 520) for j in range(DJ)]
            xfb = [B("axfD") for j in range(DJ)]
            tt_ = [abv(ATQ + 1040 * j, 1040, F32R) for j in range(DJ)]
            ttb = [B(f"a_t{j}") for j in range(DJ)]
            sqa = arv(ASQ, 520)
            sqab = B("a_sq")
            qt = [arv(AQT + 512 * m, 512) for m in range(4)]
            qtb = [B(f"a_q{m}") for m in range(4)]
            kt = [[arv(AKT + 2048 * sl + 512 * m, 512) for m in range(4)]
                  for sl in range(2)]
            ktb = [[B(f"a{l}k{sl}{m}") for m in range(4)] for sl in range(2)]
            scr = [arv(ASC3 + 512 * i, 512) for i in range(3)]
            scrb = [B(f"a_scr{i}") for i in range(3)]
            sqnt = [arv(ANR, 512), arv(ASQ, 512)]
            sqnbt = [B("a_sqn"), sqab]
            rstdBt = [axv(XARB2, 512).bitcast(F32R), axv(XAN2, 512).bitcast(F32R)]
            rstdBbt = [B("a_rstdB0"), B("a_rstdB1")]
            vb = [abv(AVB + 512 * m, 512) for m in range(4)]
            vbb = [B(f"a_v{m}") for m in range(4)]
            vT = [[abv(AVT + 2048 * sl + 64 * i, 64) for i in range(32)]
                  for sl in range(2)]
            vTb = [[B(f"a{l}vT{sl}{i}") for i in range(32)] for sl in range(2)]
            eSTt = [[abv(AES + 512 * (8 * par + m), 512) for m in range(8)]
                    for par in range(3)]
            eSTbt = [[B(f"a_e{8 * par + m}") for m in range(8)] for par in range(3)]
            negrowt = [arv(ANG, 512), arv(ANG + 512, 512)]
            negrowbt = [B("a_negrow0"), B("a_negrow1")]
            dnrowt = [AXS[0:1, XADN:XADN + 512].bitcast(F32R),
                      AXS[0:1, XARW:XARW + 512].bitcast(F32R)]
            dnrowbt = [B("a_n3"), B("a_mr")]
            recipBt = [axv(XARC, 512).bitcast(F32R)[0:64, :],
                       axv(XAN2C, 512).bitcast(F32R)[0:64, :]]
            recipBbt = [B("a_recipB0"), B("a_recipB1")]
            att = [arv(AAT + 512 * m, 512) for m in range(4)]
            attb = [B(f"a_att{m}") for m in range(4)]
            xmid = [axv(XAX1 + 512 * j, 512).bitcast(F32R) for j in range(DJ)]
            xmidb = [B("ax1D") for j in range(DJ)]
            rtab = {nm: axv(XART + 512 * i, 512).bitcast(F32R)
                    for i, nm in enumerate(("csq", "ssq", "csk", "ssk"))}
            rtabb = {nm: B("artD") for nm in rtab}
            qlw = [arv(ALW1 + 1024 * i, 1024) for i in range(2)]
            qlwb = [B(f"a_qlw{i}") for i in range(2)]
            wlw = [arv(ALW2 + 512 * i, 512) for i in range(2)]
            wlwb = [B(f"a_wlw{i}") for i in range(2)]
            muBa, rsBa = axv(XAMU, 520).bitcast(F32R), axv(XAMU + 520, 520).bitcast(F32R)
            muBab, rsBab = B("a_muB"), B("a_rsB")
            mrA = AXS[0:1, XARW:XARW + 520].bitcast(F32R)
            vrA = AXS[0:1, XARW + 520:XARW + 1040].bitcast(F32R)
            rrA = AXS[0:1, XARW + 1040:XARW + 1560].bitcast(F32R)
            rowAb = [B("a_mr"), B("a_vr"), B("a_rr")]
            nrm2t = [AXS[0:2, XARW + 520:XARW + 1032].bitcast(F32R),
                     AXS[0:2, XARW + 1040:XARW + 1552].bitcast(F32R)]
            nrm2bt = [B("a_vr"), B("a_rr")]
            nrm2ct = [AXS[0:2, XARW:XARW + 512].bitcast(F32R),
                      AXS[0:2, XADN:XADN + 512].bitcast(F32R)]
            nrm2cbt = [B("a_mr"), B("a_n3")]

            for c in range(NCH):
                sch.curlab = "Aload"
                t0 = CH * c
                sl_c, sl_p = c % 2, (c + 1) % 2
                for j in range(DJ):
                    if c == 0:
                        ld(xft[j][:, 0:1], dram["zeros_c"].bitcast(F32R)[:, 0:1],
                           [xfb[j]])
                    elif l == 0:
                        vcopy(xft[j][:, 0:1], xft[j][:, 512:513], [xfb[j]],
                              [xfb[j]])
                if l == 0:
                    for j in range(DJ):
                        ld(xft[j][:, 1:513],
                           srcdram.bitcast(F32R)[128 * j:128 * j + 128, t0:t0 + CH],
                           [xfb[j]], reads=[B("x1sD")])
                else:
                    for j in range(DJ):
                        if c == 0:
                            ld(xft[j][:, 1:513],
                               srcdram.bitcast(F32R)[128 * j:128 * j + 128,
                                                     t0:t0 + CH],
                               [xfb[j]], reads=[B("xl0D")])
                        else:
                            ld(xft[j][:, 0:513],
                               srcdram.bitcast(F32R)[128 * j:128 * j + 128,
                                                     t0 - 1:t0 + CH],
                               [xfb[j]], reads=[B("xl0D")])
                if l == 0:
                    sch.curlab = "Aln"
                    sA = psv(6, CH, p0=0, pn=1)
                    sQ = psv(7, CH, p0=0, pn=1)
                    for j in range(DJ):
                        mm(sA, svv("onesf"), xft[j][:, 1:513].bitcast(F32),
                           [xfb[j], B("constD")],
                           [PB[6]], start=(j == 0), stop=(j == DJ - 1),
                           inc=(1 if j == DJ - 1 else 0))
                    for j in range(DJ):
                        act(sqa[:, 0:512], xft[j][:, 1:513], AF.Square, [xfb[j]],
                            [sqab])
                        mm(sQ, oneskr, sqa[:, 0:512], [sqab, B("constD")],
                           [PB[7]], start=(j == 0), stop=(j == DJ - 1),
                           inc=(1 if j == DJ - 1 else 0))
                    act(mrA[:, 0:512], sA, AF.Copy, [PB[6]], [rowAb[0]], scale=1.0 / D)
                    act(vrA[:, 0:512], sQ, AF.Copy, [PB[7]], [rowAb[1]], scale=1.0 / D)
                    vtt(rrA[:, 0:512], mrA[:, 0:512], mrA[:, 0:512], ALU.mult,
                        [rowAb[0]], [rowAb[2]])
                    vtt(vrA[:, 0:512], vrA[:, 0:512], rrA[:, 0:512], ALU.subtract,
                        [rowAb[1], rowAb[2]], [rowAb[1]])
                    act(rrA[:, 0:512], vrA[:, 0:512], AF.Ln,
                        [rowAb[1], B("epsc")], [rowAb[2]],
                        bias=svv("epsc")[0:1, :])
                    act(rrA[:, 0:512], rrA[:, 0:512], AF.Exp, [rowAb[2]], [rowAb[2]],
                        scale=-0.5)
                    st(rr[4:5, 0:512], mrA[:, 0:512].bitcast(F32), [rowAb[0]],
                       writes=[B("rrA45")], eng="P")
                    st(rr[5:6, 0:512], rrA[:, 0:512].bitcast(F32), [rowAb[2]],
                       writes=[B("rrA45")], eng="P")
                    ld(muBa[:, 0:512], rr.bitcast(F32R)[4:5, 0:512].partition_broadcast(128),
                       [muBab], reads=[B("rrA45")], eng="P")
                    ld(rsBa[:, 0:512], rr.bitcast(F32R)[5:6, 0:512].partition_broadcast(128),
                       [rsBab], reads=[B("rrA45")], eng="P")
                    for j in range(DJ):
                        vstt(tt_[j][:, 0:512], xft[j][:, 1:513].bitcast(F32), 1.0,
                             muBa[:, 0:512], ALU.mult, ALU.subtract,
                             [xfb[j], muBab], [ttb[j]])
                        vtt(xft[j][:, 1:513], tt_[j][:, 0:512],
                            rsBa[:, 0:512], ALU.mult, [ttb[j], rsBab], [xfb[j]])
                if c == 0 and l == 0:
                    tapw("xf0", xft[0][:, 1:513], [xfb[0]])
                sch.curlab = "Aqkv"
                for nm in rtab:
                    ld(rtab[nm], dram[nm].bitcast(F32R)[:, t0:t0 + CH], [rtabb[nm]],
                       eng="P")

                def qkvmm(m):
                    lw, lwb = qlw[m % 2], qlwb[m % 2]
                    bank = m % 5
                    for k in range(DJ):
                        mv_ap = xft[k][:, 1:513] if k < 4 else xft[k][:, 0:512]
                        mm(psv(bank), lw[:, 128 * k:128 * k + 128], mv_ap,
                           [lwb, xfb[k]], [PB[bank]], start=(k == 0),
                           stop=(k == DJ - 1), inc=(1 if k == DJ - 1 else 0))

                def qstats(m):
                    bank = m % 5
                    par = m % 2
                    sqn, sqnb = sqnt[par], sqnbt[par]
                    nrm2, nrm2b = nrm2t[par], nrm2bt[par]
                    nrm2c, nrm2cb = nrm2ct[par], nrm2cbt[par]
                    rstdB, rstdBb = rstdBt[par], rstdBbt[par]
                    isq = m < 4
                    mt = m if isq else m - 4
                    act(sqn, psv(bank), AF.Square, [PB[bank]], [sqnb])
                    nps = PS[0:2, 512 * (5 + par):512 * (5 + par) + 512]
                    mm(nps, headones, sqn, [sqnb, B("constD")], [PB[5 + par]])
                    act(nrm2, nps, AF.Sqrt, [PB[5 + par]], [nrm2b])
                    vts1(nrm2, nrm2, 1e-12, ALU.max, [nrm2b], [nrm2b])
                    vrecip(nrm2c, nrm2, [nrm2b], [nrm2cb])
                    slot = (6 + 2 * m) if isq else (14 + 2 * mt)
                    st(rr[slot:slot + 2, 0:512], nrm2c.bitcast(F32), [nrm2cb],
                       writes=[B(f"rrn{m % 2}")], eng="P")
                    ld(rstdB[0:64, :], rr.bitcast(F32R)[slot:slot + 1, 0:512]
                       .partition_broadcast(64), [rstdBb], reads=[B(f"rrn{m % 2}")],
                       eng="P")
                    ld(rstdB[64:128, :],
                       rr.bitcast(F32R)[slot + 1:slot + 2, 0:512]
                       .partition_broadcast(64), [rstdBb],
                       reads=[rstdBb, B(f"rrn{m % 2}")], eng="P")

                def qapply(m):
                    bank = m % 5
                    if m < 8:
                        isq = m < 4
                        mt = m if isq else m - 4
                        dstt = qt[mt] if isq else kt[sl_c][mt]
                        dsttb = qtb[mt] if isq else ktb[sl_c][mt]
                        cs_t = rtab["csq" if isq else "csk"]
                        ss_t = rtab["ssq" if isq else "ssk"]
                        csb_ = rtabb["csq" if isq else "csk"]
                        ssb_ = rtabb["ssq" if isq else "ssk"]
                        sccol = svv("qs" if isq else "ks")[:, l:l + 1]
                        scbuf = B("c_qs" if isq else "c_ks")
                        par = m % 2
                        rstdB, rstdBb = rstdBt[par], rstdBbt[par]
                        vstt(scr[0], psv(bank), sccol, rstdB, ALU.mult, ALU.mult,
                             [PB[bank], rstdBb, scbuf], [scrb[0]])
                        mm(psv(7), rperm, scr[0], [scrb[0], B("constD")], [PB[7]])
                        vstt(scr[1], psv(7), 1.0, ss_t, ALU.mult, ALU.mult,
                             [PB[7], ssb_], [scrb[1]])
                        vtt(scr[2], scr[0].bitcast(F32), cs_t, ALU.mult,
                            [scrb[0], csb_], [scrb[2]])
                        vtt(dstt, scr[2].bitcast(F32),
                            scr[1].bitcast(F32), ALU.add, [scrb[1], scrb[2]], [dsttb])
                    else:
                        mt = m - 8
                        act(vb[mt], psv(bank), AF.Copy, [PB[bank]], [vbb[mt]])
                        for hh in range(2):
                            h_ = 2 * mt + hh
                            for ts in range(4):
                                pv = psv(7, 32, 0, BF16)
                                trans(pv, vb[mt][64 * hh:64 * hh + 64,
                                                 128 * ts:128 * ts + 128],
                                      identb[64 * hh:64 * hh + 64,
                                             64 * hh:64 * hh + 64],
                                      [vbb[mt], B("constD")], [PB[7]])
                                act(vT[sl_c][4 * h_ + ts], pv, AF.Copy, [PB[7]],
                                    [vTb[sl_c][4 * h_ + ts]])

                ldw(qlw[0], wqkv[l][:, 0:128], 8, 128, [qlwb[0]])
                ldw(qlw[1], wqkv[l][:, 128:256], 8, 128, [qlwb[1]])
                for m in range(14):
                    if m < 12:
                        qkvmm(m)
                    if m + 2 < 12:
                        ldw(qlw[(m + 2) % 2], wqkv[l][:, 128 * (m + 2):128 * (m + 2) + 128],
                            8, 128, [qlwb[(m + 2) % 2]])
                    if 1 <= m <= 8:
                        qstats(m - 1)
                    if m >= 2:
                        qapply(m - 2)
                if c == 0 and l == 0:
                    tapw("qt0", qt[0], [qtb[0]])
                    tapw("kt0", kt[sl_c][0], [ktb[sl_c][0]])

                def kloc(h_, i):
                    mt, hh = h_ // 2, h_ % 2
                    sl = sl_p if i < 4 else sl_c
                    ii = i % 4
                    return (kt[sl][mt][64 * hh:64 * hh + 64,
                                       128 * ii:128 * ii + 128], ktb[sl][mt])

                def vloc(h_, i):
                    if i < 4:
                        return vT[sl_p][4 * h_ + i], vTb[sl_p][4 * h_ + i]
                    return vT[sl_c][4 * h_ + i - 4], vTb[sl_c][4 * h_ + i - 4]

                m2list = list(range(8)) if c > 0 else list(range(4, 8))
                sch.curlab = "ArowS"
                ldw(wlw[0], wod[l][:, 0:128], 4, 128, [wlwb[0]])
                ldw(wlw[1], wod[l][:, 128:256], 4, 128, [wlwb[1]])
                for h_ in range(NH):
                    mt, hh = h_ // 2, h_ % 2
                    qsl = qt[mt][64 * hh:64 * hh + 64, :]
                    for a_i in range(4):
                        dlist = list(range(0, 5)) if c > 0 else list(range(4 - a_i, 5))
                        for d_ in dlist:
                            kap, kbf = kloc(h_, a_i + d_)
                            needmask = (d_ == 0 and c > 0) or (d_ == 4)
                            mm(PS[:, 128 * d_:128 * d_ + 128],
                               qsl[:, 128 * a_i:128 * a_i + 128], kap,
                               [qtb[mt], kbf], [PB[0], PB[1]], start=True,
                               stop=not needmask, inc=(0 if needmask else 0))
                            if d_ == 0 and c > 0:
                                mm(PS[:, 0:128], identr, maskd0,
                                   [B("constD"), B("constD")], [PB[0], PB[1]],
                                   start=False, stop=True, inc=0)
                            if d_ == 4:
                                mm(PS[:, 512:640], identr, maskd4,
                                   [B("constD"), B("constD")], [PB[0], PB[1]],
                                   start=False, stop=True,
                                   inc=1)
                        dmin = dlist[0]
                        ngb = B(f"a_ng{h_}_{a_i}")
                        negm = svv("altmp")[:, 4 * h_ + a_i:4 * h_ + a_i + 1]
                        negm2 = svv("altmp")[:, 32 + a_i:33 + a_i]
                        if dmin < 4:
                            vmax(negm, PS[:, 128 * dmin:512], [PB[0], PB[1]],
                                 [ngb], negate=True)
                            vmax(negm2, PS[:, 512:640], [PB[0], PB[1]],
                                 [B(f"a_ng2{a_i}")], negate=True)
                            vtt(negm, negm, negm2, ALU.min,
                                [ngb, B(f"a_ng2{a_i}")], [ngb])
                        else:
                            vmax(negm, PS[:, 512:640], [PB[0], PB[1]],
                                 [ngb], negate=True)
                        st(rr[30 + h_:31 + h_, 128 * a_i:128 * a_i + 128], negm,
                           [ngb], writes=[B(f"rrg{h_ % 2}")], eng="P")
                for h_ in range(2):
                    ld(negrowt[h_][0:1, :], rr.bitcast(F32R)[30 + h_:31 + h_, 0:512],
                       [negrowbt[h_]], reads=[B(f"rrg{h_ % 2}")], eng="P")
                for v in range(NH + 2):
                    if v < NH:
                        sch.curlab = "AsT"
                        h_ = v
                        mt, hh = h_ // 2, h_ % 2
                        eST, eSTb = eSTt[h_ % 3], eSTbt[h_ % 3]
                        negrow, negrowb = negrowt[h_ % 2], negrowbt[h_ % 2]
                        qsl = qt[mt][64 * hh:64 * hh + 64, :]
                        for m2 in m2list:
                            bank = 2 + (m2 % 4)
                            kap, kbf = kloc(h_, m2)
                            mm(psv(bank), kap, qsl, [kbf, qtb[mt]],
                               [PB[bank]], start=True, stop=False, inc=0)
                            off = 896 - 128 * m2
                            mm(psv(bank), identr, maskTv[:, off:off + 512],
                               [B("constD"), B("constD")], [PB[bank]],
                               start=False, stop=False, inc=0)
                            mm(psv(bank), onesrow, negrow[0:1, :],
                               [negrowb, B("constD")], [PB[bank]],
                               start=False, stop=True, inc=1)
                            act(eST[m2], psv(bank), AF.Exp, [PB[bank]], [eSTb[m2]])
                            if c == 0 and l == 0 and h_ == 0 and m2 == 4:
                                tapw("es40", eST[4], [eSTb[4]])
                        if c == 0 and l == 0 and h_ == 0:
                            tapw("ng0", negrow[0:1, :], [negrowb])
                        if v + 2 < NH:
                            ld(negrowt[v % 2][0:1, :],
                               rr.bitcast(F32R)[32 + v:33 + v, 0:512],
                               [negrowbt[v % 2]], reads=[B(f"rrg{v % 2}")], eng="P")
                    if 1 <= v <= NH:
                        sch.curlab = "Adn"
                        hd = v - 1
                        eST, eSTb = eSTt[hd % 3], eSTbt[hd % 3]
                        dnrow, dnrowb = dnrowt[hd % 2], dnrowbt[hd % 2]
                        recipB, recipBb = recipBt[hd % 2], recipBbt[hd % 2]
                        dn = PS[0:1, 512 * 6:512 * 6 + 512]
                        for i, m2 in enumerate(m2list):
                            mm(dn, oneskb, eST[m2], [eSTb[m2], B("constD")],
                               [PB[6]], start=(i == 0), stop=(i == len(m2list) - 1),
                               inc=(1 if i == len(m2list) - 1 else 0))
                        vrecip(dnrow, dn, [PB[6]], [dnrowb])
                        st(rr[22 + hd:23 + hd, 0:512], dnrow.bitcast(F32), [dnrowb],
                           writes=[B(f"rrd{hd % 2}")], eng="P")
                        ld(recipB, rr.bitcast(F32R)[22 + hd:23 + hd, 0:512]
                           .partition_broadcast(64),
                           [recipBb], reads=[B(f"rrd{hd % 2}")], eng="P")
                    if v >= 2:
                        sch.curlab = "Aav"
                        ha = v - 2
                        mt, hh = ha // 2, ha % 2
                        eST, eSTb = eSTt[ha % 3], eSTbt[ha % 3]
                        recipB, recipBb = recipBt[ha % 2], recipBbt[ha % 2]
                        ao = PS[0:64, 512 * 7:512 * 7 + 512]
                        for i, m2 in enumerate(m2list):
                            vap, vbf = vloc(ha, m2)
                            mm(ao, vap, eST[m2], [vbf, eSTb[m2]], [PB[7]],
                               start=(i == 0), stop=(i == len(m2list) - 1),
                               inc=(1 if i == len(m2list) - 1 else 0))
                        vtt(att[mt][64 * hh:64 * hh + 64, :], ao, recipB,
                            ALU.mult, [PB[7], recipBb], [attb[mt]])
                        if c == 0 and l == 0 and ha == 1:
                            tapw("att00", att[0], [attb[0]])
                sch.curlab = "Awo"
                for m in range(DJ):
                    lw, lwb = wlw[m % 2], wlwb[m % 2]
                    bank = m % 4
                    for k in range(4):
                        mm(psv(bank), lw[:, 128 * k:128 * k + 128], att[k],
                           [lwb, attb[k]], [PB[bank]], start=(k == 0), stop=(k == 3),
                           inc=(1 if k == 3 else 0))
                    if m + 2 < DJ:
                        ldw(wlw[m % 2], wod[l][:, 128 * (m + 2):128 * (m + 2) + 128],
                            4, 128, [wlwb[m % 2]])
                    vstt(xmid[m], psv(bank), 1.0, xft[m][:, 1:513].bitcast(F32),
                         ALU.mult, ALU.add, [PB[bank], xfb[m]], [xmidb[m]])
                    st(dstdram[128 * m:128 * m + 128, t0:t0 + CH],
                       xmid[m].bitcast(F32), [xmidb[m]], writes=[B(f"xm{l}D")])
                if c == 0 and l == 0:
                    tapw("xm00", xmid[0], [xmidb[0]])

        BTT, BW1, BHN, BMU2 = 0, 8192, 9216, 17408
        BPR, BAG, BW2 = 0, 22528, 26624
        XBXM, XBRW = 0, 8256

        def ffn_geglu(l, srcdram, dstdram, final=False):
            xmt = [axv(XBXM + 1032 * j, 1032).bitcast(F32R) for j in range(DJ)]
            xmb = [B("bxmD") for j in range(DJ)]
            hnt = [arv(BHN + 1024 * j, 1024) for j in range(DJ)]
            hnb = [B(f"b_hn{j}") for j in range(DJ)]
            ttt = [arv(BTT + 1024 * j, 1024) for j in range(DJ)]
            ttb2 = [B(f"b_tt{j}") for j in range(DJ)]
            sqbb2 = ttb2[0]
            sqb = arv(BTT, 1024)
            sqbb = sqbb2
            prod = [abv(BPR + 1024 * p, 1024) for p in range(NPAIR)]
            prodb = [B(f"b_pr{p}") for p in range(NPAIR)]
            aevt = [abv(BAG + 1024 * i, 1024) for i in range(2)]
            aevbt = [B("b_aev0"), B("b_aev1")]
            gevt = [abv(BAG + 2048 + 1024 * i, 1024) for i in range(2)]
            gevbt = [B("b_gev0"), B("b_gev1")]
            muBc = arv(BMU2, 1024)
            rsBc = arv(BMU2 + 1024, 1024)
            muBcb, rsBcb = B("b_muB"), B("b_rsB")
            w1lw = arv(BW1, 1024)
            w1lwb = B("b_w1lw")
            w2lw = abv(BW2, 2816)
            w2lwb = B("b_w2lw")
            mrB = AXS[0:1, XBRW:XBRW + 1024].bitcast(F32R)
            vrB = AXS[0:1, XBRW + 1024:XBRW + 2048].bitcast(F32R)
            rrB = AXS[0:1, XBRW + 2048:XBRW + 3072].bitcast(F32R)
            rowBb = [B("b_mr"), B("b_vr"), B("b_rr")]
            xl_t = hnt
            xl_b = hnb
            srcD = B(f"xm{l}D")
            dstD = B("xl0D")

            for s_ in range(NSC):
                sch.curlab = "Bload"
                t0 = SC * s_
                for j in range(DJ):
                    if s_ == 0:
                        ld(xmt[j][:, 0:1],
                           dram["zeros_c"].bitcast(F32R)[:, 0:1], [xmb[j]])
                        ld(xmt[j][:, 1:1025],
                           srcdram.bitcast(F32R)[128 * j:128 * j + 128, t0:t0 + SC],
                           [xmb[j]], reads=[srcD])
                    else:
                        ld(xmt[j][:, 0:1025],
                           srcdram.bitcast(F32R)[128 * j:128 * j + 128,
                                                 t0 - 1:t0 + SC],
                           [xmb[j]], reads=[srcD])

                def hap(j):
                    return xmt[j][:, 1:1025] if j < 4 else xmt[j][:, 0:1024]

                sch.curlab = "Bstat"
                for half in range(2):
                    cS = slice(512 * half, 512 * half + 512)
                    sA = PS[0:1, 512 * (4 + half):512 * (4 + half) + 512]
                    sQ = PS[0:1, 512 * (6 + half):512 * (6 + half) + 512]
                    for j in range(DJ):
                        mm(sA, svv("onesf"), hap(j)[:, cS].bitcast(F32),
                           [xmb[j], B("constD")],
                           [PB[4 + half]], start=(j == 0), stop=(j == DJ - 1),
                           inc=(1 if j == DJ - 1 else 0))
                    for j in range(DJ):
                        act(sqb[:, cS], hap(j)[:, cS], AF.Square, [xmb[j]], [sqbb])
                        mm(sQ, oneskr, sqb[:, cS], [sqbb, B("constD")],
                           [PB[6 + half]], start=(j == 0), stop=(j == DJ - 1),
                           inc=(1 if j == DJ - 1 else 0))
                    act(mrB[:, cS], sA, AF.Copy, [PB[4 + half]], [rowBb[0]],
                        scale=1.0 / D)
                    act(vrB[:, cS], sQ, AF.Copy, [PB[6 + half]], [rowBb[1]],
                        scale=1.0 / D)
                vtt(rrB, mrB, mrB, ALU.mult, [rowBb[0]], [rowBb[2]])
                vtt(vrB, vrB, rrB, ALU.subtract, [rowBb[1], rowBb[2]], [rowBb[1]])
                act(rrB, vrB, AF.Ln, [rowBb[1], B("epsc")], [rowBb[2]], bias=svv("epsc")[0:1, :])
                act(rrB, rrB, AF.Exp, [rowBb[2]], [rowBb[2]], scale=-0.5)
                st(rr[38:39, 0:SC], mrB.bitcast(F32), [rowBb[0]],
                   writes=[B("rrB38")], eng="P")
                st(rr[39:40, 0:SC], rrB.bitcast(F32), [rowBb[2]],
                   writes=[B("rrB38")], eng="P")
                ld(muBc, rr.bitcast(F32R)[38:39, 0:SC].partition_broadcast(128), [muBcb],
                   reads=[B("rrB38")], eng="P")
                ld(rsBc, rr.bitcast(F32R)[39:40, 0:SC].partition_broadcast(128), [rsBcb],
                   reads=[B("rrB38")], eng="P")
                sch.curlab = "Bln"
                for j in range(DJ):
                    vstt(ttt[j], hap(j).bitcast(F32), 1.0, muBc,
                         ALU.mult, ALU.subtract, [xmb[j], muBcb], [ttb2[j]])
                    vtt(hnt[j], ttt[j].bitcast(F32), rsBc, ALU.mult,
                        [ttb2[j], rsBcb], [hnb[j]])
                    vts2(ttt[j], hnt[j].bitcast(F32),
                         svv("lng")[:, 8 * l + j:8 * l + j + 1],
                         svv("lnb")[:, 8 * l + j:8 * l + j + 1], ALU.mult, ALU.add,
                         [hnb[j], B("constD"), B("constD")], [ttb2[j]])
                if s_ == 0 and l == 0:
                    tapw("hn0", ttt[0][:, 0:512], [ttb2[0]])
                sch.curlab = "Bw1"
                coffs = []
                off_ = 0
                for p in range(NPAIR):
                    wdt = min(128, HID - 128 * p)
                    coffs.append((off_, wdt))
                    off_ += 2 * wdt
                w1slots = [w1lw, muBc, rsBc]
                w1sbufs = [w1lwb, muBcb, rsBcb]
                groups = [(p, ag) for p in range(NPAIR) for ag in range(2)]

                def w1gld(i):
                    p, ag = groups[i]
                    coff, wdt = coffs[p]
                    ldw(w1slots[i % 3][:, 0:8 * wdt],
                        w1p[l][:, coff + ag * wdt:coff + ag * wdt + wdt],
                        8, wdt, [w1sbufs[i % 3]])

                for i in range(3):
                    w1gld(i)
                for i, (p, ag) in enumerate(groups):
                    coff, wdt = coffs[p]
                    slot, slotb = w1slots[i % 3], w1sbufs[i % 3]
                    for half in range(2):
                        bank = 2 * ag + half
                        cS = slice(512 * half, 512 * half + 512)
                        for k in range(DJ):
                            mm(PS[0:wdt, 512 * bank:512 * bank + 512],
                               slot[:, wdt * k:wdt * k + wdt], ttt[k][:, cS],
                               [slotb, ttb2[k]], [PB[bank]], start=(k == 0),
                               stop=(k == DJ - 1), inc=(1 if k == DJ - 1 else 0))
                    if i + 3 < len(groups):
                        w1gld(i + 3)
                    if ag != 1:
                        continue
                    pp = p % 2
                    aev, aevb = aevt[pp], aevbt[pp]
                    gev, gevb = gevt[pp], gevbt[pp]
                    bca = svv("b1c")[0:wdt, 2 * NPAIR * l + 2 * p:
                                     2 * NPAIR * l + 2 * p + 1]
                    bcg = svv("b1c")[0:wdt, 2 * NPAIR * l + 2 * p + 1:
                                     2 * NPAIR * l + 2 * p + 2]
                    for half in range(2):
                        cS = slice(512 * half, 512 * half + 512)
                        act(aev[0:wdt, cS], PS[0:wdt, 512 * half:512 * half + 512],
                            AF.Copy, [PB[half]], [aevb])
                        act(gev[0:wdt, cS],
                            PS[0:wdt, 1024 + 512 * half:1024 + 512 * half + 512],
                            AF.Gelu, [PB[2 + half]], [gevb], bias=bcg)
                    vstt(prod[p][0:wdt, :], aev[0:wdt, :], bca, gev[0:wdt, :],
                         ALU.add, ALU.mult, [aevb, gevb, B("constD")], [prodb[p]])
                sch.curlab = "Bw2"

                def w2qld(qi):
                    m_, q = qi // 5, qi % 5
                    kp0 = 5 * q
                    ntile = min(5, 21 - kp0)
                    sl = w2lw[:, 640 * (qi % 4):640 * (qi % 4) + 640]
                    ldw(sl[:, 0:128 * ntile],
                        w2d[l][128 * kp0:128 * (kp0 + ntile),
                               128 * m_:128 * m_ + 128],
                        ntile, 128, [B(f"w2q{qi % 4}")], eng="P")
                    if q == 4:
                        ld(sl[0:RAG, 128 * ntile:128 * ntile + 128],
                           w2d[l][2688:2730, 128 * m_:128 * m_ + 128],
                           [B(f"w2q{qi % 4}")], reads=[B(f"w2q{qi % 4}")],
                           eng="P")

                for qi in range(3):
                    w2qld(qi)
                for m in range(DJ):
                    for kp in range(NPAIR):
                        q = kp // 5
                        qi = 5 * m + q
                        wdt = min(128, HID - 128 * kp)
                        sl = w2lw[:, 640 * (qi % 4):640 * (qi % 4) + 640]
                        tl = sl[0:wdt, 128 * (kp - 5 * q):128 * (kp - 5 * q) + 128]
                        for half in range(2):
                            bank = 4 + 2 * (m % 2) + half
                            cS = slice(512 * half, 512 * half + 512)
                            mm(psv(bank), tl,
                               prod[kp][0:wdt, cS], [B(f"w2q{qi % 4}"), prodb[kp]],
                               [PB[bank]],
                               start=(kp == 0), stop=(kp == NPAIR - 1),
                               inc=(1 if kp == NPAIR - 1 else 0))
                        if kp - 5 * q == 0 and qi + 3 < 5 * DJ:
                            w2qld(qi + 3)
                    for half in range(2):
                        bank = 4 + 2 * (m % 2) + half
                        cS = slice(512 * half, 512 * half + 512)
                        vstt(xl_t[m][:, cS], psv(bank),
                             svv("b2c")[:, 8 * l + m:8 * l + m + 1],
                             xmt[m][:, 1:1025].bitcast(F32)[:, cS], ALU.add, ALU.add,
                             [PB[bank], xmb[m], B("constD")], [xl_b[m]])
                    if not final:
                        st(dstdram[128 * m:128 * m + 128, t0:t0 + SC],
                           xl_t[m].bitcast(F32), [xl_b[m]], writes=[dstD])
                if s_ == 0 and l == 0:
                    tapw("xl00", xl_t[0].bitcast(F32)[:, 0:512], [xl_b[0]])
                if final:
                    sch.curlab = "Bfin"
                    x1f = [axv(XBXM + 1024 * j, 1024).bitcast(F32R) for j in range(DJ)]
                    x1fb = [B("bxmD") for j in range(DJ)]
                    for j in range(DJ):
                        ld(x1f[j], x1s.bitcast(F32R)[128 * j:128 * j + 128,
                                                         t0:t0 + SC],
                           [B("bxmD")], reads=[B("x1sD")])
                    for half in range(2):
                        cS = slice(512 * half, 512 * half + 512)
                        sA = PS[0:1, 512 * half:512 * half + 512]
                        sQ = PS[0:1, 1024 + 512 * half:1024 + 512 * half + 512]
                        for j in range(DJ):
                            mm(sA, oneskr, xl_t[j][:, cS], [xl_b[j], B("constD")],
                               [PB[half]], start=(j == 0), stop=(j == DJ - 1),
                               inc=(1 if j == DJ - 1 else 0))
                        for j in range(DJ):
                            act(sqb[:, cS], xl_t[j][:, cS], AF.Square, [xl_b[j]],
                                [sqbb])
                            mm(sQ, oneskr, sqb[:, cS], [sqbb, B("constD")],
                               [PB[2 + half]], start=(j == 0), stop=(j == DJ - 1),
                               inc=(1 if j == DJ - 1 else 0))
                        act(mrB[:, cS], sA, AF.Copy, [PB[half]], [rowBb[0]],
                            scale=1.0 / D)
                        act(vrB[:, cS], sQ, AF.Copy, [PB[2 + half]], [rowBb[1]],
                            scale=1.0 / D)
                    vtt(rrB, mrB, mrB, ALU.mult, [rowBb[0]], [rowBb[2]])
                    vtt(vrB, vrB, rrB, ALU.subtract, [rowBb[1], rowBb[2]], [rowBb[1]])
                    act(rrB, vrB, AF.Ln, [rowBb[1], B("epsc")], [rowBb[2]], bias=svv("epsc")[0:1, :])
                    act(rrB, rrB, AF.Exp, [rowBb[2]], [rowBb[2]], scale=-0.5)
                    st(rr[40:41, 0:SC], mrB.bitcast(F32), [rowBb[0]],
                       writes=[B("rrB40")], eng="P")
                    st(rr[41:42, 0:SC], rrB.bitcast(F32), [rowBb[2]],
                       writes=[B("rrB40")], eng="P")
                    ld(muBc, rr.bitcast(F32R)[40:41, 0:SC].partition_broadcast(128), [muBcb],
                       reads=[B("rrB40")], eng="P")
                    ld(rsBc, rr.bitcast(F32R)[41:42, 0:SC].partition_broadcast(128), [rsBcb],
                       reads=[B("rrB40")], eng="P")
                    tf = abv(BAG, 2048, F32R)
                    tfb = aevb
                    o1 = arv(BW1, 1024)
                    o1b = w1lwb
                    ot = axv(XBRW, 1024).bitcast(F32R)
                    otb = rowBb[0]
                    for j in range(DJ):
                        vstt(tf, xl_t[j].bitcast(F32), 1.0, muBc, ALU.mult,
                             ALU.subtract, [xl_b[j], muBcb], [tfb, gevb])
                        vtt(o1, tf, rsBc, ALU.mult, [tfb, rsBcb], [o1b])
                        vtt(ot, o1, x1f[j].bitcast(F32), ALU.add,
                            [o1b, x1fb[j]], [otb])
                        st(out[128 * j:128 * j + 128, t0:t0 + SC],
                           ot.bitcast(F32), [otb], writes=[B("outD")])

        ph = trunc.get("ph", 5) if trunc else 5
        if ph >= 1:
            sch.mark("A1")
            ffn_attn(0, x1s, xms[0])
            sch.barrier()
        if ph >= 2:
            sch.mark("B1")
            ffn_geglu(0, xms[0], xl0)
            sch.barrier()
        if ph >= 3:
            sch.mark("A2")
            ffn_attn(1, xl0, xms[1])
            sch.barrier()
        if ph >= 4:
            sch.mark("B2")
            ffn_geglu(1, xms[1], None, final=True)
            sch.barrier()
        for sem, v in sch.cnt.items():
            if v > 0:
                sch.items["S"].append(("w", sem, v))
        global _LAST_COUNTS, _LAST_LABS
        _LAST_COUNTS = dict(sch.cnt)
        _LAST_LABS = {e: list(v) for e, v in sch.labs.items()}

        for semname, v in sch.cnt.items():
            if v > 0 or semname in Sched.ENGS:
                sems[semname] = es.enter_context(nc.semaphore(
                    semname.replace(":", "_")))
        assert len(sems) <= 200, f"too many semaphores: {len(sems)}"
        block = es.enter_context(nc.Block())

        def replay(E, ename):
            with nc.allow_non_contiguous_dma(
                    reason="1-column lead/zero loads are tiny and rare"), \
                 nc.allow_low_precision(
                    reason="float32r views are fp32-storage; PSUM accumulation fp32"):
                cur = None
                for it in sch.items[ename]:
                    if it[0] == "m":
                        if cur is not None:
                            nc.leave_named_scope(cur[0], cur[1], False)
                        sid, _ = nc.enter_named_scope(it[1], False)
                        cur = (it[1], sid)
                    elif it[0] == "w":
                        E.wait_ge(sems[it[1]], it[2])
                    else:
                        _, fn, sem, inc = it
                        ins = fn(E)
                        if inc > 0:
                            ins.then_inc(sems[sem], inc)
                if cur is not None:
                    nc.leave_named_scope(cur[0], cur[1], False)

        @block.tensor
        def _(eng):
            replay(eng, "T")

        @block.vector
        def _(eng):
            replay(eng, "V")

        @block.scalar
        def _(eng):
            replay(eng, "A")

        @block.gpsimd
        def _(eng):
            replay(eng, "P")

        @block.sync
        def _(eng):
            replay(eng, "S")

    return nc


def kernel(**inputs):
    taps = dict(_TAPS)
    consts = host_constants()
    hw = host_layout(inputs)
    nc = build(taps)
    X = np.asarray(inputs["L0_attn_x"])
    ncores = 8
    shared = {}
    shared.update({k: np.ascontiguousarray(v) for k, v in consts.items()})
    shared.update(hw)
    in_maps = []
    for b in range(ncores):
        m = dict(shared)
        m["x"] = np.ascontiguousarray(X[b].T)
        in_maps.append(m)
    res = run_bass_kernel_spmd(nc, in_maps, core_ids=list(range(ncores)))
    outs = np.stack([np.asarray(r["out"]).T for r in res.results], 0)
    if taps:
        kernel.last_taps = [{nm: np.asarray(r["tap_" + nm]) for nm in taps}
                            for r in res.results]
    return outs.astype(np.float32)


# version 05 (final)
# speedup vs baseline: 1.3289x; 1.0366x over previous
import numpy as np
import ml_dtypes
import contextlib
import concourse.bass as bass
import concourse.mybir as mybir
from concourse.bass_utils import run_bass_kernel_spmd

F32 = mybir.dt.float32
F32R = mybir.dt.float32r
BF16 = mybir.dt.bfloat16
AF = mybir.ActivationFunctionType
ALU = mybir.AluOpType
AX = mybir.AxisListType

T, D, DIN, DST, DCONV, DTR = 4096, 1024, 2048, 16, 4, 64
NH, DH, WIN, HID = 8, 64, 512, 2730
CH, SC = 512, 1024
NCH, NSC = T // CH, T // SC
DT_J, DJ = DIN // 128, D // 128
NPAIR = (HID + 127) // 128
RAG = HID - 128 * (NPAIR - 1)
NEG = -50000.0
EPS = 1e-5

_TAPS = {}
_LAST_LABS = {}
_LAST_COUNTS = {}


def host_constants():
    c = {}
    c["ident_b"] = np.eye(128, dtype=ml_dtypes.bfloat16)
    c["ident_r"] = np.eye(128, dtype=np.float32)
    P = np.zeros((128, 128), np.float32)
    for h0 in (0, 64):
        for i in range(32):
            P[h0 + i, h0 + i + 32] = -1.0
            P[h0 + i + 32, h0 + i] = 1.0
    c["rperm_r"] = np.ascontiguousarray(P.T)
    c["onesrow_r"] = np.ones((1, 128), np.float32)
    c["oneskcol_r"] = np.ones((128, 1), np.float32)
    c["oneskcol_b"] = np.ones((128, 1), dtype=ml_dtypes.bfloat16)
    ho = np.zeros((128, 2), np.float32)
    ho[0:64, 0] = 1.0
    ho[64:128, 1] = 1.0
    c["headones_r"] = ho
    qq = np.arange(128)[:, None]
    cc = np.arange(128)[None, :]
    c["mask_d0"] = np.where(qq <= cc, 0.0, NEG).astype(np.float32)
    c["mask_d4"] = np.where(qq >= cc, 0.0, NEG).astype(np.float32)
    s = np.arange(1408)[None, :]
    c["maskT"] = np.where((s - qq >= 384) & (s - qq <= 896), 0.0, NEG).astype(np.float32)
    hi = np.arange(0, DH, 2, dtype=np.float64)
    inv = 1.0 / 10000.0 ** (hi / DH)
    t = np.arange(T, dtype=np.float64)
    fr = np.concatenate([t[:, None] * inv[None, :]] * 2, 1)
    base = (hi + 0.4 * DH) / (1.4 * DH)
    power = (t - T // 2) / (WIN // 2)
    scl = np.concatenate([base[None, :] ** power[:, None]] * 2, 1)
    cos, sin = np.cos(fr), np.sin(fr)
    for nm, tab in (("csq", 8.0 * cos * scl), ("ssq", 8.0 * sin * scl),
                    ("csk", cos / scl), ("ssk", sin / scl)):
        tt = tab.T.astype(np.float32)
        c[nm] = np.ascontiguousarray(np.concatenate([tt, tt], 0))
    c["zeros_c"] = np.zeros((128, 4), np.float32)
    return c


def host_layout(inp):
    f = np.ascontiguousarray
    h = {}
    h["inwT"] = f(inp["L0_attn_in_proj_w"].T)
    h["outwT"] = f(inp["L0_attn_out_proj_w"].T)
    h["xpwT"] = f(inp["L0_attn_x_proj_w"].T)
    h["dtwT"] = f(inp["L0_attn_dt_proj_w"].T)
    cw = inp["L0_attn_conv_w"]
    cd = np.zeros((DT_J, DCONV, 128, 128), np.float32)
    for j in range(DT_J):
        for k in range(DCONV):
            np.fill_diagonal(cd[j, k], cw[128 * j:128 * (j + 1), k])
    h["cdiag"] = cd
    dd = np.zeros((DT_J, 128, 128), np.float32)
    for j in range(DT_J):
        np.fill_diagonal(dd[j], inp["L0_attn_Dp"][128 * j:128 * (j + 1)])
    h["ddiag"] = dd
    h["convb2"] = f(inp["L0_attn_conv_b"].reshape(DT_J, 128).T)
    h["dtb2"] = f(inp["L0_attn_dt_proj_b"].reshape(DT_J, 128).T)
    h["alog2"] = f(inp["L0_attn_A_log"].reshape(DT_J, 128, DST)
                   .transpose(1, 0, 2).reshape(128, DT_J * DST))
    for l in range(2):
        h[f"wqkv{l}"] = f(inp["L0_ffn_wqkv"][l])
        h[f"wo{l}"] = f(inp["L0_ffn_wo"][l])
        h[f"qs{l}"] = f(np.tile(inp["L0_ffn_q_scale"][l], 2)[:, None])
        h[f"ks{l}"] = f(np.tile(inp["L0_ffn_k_scale"][l], 2)[:, None])
        h[f"lng{l}"] = f(inp["L0_ffn_ln_g"][l].reshape(DJ, 128).T)
        h[f"lnb{l}"] = f(inp["L0_ffn_ln_b"][l].reshape(DJ, 128).T)
        h[f"b2t{l}"] = f(inp["L0_ffn_b2"][l].reshape(DJ, 128).T)
        w1 = inp["L0_ffn_w1"][l]
        b1 = inp["L0_ffn_b1"][l]
        perm = []
        bt = np.zeros((128, 2 * NPAIR), np.float32)
        for p in range(NPAIR):
            wdt = min(128, HID - 128 * p)
            ia = np.arange(128 * p, 128 * p + wdt)
            perm.extend(ia)
            perm.extend(ia + HID)
            bt[:wdt, 2 * p] = b1[ia]
            bt[:wdt, 2 * p + 1] = b1[ia + HID]
        h[f"w1p{l}"] = f(w1[:, np.array(perm)])
        h[f"b1t{l}"] = bt
        h[f"w2_{l}"] = f(inp["L0_ffn_w2"][l])
    return h


class Buf:
    __slots__ = ("w", "r", "name")

    def __init__(self, name=""):
        self.w, self.r, self.name = None, {}, name


class Sched:
    ENGS = ("T", "V", "A", "P", "S")
    SELF_WAIT = ("A", "P", "V")

    def __init__(self):
        self.cnt = {e: 0 for e in self.ENGS}
        self.seen = {e: {} for e in self.ENGS}
        self.items = {e: [] for e in self.ENGS}
        self.curlab = ""
        self.labs = {e: [] for e in self.ENGS}

    def mark(self, name):
        for e in self.ENGS:
            self.items[e].append(("m", name))
            self.labs[e].append("mark")

    def _wait(self, e, sem, v):
        if v > self.seen[e].get(sem, 0):
            self.items[e].append(("w", sem, v))
            self.labs[e].append("w." + self.curlab + ":" + sem)
            self.seen[e][sem] = v

    def op(self, e, fn, reads=(), writes=(), inc=1, dma=False):
        if dma:
            assert writes, "DMA needs a write target buf"
            sem = "D:" + writes[0].name
            self.cnt.setdefault(sem, 0)
        else:
            sem = e
        deps = {}
        for b in reads:
            if b.w is not None:
                deps[b.w[0]] = max(deps.get(b.w[0], 0), b.w[1])
        for b in writes:
            if b.w is not None:
                deps[b.w[0]] = max(deps.get(b.w[0], 0), b.w[1])
            for s2, v in b.r.items():
                deps[s2] = max(deps.get(s2, 0), v)
        for s2, v in deps.items():
            if s2 == e:
                if e in self.SELF_WAIT:
                    self._wait(e, s2, v)
                continue
            self._wait(e, s2, v)
        self.cnt[sem] += inc
        self.items[e].append(("i", fn, sem, inc))
        self.labs[e].append("i." + self.curlab)
        rmark = self.cnt[sem] + (1 if inc == 0 and not dma else 0)
        for b in reads:
            b.r[sem] = max(b.r.get(sem, 0), rmark)
        if inc > 0:
            for b in writes:
                b.w = (sem, self.cnt[sem])
                b.r = {}

    def barrier(self):
        tgt = dict(self.cnt)
        for e in self.ENGS:
            for sem, v in tgt.items():
                if sem != e and v > 0:
                    self._wait(e, sem, v)


def build(taps=None, trunc=None):
    taps = taps or {}
    nc = bass.Bass("TRN2", target_bir_lowering=False, debug=False)
    dram = {}

    def din(name, shape, dt):
        dram[name] = nc.dram_tensor(name, shape, dt, kind="ExternalInput").ap()
        return dram[name]

    def dscr(name, shape, dt):
        dram[name] = nc.dram_tensor(name, shape, dt, kind="Internal").ap()
        return dram[name]

    x_in = din("x", [D, T], F32R)
    inwT = din("inwT", [D, 2 * DIN], F32)
    outwT = din("outwT", [DIN, D], F32)
    xpwT = din("xpwT", [DIN, DTR + 2 * DST], F32R)
    dtwT = din("dtwT", [DTR, DIN], F32R)
    cdiagd = din("cdiag", [DT_J, DCONV, 128, 128], F32)
    ddiagd = din("ddiag", [DT_J, 128, 128], F32)
    convb2 = din("convb2", [128, DT_J], F32)
    dtb2 = din("dtb2", [128, DT_J], F32)
    alog2 = din("alog2", [128, DT_J * DST], F32)
    wqkv = [din(f"wqkv{l}", [D, 3 * 512], F32R) for l in range(2)]
    wod = [din(f"wo{l}", [512, D], F32R) for l in range(2)]
    qs2 = [din(f"qs{l}", [128, 1], F32) for l in range(2)]
    ks2 = [din(f"ks{l}", [128, 1], F32) for l in range(2)]
    lng2 = [din(f"lng{l}", [128, DJ], F32) for l in range(2)]
    lnb2 = [din(f"lnb{l}", [128, DJ], F32) for l in range(2)]
    b2t2 = [din(f"b2t{l}", [128, DJ], F32) for l in range(2)]
    w1p = [din(f"w1p{l}", [D, 2 * HID], F32R) for l in range(2)]
    b1t2 = [din(f"b1t{l}", [128, 2 * NPAIR], F32) for l in range(2)]
    w2d = [din(f"w2_{l}", [HID, D], F32) for l in range(2)]
    cmeta = {"ident_b": ([128, 128], BF16), "ident_r": ([128, 128], F32R),
             "rperm_r": ([128, 128], F32R), "onesrow_r": ([1, 128], F32R),
             "oneskcol_r": ([128, 1], F32R), "oneskcol_b": ([128, 1], BF16),
             "headones_r": ([128, 2], F32R), "mask_d0": ([128, 128], F32R),
             "mask_d4": ([128, 128], F32R), "maskT": ([128, 1408], F32R),
             "csq": ([128, T], F32), "ssq": ([128, T], F32),
             "csk": ([128, T], F32), "ssk": ([128, T], F32),
             "zeros_c": ([128, 4], F32)}
    for nm, (shape, dt) in cmeta.items():
        din(nm, shape, dt)

    rowsB = dscr("rowsB", [NCH, DST, CH], F32)
    rowsC = dscr("rowsC", [NCH, DST, CH], BF16)
    rr = dscr("rr", [48, SC], F32)
    inwb = dscr("inwb", [D, 2 * DIN], BF16)
    outwb = dscr("outwb", [DIN, D], BF16)
    cdgb = dscr("cdgb", [DT_J, DCONV, 128, 128], BF16)
    w2b = [dscr(f"w2b_{l}", [HID, D], BF16) for l in range(2)]
    x1s = dscr("x1s", [D, T], F32)
    xms = [dscr(f"xm{l}", [D, T], F32) for l in range(2)]
    xl0 = dscr("xl0", [D, T], F32)
    out = nc.dram_tensor("out", [D, T], F32, kind="ExternalOutput").ap()
    dram["out"] = out
    tap_out = {nm: nc.dram_tensor("tap_" + nm, sh, dt, kind="ExternalOutput").ap()
               for nm, (sh, dt) in taps.items()}

    sch = Sched()
    es = contextlib.ExitStack()
    with es:
        AR = es.enter_context(nc.sbuf_tensor("AR", [128, 19456], F32R))
        AB = es.enter_context(nc.sbuf_tensor("AB", [128, 29824], BF16))
        AXS = es.enter_context(nc.sbuf_tensor("AX", [128, 11328], F32))
        SM = es.enter_context(nc.sbuf_tensor("SM", [128, 1024], F32))
        CST = es.enter_context(nc.sbuf_tensor("CST", [128, 3712], F32R))
        CSB = es.enter_context(nc.sbuf_tensor("CSB", [128, 256], BF16))
        DTT = es.enter_context(nc.sbuf_tensor("DTT", [64, 2048], F32R))
        PS = es.enter_context(nc.psum_tensor("PS", [128, 4096], F32))
        sems = {}
        block = None

        bufs = {}

        def B(name):
            if name not in bufs:
                bufs[name] = Buf(name)
            return bufs[name]

        CVo, _co = {}, [0]

        def _cal(nm, n):
            CVo[nm] = _co[0]
            _co[0] += n

        for nm_, n_ in [("maskd0", 128), ("maskd4", 128), ("maskT", 1408),
                        ("identr", 128), ("rperm", 128), ("onesrow", 128),
                        ("oneskr", 1), ("headones", 2), ("xpt", 96 * DT_J)]:
            _cal(nm_, n_)
        assert _co[0] <= 3712, _co[0]

        def cvv(nm, n):
            return CST[:, CVo[nm]:CVo[nm] + n]

        maskd0, maskd4, maskTv = cvv("maskd0", 128), cvv("maskd4", 128), cvv("maskT", 1408)
        identr, rperm = cvv("identr", 128), cvv("rperm", 128)
        onesrow = cvv("onesrow", 128)[0:1, 0:128]
        oneskr, headones = cvv("oneskr", 1), cvv("headones", 2)
        XPT = cvv("xpt", 96 * DT_J)
        identb, oneskb = CSB[:, 0:128], CSB[:, 128:129]

        SVdef = [("negA", 256), ("convb", 16), ("dtb", 16), ("hcarry", 256),
                 ("qs", 2), ("ks", 2), ("lng", 16), ("lnb", 16), ("b2c", 16),
                 ("b1c", 2 * NPAIR * 2), ("altmp", 256), ("sp1", 16), ("xit", 24), ("epsc", 1), ("onesf", 1)]
        so = {nm: sum(n for _, n in SVdef[:i]) for i, (nm, _) in enumerate(SVdef)}
        assert sum(n for _, n in SVdef) <= 1024

        def svv(nm):
            n = dict(SVdef)[nm]
            return SM[:, so[nm]:so[nm] + n]

        def ld(dst, src, writes, reads=(), eng="S"):
            sch.op(eng, (lambda d, s: (lambda E: E.dma_start(out=d, in_=s)))(dst, src),
                   reads=reads, writes=writes, inc=16, dma=True)

        def st(dst, src, reads, writes=(), eng="S"):
            wr = list(writes) if writes else [B("stsink")]
            sch.op(eng, (lambda d, s: (lambda E: E.dma_start(out=d, in_=s)))(dst, src),
                   reads=reads, writes=wr, inc=16, dma=True)

        def ldw(lw2d, src2d, nk, ncc, writes, reads=(), eng="S"):
            ld(lw2d.rearrange("p (k c) -> p k c", k=nk),
               src2d.rearrange("(k p) c -> p k c", p=128), writes, reads, eng)

        def mm(ps_ap, lhsT, rhs, reads, writes, start=True, stop=True, inc=1):
            sch.op("T", (lambda o_, a_, b_, s_, p_:
                         (lambda E: E.matmul(o_, a_, b_, start=s_, stop=p_)))
                   (ps_ap, lhsT, rhs, start, stop), reads=reads, writes=writes, inc=1)

        def act(out_ap, in_ap, fn, reads, writes, bias=0.0, scale=1.0, accum=None):
            sch.op("A", (lambda o_, i_, f_, bi, s_, ac:
                         (lambda E: E.activation(o_, i_, f_, bias=bi, scale=s_,
                                                 accum_out=ac)))
                   (out_ap, in_ap, fn, bias, scale, accum), reads=reads, writes=writes)

        def vtt(out_ap, a_ap, b_ap, op, reads, writes, eng="V"):
            sch.op(eng, (lambda o_, a_, b_, p_: (lambda E: E.tensor_tensor(o_, a_, b_, p_)))
                   (out_ap, a_ap, b_ap, op), reads=reads, writes=writes)

        def vstt(out_ap, in0, scal, in1, op0, op1, reads, writes, eng="V"):
            sch.op(eng, (lambda o_, a_, s_, b_, p0, p1:
                         (lambda E: E.scalar_tensor_tensor(o_, a_, s_, b_, p0, p1)))
                   (out_ap, in0, scal, in1, op0, op1), reads=reads, writes=writes)

        def vts2(out_ap, in0, s1, s2, op0, op1, reads, writes, eng="V"):
            sch.op(eng, (lambda o_, a_, y1, y2, p0, p1:
                         (lambda E: E.tensor_scalar(o_, a_, y1, y2, p0, p1)))
                   (out_ap, in0, s1, s2, op0, op1), reads=reads, writes=writes)

        def vts1(out_ap, in0, s1, op, reads, writes, eng="V"):
            sch.op(eng, (lambda o_, a_, y1, p_:
                         (lambda E: E.tensor_single_scalar(o_, a_, y1, p_)))
                   (out_ap, in0, s1, op), reads=reads, writes=writes)

        def vcopy(out_ap, in_ap, reads, writes, eng="V"):
            sch.op(eng, (lambda o_, i_: (lambda E: E.tensor_copy(o_, i_)))(out_ap, in_ap),
                   reads=reads, writes=writes)

        def vscan(out_ap, a_ap, b_ap, init, reads, writes):
            sch.op("V", (lambda o_, a_, b_, i_:
                         (lambda E: E.tensor_tensor_scan(o_, a_, b_, i_, ALU.mult, ALU.add)))
                   (out_ap, a_ap, b_ap, init), reads=reads, writes=writes)

        def vmax(out_ap, in_ap, reads, writes, negate=False):
            sch.op("V", (lambda o_, i_, n_: (lambda E: E.reduce_max(o_, i_, AX.X, negate=n_)))
                   (out_ap, in_ap, negate), reads=reads, writes=writes)

        def vrecip(out_ap, in_ap, reads, writes):
            sch.op("V", (lambda o_, i_: (lambda E: E.reciprocal(o_, i_)))(out_ap, in_ap),
                   reads=reads, writes=writes)

        def vmemset(ap, v, writes, eng="V"):
            sch.op(eng, (lambda a_, v_: (lambda E: E.memset(a_, v_)))(ap, v), writes=writes)

        def trans(ps_ap, in_ap, id_ap, reads, writes):
            sch.op("T", (lambda o_, i_, d_: (lambda E: E.transpose(o_, i_, d_)))
                   (ps_ap, in_ap, id_ap), reads=reads, writes=writes)

        PB = [B(f"ps{i}") for i in range(8)]

        def psv(bank, n=512, off=0, dt=None, p0=0, pn=128):
            ap = PS[p0:p0 + pn, 512 * bank + off:512 * bank + off + n]
            return ap.bitcast(dt) if dt else ap

        def tapw(nm, src_ap, rbufs):
            if nm in taps:
                want = taps[nm][1]
                if src_ap.dtype != want:
                    src_ap = src_ap.bitcast(want)
                st(tap_out[nm][:, :], src_ap, rbufs, writes=[B("tap" + nm)])

        def arv(off, n, dt=None):
            assert off + n <= 19456, (off, n)
            ap = AR[:, off:off + n]
            return ap.bitcast(dt) if dt else ap

        def abv(off, n, dt=None):
            assert off + n <= 29824, (off, n)
            ap = AB[:, off:off + n]
            return ap.bitcast(dt) if dt else ap

        def axv(off, n):
            assert off + n <= 11328, (off, n)
            return AXS[:, off:off + n]

        def row0(off, n):
            return AR[0:1, off:off + n]

        for nm in ("ident_b", "oneskcol_b", "oneskcol_r", "headones_r", "rperm_r",
                   "mask_d0", "mask_d4", "maskT", "ident_r"):
            tgt = {"ident_b": identb, "oneskcol_b": oneskb, "oneskcol_r": oneskr,
                   "headones_r": headones, "rperm_r": rperm, "mask_d0": maskd0,
                   "mask_d4": maskd4, "maskT": maskTv, "ident_r": identr}[nm]
            ld(tgt, dram[nm][:, :], [B("constD")])
        ld(onesrow, dram["onesrow_r"][0:1, :], [B("constD")])
        ld(svv("convb"), convb2[:, :], [B("constD")])
        ld(svv("dtb"), dtb2[:, :], [B("constD")])
        for l in range(2):
            ld(svv("qs")[:, l:l + 1], qs2[l][:, :], [B("constD")])
            ld(svv("ks")[:, l:l + 1], ks2[l][:, :], [B("constD")])
            ld(svv("lng")[:, 8 * l:8 * l + 8], lng2[l][:, :], [B("constD")])
            ld(svv("lnb")[:, 8 * l:8 * l + 8], lnb2[l][:, :], [B("constD")])
            ld(svv("b2c")[:, 8 * l:8 * l + 8], b2t2[l][:, :], [B("constD")])
            ld(svv("b1c")[:, 2 * NPAIR * l:2 * NPAIR * (l + 1)], b1t2[l][:, :],
               [B("constD")])
        ld(svv("altmp"), alog2[:, :], [B("constD")])
        act(svv("negA"), svv("altmp"), AF.Exp, [B("constD")], [B("negA")])
        vts1(svv("negA"), svv("negA"), -1.0, ALU.mult, [B("negA")], [B("negA")])
        for k in range(DT_J):
            ld(XPT[:, 96 * k:96 * k + 96], xpwT[128 * k:128 * k + 128, :], [B("constD")])
        ld(DTT[:, :], dtwT[:, :], [B("constD")])
        vmemset(svv("hcarry"), 0.0, [B("hcarry")])
        ld(inwb, inwT, [B("inwbD")], eng="P")
        ld(outwb, outwT, [B("outwbD")], eng="P")
        for j in range(DT_J):
            ld(cdgb[j], cdiagd[j], [B("cdgbD")], eng="P")
        for l in range(2):
            ld(w2b[l], w2d[l], [B(f"w2b{l}D")], eng="P")
        vmemset(svv("epsc"), EPS, [B("epsc")])
        vmemset(svv("onesf"), 1.0, [B("constD")])

        RXb = [B(f"RX{i}") for i in range(16)]
        XIb = [B(f"XI{i}") for i in range(8)]
        rx = lambda i: arv(512 * i, 512)
        xi_t = [abv(520 * j, 520) for j in range(DT_J)]
        y3_t = [abv(512 * j, 512) for j in range(DT_J)]

        XIr = B("XIr")

        def xib(j):
            return XIr

        def y3b(j):
            return XIr

        dtlo = arv(8192, 512)
        dtlo_b = B("dtlo")
        bcT = arv(15360, 512)
        bc_b = B("bcT")
        ddiag_t = arv(18944, 128)
        ddiag_b = B("ddiagb")
        dt_t = [arv(14336 + 512 * i, 512) for i in range(2)]
        dt_b = [B(f"dt{i}") for i in range(2)]
        u_t = [abv(8320 + 512 * i, 512) for i in range(2)]
        u_b = [B(f"u{i}") for i in range(2)]
        a_t = [arv(12800 + 512 * i, 512) for i in range(3)]
        a_b = [B(f"a{i}") for i in range(3)]
        b_t = [axv(8704 + 512 * i, 512).bitcast(F32R) for i in range(3)]
        b_b = [B(f"b{i}") for i in range(3)]
        w_t = [abv(9344 + 512 * i, 512) for i in range(3)]
        w_b = [B(f"w{i}") for i in range(3)]
        cr_t = [abv(17024 + 512 * s, 512) for s in range(DST)]
        cr_b = [B("crD") for s in range(DST)]
        hb_t = [abv(10880 + 512 * i, 512) for i in range(4)]
        hb_b = [B(f"hb{i}") for i in range(4)]
        xlnB = [abv(12928 + 512 * j, 512) for j in range(DJ)]
        xln_b = [B(f"xln{j}") for j in range(DJ)]
        br_t = [axv(512 + 512 * s, 512) for s in range(DST)]
        br_b = [B("brD") for s in range(DST)]
        cdiag_t = abv(29312, 512)
        outm_t = [abv(27264, 2048)]
        outm_b = [[B("outm0")]]

        xch = [arv(8704 + 512 * j, 512) for j in range(DJ)]
        xch_b = [B("xchD") for j in range(DJ)]
        inm_t = [abv(25216 + 1024 * i, 1024) for i in range(2)]
        inm_b = [B(f"inm{i}") for i in range(2)]
        muB, rsB = arv(16896, 512), arv(17408, 512)
        g_t = [arv(17920, 512), arv(18432, 512)]
        sp1, sp2 = arv(15872, 512), arv(16384, 512)
        mrow, vrow, rrow = row0(512 * 9, 512), row0(512 * 10, 512), row0(512 * 11, 512)
        rowb = [RXb[9], RXb[10], RXb[11]]
        rrD = B("rrD")
        rowsBD, rowsCD = B("rowsBD"), B("rowsCD")

        def ln_rows(sumA, sumQ, bA, bQ, n, nrm, slot, dq="S"):
            act(mrow[:, 0:n], sumA, AF.Copy, [bA], [rowb[0]], scale=1.0 / nrm)
            act(vrow[:, 0:n], sumQ, AF.Copy, [bQ], [rowb[1]], scale=1.0 / nrm)
            vtt(rrow[:, 0:n], mrow[:, 0:n], mrow[:, 0:n], ALU.mult, [rowb[0]], [rowb[2]])
            vtt(vrow[:, 0:n], vrow[:, 0:n], rrow[:, 0:n], ALU.subtract,
                [rowb[1], rowb[2]], [rowb[1]])
            act(rrow[:, 0:n], vrow[:, 0:n], AF.Ln, [rowb[1], B("epsc")], [rowb[2]], bias=svv("epsc")[0:1, :])
            act(rrow[:, 0:n], rrow[:, 0:n], AF.Exp, [rowb[2]], [rowb[2]], scale=-0.5)
            st(rr[slot:slot + 1, 0:n], mrow[:, 0:n].bitcast(F32), [rowb[0]],
               writes=[B(f"rrM{slot}")], eng=dq)
            st(rr[slot + 1:slot + 2, 0:n], rrow[:, 0:n].bitcast(F32), [rowb[2]],
               writes=[B(f"rrM{slot}")], eng=dq)

        sch.mark("M")
        mchunks = NCH if not trunc else min(trunc.get("mc", NCH), NCH)
        mstage = 99 if not trunc else trunc.get("ms", 99)
        for c in range(mchunks):
            t0 = CH * c
            sch.curlab = "M1"
            for j in range(DJ):
                ld(xch[j], x_in[128 * j:128 * j + 128, t0:t0 + CH], [xch_b[j]])
            if mstage < 2:
                continue
            sA, sQ = psv(6, CH, p0=0, pn=1), psv(7, CH, p0=0, pn=1)
            for j in range(DJ):
                mm(sA, oneskr, xch[j], [xch_b[j], B("constD")], [PB[6]],
                   start=(j == 0), stop=(j == DJ - 1), inc=(1 if j == DJ - 1 else 0))
            sq1 = rx(8)
            for j in range(DJ):
                act(sq1, xch[j], AF.Square, [xch_b[j]], [RXb[8]])
                mm(sQ, oneskr, sq1, [RXb[8], B("constD")], [PB[7]],
                   start=(j == 0), stop=(j == DJ - 1), inc=(1 if j == DJ - 1 else 0))
            ln_rows(sA, sQ, PB[6], PB[7], CH, D, 0)
            if mstage < 3:
                continue
            ld(muB, rr.bitcast(F32R)[0:1, 0:CH].partition_broadcast(128), [B("muBD")],
               reads=[B("rrM0")])
            ld(rsB, rr.bitcast(F32R)[1:2, 0:CH].partition_broadcast(128), [B("rsBD")],
               reads=[B("rrM0")])
            for j in range(DJ):
                t1 = rx(j)
                vstt(t1, xch[j].bitcast(F32), 1.0, muB, ALU.mult,
                     ALU.subtract, [xch_b[j], B("muBD")], [RXb[j]])
                vtt(xlnB[j], t1.bitcast(F32), rsB, ALU.mult, [RXb[j], B("rsBD")],
                    [xln_b[j]])
            if c == 0:
                tapw("xln", xlnB[0], [xln_b[0]])
            if mstage < 4:
                continue
            sch.curlab = "M2"
            xitv = svv("xit").bitcast(BF16)
            for j in range(DT_J):
                if c == 0:
                    ld(xi_t[j][:, 0:3], dram["zeros_c"][:, 0:3], [xib(j)],
                       eng="P")
                else:
                    vcopy(xi_t[j][:, 0:3], xitv[:, 3 * j:3 * j + 3],
                          [B("xitB")], [xib(j)])
            ldw(inm_t[0], inwb[:, 0:128], 8, 128, [inm_b[0]], reads=[B("inwbD")], eng="P")
            ldw(inm_t[1], inwb[:, 128:256], 8, 128, [inm_b[1]], reads=[B("inwbD")], eng="P")
            for m in range(DT_J):
                lw, lwb = inm_t[m % 2], inm_b[m % 2]
                bank = m % 6
                for k in range(DJ):
                    mm(psv(bank), lw[:, 128 * k:128 * k + 128], xlnB[k],
                       [lwb, xln_b[k]], [PB[bank]], start=(k == 0),
                       stop=(k == DJ - 1), inc=(1 if k == DJ - 1 else 0))
                if m + 2 < DT_J:
                    ldw(inm_t[m % 2], inwb[:, 128 * (m + 2):128 * (m + 2) + 128],
                        8, 128, [inm_b[m % 2]], reads=[B("inwbD")], eng="P")
                act(xi_t[m][:, 3:515], psv(bank), AF.Copy, [PB[bank]], [xib(m)])
            if mstage < 5:
                continue
            sch.curlab = "M3"
            for j in range(DT_J):
                ld(cdiag_t.rearrange("p (k c) -> p k c", k=DCONV),
                   cdgb[j].rearrange("k p c -> p k c"), [B("cdiagb")],
                   reads=[B("cdgbD")], eng="P")
                bank = j % 2
                for k in range(DCONV):
                    mm(psv(bank), cdiag_t[:, 128 * k:128 * k + 128],
                       xi_t[j][:, k:k + 512], [B("cdiagb"), xib(j)], [PB[bank]],
                       start=(k == 0), stop=(k == DCONV - 1),
                       inc=(1 if k == DCONV - 1 else 0))
                act(rx(j), psv(bank), AF.Silu, [PB[bank]], [RXb[j]],
                    bias=svv("convb")[:, j:j + 1])
                vcopy(xitv[:, 3 * j:3 * j + 3], xi_t[j][:, 512:515], [xib(j)],
                      [B("xitB")])
            if mstage < 6:
                continue
            sch.curlab = "M4"
            xdbl = PS[0:96, 1024:1024 + CH]
            for k in range(DT_J):
                mm(xdbl, XPT[:, 96 * k:96 * k + 96], rx(k), [RXb[k], B("constD")],
                   [PB[2]], start=(k == 0), stop=(k == DT_J - 1),
                   inc=(1 if k == DT_J - 1 else 0))
            act(dtlo[0:64, :], PS[0:64, 1024:1024 + CH], AF.Copy, [PB[2]], [dtlo_b])
            act(bcT[0:32, :], PS[64:96, 1024:1024 + CH], AF.Copy, [PB[2]], [bc_b])
            st(rowsB[c], bcT[0:16, :].bitcast(F32), [bc_b], writes=[rowsBD])
            st(rowsC[c], bcT[16:32, :], [bc_b], writes=[rowsCD], eng="P")
            if c == 0:
                tapw("xc", rx(0), [RXb[0]])
            for s_i in range(DST):
                ld(br_t[s_i], rowsB[c][s_i:s_i + 1, :].partition_broadcast(128),
                   [br_b[s_i]], reads=[rowsBD])
                ld(cr_t[s_i], rowsC[c][s_i:s_i + 1, :].partition_broadcast(128),
                   [cr_b[s_i]], reads=[rowsCD])
            if mstage < 7:
                continue
            sch.curlab = "M5"
            dtiles = DT_J if mstage >= 8 else 1
            ldw(inm_t[0], inwb[:, 128 * DT_J:128 * DT_J + 128], 8, 128,
                [inm_b[0]], reads=[B("inwbD")], eng="P")

            def m5pro(j):
                dbank = 3 + (j % 2)
                mm(psv(dbank), DTT[:, 128 * j:128 * j + 128], dtlo[0:64, :],
                   [B("constD"), dtlo_b], [PB[dbank]])
                dtj, dtjb = dt_t[j % 2], dt_b[j % 2]
                uj, ujb = u_t[j % 2], u_b[j % 2]
                act(sp1, psv(dbank), AF.Abs, [PB[dbank]], [RXb[14]],
                    bias=svv("dtb")[:, j:j + 1])
                act(sp2, sp1, AF.Exp, [RXb[14]], [RXb[15]], scale=-1.0)
                act(sp1, sp2, AF.Ln, [RXb[15]], [RXb[14]], bias=1.0)
                vts2(sp2, psv(dbank), svv("dtb")[:, j:j + 1], 0.0, ALU.add, ALU.max,
                     [PB[dbank]], [RXb[15]])
                vtt(dtj, sp1, sp2, ALU.add, [RXb[14], RXb[15]], [dtjb])
                vtt(uj, dtj, rx(j).bitcast(F32), ALU.mult, [dtjb, RXb[j]], [ujb])
                if c == 0 and j == 0:
                    tapw("dt0", dtj.bitcast(F32R), [dtjb])
                lw, lwb = inm_t[j % 2], inm_b[j % 2]
                zbank = 7
                for k in range(DJ):
                    mm(psv(zbank), lw[:, 128 * k:128 * k + 128], xlnB[k],
                       [lwb, xln_b[k]], [PB[zbank]], start=(k == 0),
                       stop=(k == DJ - 1), inc=(1 if k == DJ - 1 else 0))
                gj = g_t[j % 2]
                gjb = B(f"gring{j % 2}")
                act(gj, psv(zbank), AF.Silu, [PB[zbank]], [gjb])
                if j + 1 < DT_J:
                    mz1 = DT_J + j + 1
                    ldw(inm_t[(j + 1) % 2], inwb[:, 128 * mz1:128 * mz1 + 128],
                        8, 128, [inm_b[(j + 1) % 2]], reads=[B("inwbD")], eng="P")

            m5pro(0)
            for j in range(dtiles):
                dtj, dtjb = dt_t[j % 2], dt_b[j % 2]
                uj, ujb = u_t[j % 2], u_b[j % 2]
                gj = g_t[j % 2]
                gjb = B(f"gring{j % 2}")
                sch.curlab = "M6scan"
                ybank = 5 + (j % 2)
                for s_i in range(DST):
                    sl = s_i % 3
                    hs = s_i % 4
                    act(a_t[sl], dtj, AF.Exp, [dtjb], [a_b[sl]],
                        scale=svv("negA")[:, DST * j + s_i:DST * j + s_i + 1])
                    vtt(b_t[sl], uj, br_t[s_i], ALU.mult, [ujb, br_b[s_i]],
                        [b_b[sl]], eng="P")
                    vscan(hb_t[hs], a_t[sl], b_t[sl],
                          svv("hcarry")[:, DST * j + s_i:DST * j + s_i + 1],
                          [a_b[sl], b_b[sl], B("hcarry")], [hb_b[hs]])
                    vtt(w_t[sl], hb_t[hs], cr_t[s_i], ALU.mult,
                        [hb_b[hs], cr_b[s_i]], [w_b[sl]])
                    mm(psv(ybank), identb, w_t[sl], [w_b[sl], B("constD")],
                       [PB[ybank]], start=(s_i == 0), stop=False, inc=0)
                    if s_i % 4 == 3:
                        q4 = s_i // 4
                        hc_src = AB[:, 10880:10880 + 2048] \
                            .rearrange("p (i c) -> p i c", c=512)[:, :, 511:512]
                        vcopy(svv("hcarry")[:, DST * j + 4 * q4:
                                            DST * j + 4 * q4 + 4],
                              hc_src, [hb_b[i] for i in range(4)], [B("hcarry")])
                    if s_i == 2 and j + 1 < dtiles:
                        sch.curlab = "M5"
                        m5pro(j + 1)
                        sch.curlab = "M6scan"
                sch.curlab = "M6y"
                ld(ddiag_t, ddiagd[j].bitcast(F32R)[:, :], [ddiag_b])
                mm(psv(ybank), ddiag_t, rx(j), [ddiag_b, RXb[j]], [PB[ybank]],
                   start=False, stop=True, inc=1)
                vtt(y3_t[j], psv(ybank), gj, ALU.mult, [PB[ybank], gjb], [y3b(j)])
            if c == 0:
                tapw("y30", y3_t[0], [y3b(0)])
            if mstage < 9:
                continue
            sch.curlab = "M7"
            for m in range(DJ):
                lw, lwbs = outm_t[0], outm_b[0]
                ldw(lw, outwb[:, 128 * m:128 * m + 128], 16, 128, lwbs,
                    reads=[B("outwbD")], eng="P")
                bank = m % 4
                for k in range(DT_J):
                    mm(psv(bank), lw[:, 128 * k:128 * k + 128], y3_t[k],
                       lwbs + [y3b(k)], [PB[bank]], start=(k == 0),
                       stop=(k == DT_J - 1), inc=(1 if k == DT_J - 1 else 0))
                act(rx(m), psv(bank), AF.Copy, [PB[bank]], [RXb[m]])
            if c == 0:
                tapw("mo0", rx(0), [RXb[0]])
            if mstage < 10:
                continue
            sch.curlab = "M8"
            sA2, sQ2 = psv(6, CH, p0=0, pn=1), psv(7, CH, p0=0, pn=1)
            for j in range(DJ):
                mm(sA2, oneskr, rx(j), [RXb[j], B("constD")], [PB[6]],
                   start=(j == 0), stop=(j == DJ - 1), inc=(1 if j == DJ - 1 else 0))
            sq2 = rx(8)
            for j in range(DJ):
                act(sq2, rx(j), AF.Square, [RXb[j]], [RXb[8]])
                mm(sQ2, oneskr, sq2, [RXb[8], B("constD")], [PB[7]],
                   start=(j == 0), stop=(j == DJ - 1), inc=(1 if j == DJ - 1 else 0))
            ln_rows(sA2, sQ2, PB[6], PB[7], CH, D, 2)
            ld(muB, rr.bitcast(F32R)[2:3, 0:CH].partition_broadcast(128), [B("muBD")],
               reads=[B("rrM2")])
            ld(rsB, rr.bitcast(F32R)[3:4, 0:CH].partition_broadcast(128), [B("rsBD")],
               reads=[B("rrM2")])
            for j in range(DJ):
                ld(xch[j], x_in[128 * j:128 * j + 128, t0:t0 + CH], [xch_b[j]])
            for j in range(DJ):
                tmp = axv(0, 512)
                vstt(tmp, rx(j).bitcast(F32), 1.0, muB, ALU.mult, ALU.subtract,
                     [RXb[j], B("muBD")], [XIr])
                vtt(rx(j), tmp, rsB, ALU.mult, [XIr, B("rsBD")],
                    [RXb[j]])
                vtt(tmp, rx(j).bitcast(F32), xch[j].bitcast(F32), ALU.add,
                    [RXb[j], xch_b[j]], [XIr])
                st(x1s[128 * j:128 * j + 128, t0:t0 + CH], tmp.bitcast(F32),
                   [XIr], writes=[B("x1sD")])
                if c == 0 and j == 0:
                    tapw("x10", tmp.bitcast(F32R), [XIr])

        sch.barrier()

        AXF, AQT, AKT, ASC3 = 0, 4160, 6208, 10304
        ANR, ASQ, AAT = 11840, 12352, 12872
        ALW1, ALW2, ANG = 14920, 16968, 17992
        ATQ, AVB, AVT, AES = 0, 8320, 10368, 14464
        XAX1, XART, XAMU, XARW = 0, 4096, 6144, 7184
        XAN2, XAN2C, XARB2 = 8744, 9256, 9768
        XARC, XADN = 10280, 10792

        def ffn_attn(l, srcdram, dstdram):
            x1t = [axv(XAX1 + 512 * j, 512).bitcast(F32R) for j in range(DJ)]
            x1b = [B("ax1D") for j in range(DJ)]
            xft = [arv(AXF + 520 * j, 520) for j in range(DJ)]
            xfb = [B("axfD") for j in range(DJ)]
            tt_ = [abv(ATQ + 1040 * j, 1040, F32R) for j in range(DJ)]
            ttb = [B(f"a_t{j}") for j in range(DJ)]
            sqa = arv(ASQ, 520)
            sqab = B("a_sq")
            qt = [arv(AQT + 512 * m, 512) for m in range(4)]
            qtb = [B(f"a_q{m}") for m in range(4)]
            kt = [[arv(AKT + 2048 * sl + 512 * m, 512) for m in range(4)]
                  for sl in range(2)]
            ktb = [[B(f"a{l}k{sl}{m}") for m in range(4)] for sl in range(2)]
            scr = [arv(ASC3 + 512 * i, 512) for i in range(3)]
            scrb = [B(f"a_scr{i}") for i in range(3)]
            sqnt = [arv(ANR, 512), arv(ASQ, 512)]
            sqnbt = [B("a_sqn"), sqab]
            rstdBt = [axv(XARB2, 512).bitcast(F32R), axv(XAN2, 512).bitcast(F32R)]
            rstdBbt = [B("a_rstdB0"), B("a_rstdB1")]
            vb = [abv(AVB + 512 * m, 512) for m in range(4)]
            vbb = [B(f"a_v{m}") for m in range(4)]
            vT = [[abv(AVT + 2048 * sl + 64 * i, 64) for i in range(32)]
                  for sl in range(2)]
            vTb = [[B(f"a{l}vT{sl}{i}") for i in range(32)] for sl in range(2)]
            eSTt = [[abv(AES + 512 * (8 * par + m), 512) for m in range(8)]
                    for par in range(3)]
            eSTbt = [[B(f"a_e{8 * par + m}") for m in range(8)] for par in range(3)]
            negrowt = [arv(ANG, 512), arv(ANG + 512, 512)]
            negrowbt = [B("a_negrow0"), B("a_negrow1")]
            dnrowt = [AXS[0:1, XADN:XADN + 512].bitcast(F32R),
                      AXS[0:1, XARW:XARW + 512].bitcast(F32R)]
            dnrowbt = [B("a_n3"), B("a_mr")]
            recipBt = [axv(XARC, 512).bitcast(F32R)[0:64, :],
                       axv(XAN2C, 512).bitcast(F32R)[0:64, :]]
            recipBbt = [B("a_recipB0"), B("a_recipB1")]
            att = [arv(AAT + 512 * m, 512) for m in range(4)]
            attb = [B(f"a_att{m}") for m in range(4)]
            xmid = [axv(XAX1 + 512 * j, 512).bitcast(F32R) for j in range(DJ)]
            xmidb = [B("ax1D") for j in range(DJ)]
            rtab = {nm: axv(XART + 512 * i, 512).bitcast(F32R)
                    for i, nm in enumerate(("csq", "ssq", "csk", "ssk"))}
            rtabb = {nm: B("artD") for nm in rtab}
            qlw = [arv(ALW1 + 1024 * i, 1024) for i in range(2)]
            qlwb = [B(f"a_qlw{i}") for i in range(2)]
            wlw = [arv(ALW2 + 512 * i, 512) for i in range(2)]
            wlwb = [B(f"a_wlw{i}") for i in range(2)]
            muBa, rsBa = axv(XAMU, 520).bitcast(F32R), axv(XAMU + 520, 520).bitcast(F32R)
            muBab, rsBab = B("a_muB"), B("a_rsB")
            mrA = AXS[0:1, XARW:XARW + 520].bitcast(F32R)
            vrA = AXS[0:1, XARW + 520:XARW + 1040].bitcast(F32R)
            rrA = AXS[0:1, XARW + 1040:XARW + 1560].bitcast(F32R)
            rowAb = [B("a_mr"), B("a_vr"), B("a_rr")]
            nrm2t = [AXS[0:2, XARW + 520:XARW + 1032].bitcast(F32R),
                     AXS[0:2, XARW + 1040:XARW + 1552].bitcast(F32R)]
            nrm2bt = [B("a_vr"), B("a_rr")]
            nrm2ct = [AXS[0:2, XARW:XARW + 512].bitcast(F32R),
                      AXS[0:2, XADN:XADN + 512].bitcast(F32R)]
            nrm2cbt = [B("a_mr"), B("a_n3")]

            for c in range(NCH):
                sch.curlab = "Aload"
                t0 = CH * c
                sl_c, sl_p = c % 2, (c + 1) % 2
                for j in range(DJ):
                    if c == 0:
                        ld(xft[j][:, 0:1], dram["zeros_c"].bitcast(F32R)[:, 0:1],
                           [xfb[j]])
                    elif l == 0:
                        vcopy(xft[j][:, 0:1], xft[j][:, 512:513], [xfb[j]],
                              [xfb[j]])
                if l == 0:
                    for j in range(DJ):
                        ld(xft[j][:, 1:513],
                           srcdram.bitcast(F32R)[128 * j:128 * j + 128, t0:t0 + CH],
                           [xfb[j]], reads=[B("x1sD")])
                else:
                    for j in range(DJ):
                        if c == 0:
                            ld(xft[j][:, 1:513],
                               srcdram.bitcast(F32R)[128 * j:128 * j + 128,
                                                     t0:t0 + CH],
                               [xfb[j]], reads=[B("xl0D")])
                        else:
                            ld(xft[j][:, 0:513],
                               srcdram.bitcast(F32R)[128 * j:128 * j + 128,
                                                     t0 - 1:t0 + CH],
                               [xfb[j]], reads=[B("xl0D")])
                if l == 0:
                    sch.curlab = "Aln"
                    sA = psv(6, CH, p0=0, pn=1)
                    sQ = psv(7, CH, p0=0, pn=1)
                    for j in range(DJ):
                        mm(sA, svv("onesf"), xft[j][:, 1:513].bitcast(F32),
                           [xfb[j], B("constD")],
                           [PB[6]], start=(j == 0), stop=(j == DJ - 1),
                           inc=(1 if j == DJ - 1 else 0))
                    for j in range(DJ):
                        act(sqa[:, 0:512], xft[j][:, 1:513], AF.Square, [xfb[j]],
                            [sqab])
                        mm(sQ, oneskr, sqa[:, 0:512], [sqab, B("constD")],
                           [PB[7]], start=(j == 0), stop=(j == DJ - 1),
                           inc=(1 if j == DJ - 1 else 0))
                    act(mrA[:, 0:512], sA, AF.Copy, [PB[6]], [rowAb[0]], scale=1.0 / D)
                    act(vrA[:, 0:512], sQ, AF.Copy, [PB[7]], [rowAb[1]], scale=1.0 / D)
                    vtt(rrA[:, 0:512], mrA[:, 0:512], mrA[:, 0:512], ALU.mult,
                        [rowAb[0]], [rowAb[2]])
                    vtt(vrA[:, 0:512], vrA[:, 0:512], rrA[:, 0:512], ALU.subtract,
                        [rowAb[1], rowAb[2]], [rowAb[1]])
                    act(rrA[:, 0:512], vrA[:, 0:512], AF.Ln,
                        [rowAb[1], B("epsc")], [rowAb[2]],
                        bias=svv("epsc")[0:1, :])
                    act(rrA[:, 0:512], rrA[:, 0:512], AF.Exp, [rowAb[2]], [rowAb[2]],
                        scale=-0.5)
                    st(rr[4:5, 0:512], mrA[:, 0:512].bitcast(F32), [rowAb[0]],
                       writes=[B("rrA45")], eng="P")
                    st(rr[5:6, 0:512], rrA[:, 0:512].bitcast(F32), [rowAb[2]],
                       writes=[B("rrA45")], eng="P")
                    ld(muBa[:, 0:512], rr.bitcast(F32R)[4:5, 0:512].partition_broadcast(128),
                       [muBab], reads=[B("rrA45")], eng="P")
                    ld(rsBa[:, 0:512], rr.bitcast(F32R)[5:6, 0:512].partition_broadcast(128),
                       [rsBab], reads=[B("rrA45")], eng="P")
                    for j in range(DJ):
                        vstt(tt_[j][:, 0:512], xft[j][:, 1:513].bitcast(F32), 1.0,
                             muBa[:, 0:512], ALU.mult, ALU.subtract,
                             [xfb[j], muBab], [ttb[j]])
                        vtt(xft[j][:, 1:513], tt_[j][:, 0:512],
                            rsBa[:, 0:512], ALU.mult, [ttb[j], rsBab], [xfb[j]])
                if c == 0 and l == 0:
                    tapw("xf0", xft[0][:, 1:513], [xfb[0]])
                sch.curlab = "Aqkv"
                for nm in rtab:
                    ld(rtab[nm], dram[nm].bitcast(F32R)[:, t0:t0 + CH], [rtabb[nm]],
                       eng="P")

                def qkvmm(m):
                    lw, lwb = qlw[m % 2], qlwb[m % 2]
                    bank = m % 5
                    for k in range(DJ):
                        mv_ap = xft[k][:, 1:513] if k < 4 else xft[k][:, 0:512]
                        mm(psv(bank), lw[:, 128 * k:128 * k + 128], mv_ap,
                           [lwb, xfb[k]], [PB[bank]], start=(k == 0),
                           stop=(k == DJ - 1), inc=(1 if k == DJ - 1 else 0))

                def qstats(m):
                    bank = m % 5
                    par = m % 2
                    sqn, sqnb = sqnt[par], sqnbt[par]
                    nrm2, nrm2b = nrm2t[par], nrm2bt[par]
                    nrm2c, nrm2cb = nrm2ct[par], nrm2cbt[par]
                    rstdB, rstdBb = rstdBt[par], rstdBbt[par]
                    isq = m < 4
                    mt = m if isq else m - 4
                    act(sqn, psv(bank), AF.Square, [PB[bank]], [sqnb])
                    nps = PS[0:2, 512 * (5 + par):512 * (5 + par) + 512]
                    mm(nps, headones, sqn, [sqnb, B("constD")], [PB[5 + par]])
                    act(nrm2, nps, AF.Sqrt, [PB[5 + par]], [nrm2b])
                    vts1(nrm2, nrm2, 1e-12, ALU.max, [nrm2b], [nrm2b])
                    vrecip(nrm2c, nrm2, [nrm2b], [nrm2cb])
                    slot = (6 + 2 * m) if isq else (14 + 2 * mt)
                    st(rr[slot:slot + 2, 0:512], nrm2c.bitcast(F32), [nrm2cb],
                       writes=[B(f"rrn{m % 2}")], eng="P")
                    ld(rstdB[0:64, :], rr.bitcast(F32R)[slot:slot + 1, 0:512]
                       .partition_broadcast(64), [rstdBb], reads=[B(f"rrn{m % 2}")],
                       eng="P")
                    ld(rstdB[64:128, :],
                       rr.bitcast(F32R)[slot + 1:slot + 2, 0:512]
                       .partition_broadcast(64), [rstdBb],
                       reads=[rstdBb, B(f"rrn{m % 2}")], eng="P")

                def qapply(m):
                    bank = m % 5
                    if m < 8:
                        isq = m < 4
                        mt = m if isq else m - 4
                        dstt = qt[mt] if isq else kt[sl_c][mt]
                        dsttb = qtb[mt] if isq else ktb[sl_c][mt]
                        cs_t = rtab["csq" if isq else "csk"]
                        ss_t = rtab["ssq" if isq else "ssk"]
                        csb_ = rtabb["csq" if isq else "csk"]
                        ssb_ = rtabb["ssq" if isq else "ssk"]
                        sccol = svv("qs" if isq else "ks")[:, l:l + 1]
                        scbuf = B("c_qs" if isq else "c_ks")
                        par = m % 2
                        rstdB, rstdBb = rstdBt[par], rstdBbt[par]
                        vstt(scr[0], psv(bank), sccol, rstdB, ALU.mult, ALU.mult,
                             [PB[bank], rstdBb, scbuf], [scrb[0]])
                        mm(psv(7), rperm, scr[0], [scrb[0], B("constD")], [PB[7]])
                        vstt(scr[1], psv(7), 1.0, ss_t, ALU.mult, ALU.mult,
                             [PB[7], ssb_], [scrb[1]])
                        vtt(scr[2], scr[0].bitcast(F32), cs_t, ALU.mult,
                            [scrb[0], csb_], [scrb[2]])
                        vtt(dstt, scr[2].bitcast(F32),
                            scr[1].bitcast(F32), ALU.add, [scrb[1], scrb[2]], [dsttb])
                    else:
                        mt = m - 8
                        act(vb[mt], psv(bank), AF.Copy, [PB[bank]], [vbb[mt]])
                        for hh in range(2):
                            h_ = 2 * mt + hh
                            for ts in range(4):
                                pv = psv(7, 32, 0, BF16)
                                trans(pv, vb[mt][64 * hh:64 * hh + 64,
                                                 128 * ts:128 * ts + 128],
                                      identb[64 * hh:64 * hh + 64,
                                             64 * hh:64 * hh + 64],
                                      [vbb[mt], B("constD")], [PB[7]])
                                act(vT[sl_c][4 * h_ + ts], pv, AF.Copy, [PB[7]],
                                    [vTb[sl_c][4 * h_ + ts]])

                ldw(qlw[0], wqkv[l][:, 0:128], 8, 128, [qlwb[0]])
                ldw(qlw[1], wqkv[l][:, 128:256], 8, 128, [qlwb[1]])
                for m in range(14):
                    if m < 12:
                        qkvmm(m)
                    if m + 2 < 12:
                        ldw(qlw[(m + 2) % 2], wqkv[l][:, 128 * (m + 2):128 * (m + 2) + 128],
                            8, 128, [qlwb[(m + 2) % 2]])
                    if 1 <= m <= 8:
                        qstats(m - 1)
                    if m >= 2:
                        qapply(m - 2)
                if c == 0 and l == 0:
                    tapw("qt0", qt[0], [qtb[0]])
                    tapw("kt0", kt[sl_c][0], [ktb[sl_c][0]])

                def kloc(h_, i):
                    mt, hh = h_ // 2, h_ % 2
                    sl = sl_p if i < 4 else sl_c
                    ii = i % 4
                    return (kt[sl][mt][64 * hh:64 * hh + 64,
                                       128 * ii:128 * ii + 128], ktb[sl][mt])

                def vloc(h_, i):
                    if i < 4:
                        return vT[sl_p][4 * h_ + i], vTb[sl_p][4 * h_ + i]
                    return vT[sl_c][4 * h_ + i - 4], vTb[sl_c][4 * h_ + i - 4]

                m2list = list(range(8)) if c > 0 else list(range(4, 8))
                sch.curlab = "ArowS"
                ldw(wlw[0], wod[l][:, 0:128], 4, 128, [wlwb[0]])
                ldw(wlw[1], wod[l][:, 128:256], 4, 128, [wlwb[1]])
                for h_ in range(NH):
                    mt, hh = h_ // 2, h_ % 2
                    qsl = qt[mt][64 * hh:64 * hh + 64, :]
                    for a_i in range(4):
                        dlist = list(range(0, 5)) if c > 0 else list(range(4 - a_i, 5))
                        for d_ in dlist:
                            kap, kbf = kloc(h_, a_i + d_)
                            needmask = (d_ == 0 and c > 0) or (d_ == 4)
                            mm(PS[:, 128 * d_:128 * d_ + 128],
                               qsl[:, 128 * a_i:128 * a_i + 128], kap,
                               [qtb[mt], kbf], [PB[0], PB[1]], start=True,
                               stop=not needmask, inc=(0 if needmask else 0))
                            if d_ == 0 and c > 0:
                                mm(PS[:, 0:128], identr, maskd0,
                                   [B("constD"), B("constD")], [PB[0], PB[1]],
                                   start=False, stop=True, inc=0)
                            if d_ == 4:
                                mm(PS[:, 512:640], identr, maskd4,
                                   [B("constD"), B("constD")], [PB[0], PB[1]],
                                   start=False, stop=True,
                                   inc=1)
                        dmin = dlist[0]
                        ngb = B(f"a_ng{h_}_{a_i}")
                        negm = svv("altmp")[:, 4 * h_ + a_i:4 * h_ + a_i + 1]
                        negm2 = svv("altmp")[:, 32 + a_i:33 + a_i]
                        if dmin < 4:
                            vmax(negm, PS[:, 128 * dmin:512], [PB[0], PB[1]],
                                 [ngb], negate=True)
                            vmax(negm2, PS[:, 512:640], [PB[0], PB[1]],
                                 [B(f"a_ng2{a_i}")], negate=True)
                            vtt(negm, negm, negm2, ALU.min,
                                [ngb, B(f"a_ng2{a_i}")], [ngb])
                        else:
                            vmax(negm, PS[:, 512:640], [PB[0], PB[1]],
                                 [ngb], negate=True)
                        st(rr[30 + h_:31 + h_, 128 * a_i:128 * a_i + 128], negm,
                           [ngb], writes=[B(f"rrg{h_ % 2}")], eng="P")
                for h_ in range(2):
                    ld(negrowt[h_][0:1, :], rr.bitcast(F32R)[30 + h_:31 + h_, 0:512],
                       [negrowbt[h_]], reads=[B(f"rrg{h_ % 2}")], eng="P")
                for v in range(NH + 2):
                    if v < NH:
                        sch.curlab = "AsT"
                        h_ = v
                        mt, hh = h_ // 2, h_ % 2
                        eST, eSTb = eSTt[h_ % 3], eSTbt[h_ % 3]
                        negrow, negrowb = negrowt[h_ % 2], negrowbt[h_ % 2]
                        qsl = qt[mt][64 * hh:64 * hh + 64, :]
                        for m2 in m2list:
                            bank = 2 + (m2 % 4)
                            kap, kbf = kloc(h_, m2)
                            mm(psv(bank), kap, qsl, [kbf, qtb[mt]],
                               [PB[bank]], start=True, stop=False, inc=0)
                            off = 896 - 128 * m2
                            mm(psv(bank), identr, maskTv[:, off:off + 512],
                               [B("constD"), B("constD")], [PB[bank]],
                               start=False, stop=False, inc=0)
                            mm(psv(bank), onesrow, negrow[0:1, :],
                               [negrowb, B("constD")], [PB[bank]],
                               start=False, stop=True, inc=1)
                            act(eST[m2], psv(bank), AF.Exp, [PB[bank]], [eSTb[m2]])
                            if c == 0 and l == 0 and h_ == 0 and m2 == 4:
                                tapw("es40", eST[4], [eSTb[4]])
                        if c == 0 and l == 0 and h_ == 0:
                            tapw("ng0", negrow[0:1, :], [negrowb])
                        if v + 2 < NH:
                            ld(negrowt[v % 2][0:1, :],
                               rr.bitcast(F32R)[32 + v:33 + v, 0:512],
                               [negrowbt[v % 2]], reads=[B(f"rrg{v % 2}")], eng="P")
                    if 1 <= v <= NH:
                        sch.curlab = "Adn"
                        hd = v - 1
                        eST, eSTb = eSTt[hd % 3], eSTbt[hd % 3]
                        dnrow, dnrowb = dnrowt[hd % 2], dnrowbt[hd % 2]
                        recipB, recipBb = recipBt[hd % 2], recipBbt[hd % 2]
                        dn = PS[0:1, 512 * 6:512 * 6 + 512]
                        for i, m2 in enumerate(m2list):
                            mm(dn, oneskb, eST[m2], [eSTb[m2], B("constD")],
                               [PB[6]], start=(i == 0), stop=(i == len(m2list) - 1),
                               inc=(1 if i == len(m2list) - 1 else 0))
                        vrecip(dnrow, dn, [PB[6]], [dnrowb])
                        st(rr[22 + hd:23 + hd, 0:512], dnrow.bitcast(F32), [dnrowb],
                           writes=[B(f"rrd{hd % 2}")], eng="P")
                        ld(recipB, rr.bitcast(F32R)[22 + hd:23 + hd, 0:512]
                           .partition_broadcast(64),
                           [recipBb], reads=[B(f"rrd{hd % 2}")], eng="P")
                    if v >= 2:
                        sch.curlab = "Aav"
                        ha = v - 2
                        mt, hh = ha // 2, ha % 2
                        eST, eSTb = eSTt[ha % 3], eSTbt[ha % 3]
                        recipB, recipBb = recipBt[ha % 2], recipBbt[ha % 2]
                        ao = PS[0:64, 512 * 7:512 * 7 + 512]
                        for i, m2 in enumerate(m2list):
                            vap, vbf = vloc(ha, m2)
                            mm(ao, vap, eST[m2], [vbf, eSTb[m2]], [PB[7]],
                               start=(i == 0), stop=(i == len(m2list) - 1),
                               inc=(1 if i == len(m2list) - 1 else 0))
                        vtt(att[mt][64 * hh:64 * hh + 64, :], ao, recipB,
                            ALU.mult, [PB[7], recipBb], [attb[mt]])
                        if c == 0 and l == 0 and ha == 1:
                            tapw("att00", att[0], [attb[0]])
                sch.curlab = "Awo"
                for m in range(DJ):
                    lw, lwb = wlw[m % 2], wlwb[m % 2]
                    bank = m % 4
                    for k in range(4):
                        mm(psv(bank), lw[:, 128 * k:128 * k + 128], att[k],
                           [lwb, attb[k]], [PB[bank]], start=(k == 0), stop=(k == 3),
                           inc=(1 if k == 3 else 0))
                    if m + 2 < DJ:
                        ldw(wlw[m % 2], wod[l][:, 128 * (m + 2):128 * (m + 2) + 128],
                            4, 128, [wlwb[m % 2]])
                    vstt(xmid[m], psv(bank), 1.0, xft[m][:, 1:513].bitcast(F32),
                         ALU.mult, ALU.add, [PB[bank], xfb[m]], [xmidb[m]])
                    st(dstdram[128 * m:128 * m + 128, t0:t0 + CH],
                       xmid[m].bitcast(F32), [xmidb[m]], writes=[B(f"xm{l}D")])
                if c == 0 and l == 0:
                    tapw("xm00", xmid[0], [xmidb[0]])

        BTT, BW1, BHN, BMU2 = 0, 8192, 9216, 17408
        BPR, BAG, BW2 = 0, 22528, 26624
        XBXM, XBRW = 0, 8256

        def ffn_geglu(l, srcdram, dstdram, final=False):
            xmt = [axv(XBXM + 1032 * j, 1032).bitcast(F32R) for j in range(DJ)]
            xmb = [B("bxmD") for j in range(DJ)]
            hnt = [arv(BHN + 1024 * j, 1024) for j in range(DJ)]
            hnb = [B(f"b_hn{j}") for j in range(DJ)]
            ttt = [arv(BTT + 1024 * j, 1024) for j in range(DJ)]
            ttb2 = [B(f"b_tt{j}") for j in range(DJ)]
            sqbb2 = ttb2[0]
            sqb = arv(BTT, 1024)
            sqbb = sqbb2
            prod = [abv(BPR + 1024 * p, 1024) for p in range(NPAIR)]
            prodb = [B(f"b_pr{p}") for p in range(NPAIR)]
            aevt = [abv(BAG + 1024 * i, 1024) for i in range(2)]
            aevbt = [B("b_aev0"), B("b_aev1")]
            gevt = [abv(BAG + 2048 + 1024 * i, 1024) for i in range(2)]
            gevbt = [B("b_gev0"), B("b_gev1")]
            muBc = arv(BMU2, 1024)
            rsBc = arv(BMU2 + 1024, 1024)
            muBcb, rsBcb = B("b_muB"), B("b_rsB")
            w1lw = arv(BW1, 1024)
            w1lwb = B("b_w1lw")
            w2lw = abv(BW2, 2816)
            w2lwb = B("b_w2lw")
            mrB = AXS[0:1, XBRW:XBRW + 1024].bitcast(F32R)
            vrB = AXS[0:1, XBRW + 1024:XBRW + 2048].bitcast(F32R)
            rrB = AXS[0:1, XBRW + 2048:XBRW + 3072].bitcast(F32R)
            rowBb = [B("b_mr"), B("b_vr"), B("b_rr")]
            xl_t = hnt
            xl_b = hnb
            srcD = B(f"xm{l}D")
            dstD = B("xl0D")

            for s_ in range(NSC):
                sch.curlab = "Bload"
                t0 = SC * s_
                for j in range(DJ):
                    if s_ == 0:
                        ld(xmt[j][:, 0:1],
                           dram["zeros_c"].bitcast(F32R)[:, 0:1], [xmb[j]])
                        ld(xmt[j][:, 1:1025],
                           srcdram.bitcast(F32R)[128 * j:128 * j + 128, t0:t0 + SC],
                           [xmb[j]], reads=[srcD])
                    else:
                        ld(xmt[j][:, 0:1025],
                           srcdram.bitcast(F32R)[128 * j:128 * j + 128,
                                                 t0 - 1:t0 + SC],
                           [xmb[j]], reads=[srcD])

                def hap(j):
                    return xmt[j][:, 1:1025] if j < 4 else xmt[j][:, 0:1024]

                sch.curlab = "Bstat"
                for half in range(2):
                    cS = slice(512 * half, 512 * half + 512)
                    sA = PS[0:1, 512 * (4 + half):512 * (4 + half) + 512]
                    sQ = PS[0:1, 512 * (6 + half):512 * (6 + half) + 512]
                    for j in range(DJ):
                        mm(sA, svv("onesf"), hap(j)[:, cS].bitcast(F32),
                           [xmb[j], B("constD")],
                           [PB[4 + half]], start=(j == 0), stop=(j == DJ - 1),
                           inc=(1 if j == DJ - 1 else 0))
                    for j in range(DJ):
                        act(sqb[:, cS], hap(j)[:, cS], AF.Square, [xmb[j]], [sqbb])
                        mm(sQ, oneskr, sqb[:, cS], [sqbb, B("constD")],
                           [PB[6 + half]], start=(j == 0), stop=(j == DJ - 1),
                           inc=(1 if j == DJ - 1 else 0))
                    act(mrB[:, cS], sA, AF.Copy, [PB[4 + half]], [rowBb[0]],
                        scale=1.0 / D)
                    act(vrB[:, cS], sQ, AF.Copy, [PB[6 + half]], [rowBb[1]],
                        scale=1.0 / D)
                vtt(rrB, mrB, mrB, ALU.mult, [rowBb[0]], [rowBb[2]])
                vtt(vrB, vrB, rrB, ALU.subtract, [rowBb[1], rowBb[2]], [rowBb[1]])
                act(rrB, vrB, AF.Ln, [rowBb[1], B("epsc")], [rowBb[2]], bias=svv("epsc")[0:1, :])
                act(rrB, rrB, AF.Exp, [rowBb[2]], [rowBb[2]], scale=-0.5)
                st(rr[38:39, 0:SC], mrB.bitcast(F32), [rowBb[0]],
                   writes=[B("rrB38")], eng="P")
                st(rr[39:40, 0:SC], rrB.bitcast(F32), [rowBb[2]],
                   writes=[B("rrB38")], eng="P")
                ld(muBc, rr.bitcast(F32R)[38:39, 0:SC].partition_broadcast(128), [muBcb],
                   reads=[B("rrB38")], eng="P")
                ld(rsBc, rr.bitcast(F32R)[39:40, 0:SC].partition_broadcast(128), [rsBcb],
                   reads=[B("rrB38")], eng="P")
                sch.curlab = "Bln"
                for j in range(DJ):
                    vstt(ttt[j], hap(j).bitcast(F32), 1.0, muBc,
                         ALU.mult, ALU.subtract, [xmb[j], muBcb], [ttb2[j]])
                    vtt(hnt[j], ttt[j].bitcast(F32), rsBc, ALU.mult,
                        [ttb2[j], rsBcb], [hnb[j]])
                    vts2(ttt[j], hnt[j].bitcast(F32),
                         svv("lng")[:, 8 * l + j:8 * l + j + 1],
                         svv("lnb")[:, 8 * l + j:8 * l + j + 1], ALU.mult, ALU.add,
                         [hnb[j], B("constD"), B("constD")], [ttb2[j]])
                if s_ == 0 and l == 0:
                    tapw("hn0", ttt[0][:, 0:512], [ttb2[0]])
                sch.curlab = "Bw1"
                coffs = []
                off_ = 0
                for p in range(NPAIR):
                    wdt = min(128, HID - 128 * p)
                    coffs.append((off_, wdt))
                    off_ += 2 * wdt
                w1slots = [w1lw, muBc, rsBc]
                w1sbufs = [w1lwb, muBcb, rsBcb]
                groups = [(p, ag) for p in range(NPAIR) for ag in range(2)]

                def w1gld(i):
                    p, ag = groups[i]
                    coff, wdt = coffs[p]
                    ldw(w1slots[i % 3][:, 0:8 * wdt],
                        w1p[l][:, coff + ag * wdt:coff + ag * wdt + wdt],
                        8, wdt, [w1sbufs[i % 3]])

                for i in range(3):
                    w1gld(i)
                for i, (p, ag) in enumerate(groups):
                    coff, wdt = coffs[p]
                    slot, slotb = w1slots[i % 3], w1sbufs[i % 3]
                    for half in range(2):
                        bank = 2 * ag + half
                        cS = slice(512 * half, 512 * half + 512)
                        for k in range(DJ):
                            mm(PS[0:wdt, 512 * bank:512 * bank + 512],
                               slot[:, wdt * k:wdt * k + wdt], ttt[k][:, cS],
                               [slotb, ttb2[k]], [PB[bank]], start=(k == 0),
                               stop=(k == DJ - 1), inc=(1 if k == DJ - 1 else 0))
                    if i + 3 < len(groups):
                        w1gld(i + 3)
                    if ag != 1:
                        continue
                    pp = p % 2
                    aev, aevb = aevt[pp], aevbt[pp]
                    gev, gevb = gevt[pp], gevbt[pp]
                    bca = svv("b1c")[0:wdt, 2 * NPAIR * l + 2 * p:
                                     2 * NPAIR * l + 2 * p + 1]
                    bcg = svv("b1c")[0:wdt, 2 * NPAIR * l + 2 * p + 1:
                                     2 * NPAIR * l + 2 * p + 2]
                    for half in range(2):
                        cS = slice(512 * half, 512 * half + 512)
                        act(aev[0:wdt, cS], PS[0:wdt, 512 * half:512 * half + 512],
                            AF.Copy, [PB[half]], [aevb])
                        act(gev[0:wdt, cS],
                            PS[0:wdt, 1024 + 512 * half:1024 + 512 * half + 512],
                            AF.Gelu, [PB[2 + half]], [gevb], bias=bcg)
                    vstt(prod[p][0:wdt, :], aev[0:wdt, :], bca, gev[0:wdt, :],
                         ALU.add, ALU.mult, [aevb, gevb, B("constD")], [prodb[p]])
                sch.curlab = "Bw2"

                def w2qld(qi):
                    m_, q = qi // 5, qi % 5
                    kp0 = 5 * q
                    ntile = min(5, 21 - kp0)
                    sl = w2lw[:, 640 * (qi % 4):640 * (qi % 4) + 640]
                    ldw(sl[:, 0:128 * ntile],
                        w2b[l][128 * kp0:128 * (kp0 + ntile),
                               128 * m_:128 * m_ + 128],
                        ntile, 128, [B(f"w2q{qi % 4}")],
                        reads=[B(f"w2b{l}D")], eng="P")
                    if q == 4:
                        ld(sl[0:RAG, 128 * ntile:128 * ntile + 128],
                           w2b[l][2688:2730, 128 * m_:128 * m_ + 128],
                           [B(f"w2q{qi % 4}")],
                           reads=[B(f"w2q{qi % 4}"), B(f"w2b{l}D")],
                           eng="P")

                for qi in range(3):
                    w2qld(qi)
                for m in range(DJ):
                    for kp in range(NPAIR):
                        q = kp // 5
                        qi = 5 * m + q
                        wdt = min(128, HID - 128 * kp)
                        sl = w2lw[:, 640 * (qi % 4):640 * (qi % 4) + 640]
                        tl = sl[0:wdt, 128 * (kp - 5 * q):128 * (kp - 5 * q) + 128]
                        for half in range(2):
                            bank = 4 + 2 * (m % 2) + half
                            cS = slice(512 * half, 512 * half + 512)
                            mm(psv(bank), tl,
                               prod[kp][0:wdt, cS], [B(f"w2q{qi % 4}"), prodb[kp]],
                               [PB[bank]],
                               start=(kp == 0), stop=(kp == NPAIR - 1),
                               inc=(1 if kp == NPAIR - 1 else 0))
                        if kp - 5 * q == 0 and qi + 3 < 5 * DJ:
                            w2qld(qi + 3)
                    for half in range(2):
                        bank = 4 + 2 * (m % 2) + half
                        cS = slice(512 * half, 512 * half + 512)
                        vstt(xl_t[m][:, cS], psv(bank),
                             svv("b2c")[:, 8 * l + m:8 * l + m + 1],
                             xmt[m][:, 1:1025].bitcast(F32)[:, cS], ALU.add, ALU.add,
                             [PB[bank], xmb[m], B("constD")], [xl_b[m]])
                    if not final:
                        st(dstdram[128 * m:128 * m + 128, t0:t0 + SC],
                           xl_t[m].bitcast(F32), [xl_b[m]], writes=[dstD])
                if s_ == 0 and l == 0:
                    tapw("xl00", xl_t[0].bitcast(F32)[:, 0:512], [xl_b[0]])
                if final:
                    sch.curlab = "Bfin"
                    x1f = [axv(XBXM + 1024 * j, 1024).bitcast(F32R) for j in range(DJ)]
                    x1fb = [B("bxmD") for j in range(DJ)]
                    for j in range(DJ):
                        ld(x1f[j], x1s.bitcast(F32R)[128 * j:128 * j + 128,
                                                         t0:t0 + SC],
                           [B("bxmD")], reads=[B("x1sD")])
                    for half in range(2):
                        cS = slice(512 * half, 512 * half + 512)
                        sA = PS[0:1, 512 * half:512 * half + 512]
                        sQ = PS[0:1, 1024 + 512 * half:1024 + 512 * half + 512]
                        for j in range(DJ):
                            mm(sA, oneskr, xl_t[j][:, cS], [xl_b[j], B("constD")],
                               [PB[half]], start=(j == 0), stop=(j == DJ - 1),
                               inc=(1 if j == DJ - 1 else 0))
                        for j in range(DJ):
                            act(sqb[:, cS], xl_t[j][:, cS], AF.Square, [xl_b[j]],
                                [sqbb])
                            mm(sQ, oneskr, sqb[:, cS], [sqbb, B("constD")],
                               [PB[2 + half]], start=(j == 0), stop=(j == DJ - 1),
                               inc=(1 if j == DJ - 1 else 0))
                        act(mrB[:, cS], sA, AF.Copy, [PB[half]], [rowBb[0]],
                            scale=1.0 / D)
                        act(vrB[:, cS], sQ, AF.Copy, [PB[2 + half]], [rowBb[1]],
                            scale=1.0 / D)
                    vtt(rrB, mrB, mrB, ALU.mult, [rowBb[0]], [rowBb[2]])
                    vtt(vrB, vrB, rrB, ALU.subtract, [rowBb[1], rowBb[2]], [rowBb[1]])
                    act(rrB, vrB, AF.Ln, [rowBb[1], B("epsc")], [rowBb[2]], bias=svv("epsc")[0:1, :])
                    act(rrB, rrB, AF.Exp, [rowBb[2]], [rowBb[2]], scale=-0.5)
                    st(rr[40:41, 0:SC], mrB.bitcast(F32), [rowBb[0]],
                       writes=[B("rrB40")], eng="P")
                    st(rr[41:42, 0:SC], rrB.bitcast(F32), [rowBb[2]],
                       writes=[B("rrB40")], eng="P")
                    ld(muBc, rr.bitcast(F32R)[40:41, 0:SC].partition_broadcast(128), [muBcb],
                       reads=[B("rrB40")], eng="P")
                    ld(rsBc, rr.bitcast(F32R)[41:42, 0:SC].partition_broadcast(128), [rsBcb],
                       reads=[B("rrB40")], eng="P")
                    tf = abv(BAG, 2048, F32R)
                    tfb = aevb
                    o1 = arv(BW1, 1024)
                    o1b = w1lwb
                    ot = axv(XBRW, 1024).bitcast(F32R)
                    otb = rowBb[0]
                    for j in range(DJ):
                        vstt(tf, xl_t[j].bitcast(F32), 1.0, muBc, ALU.mult,
                             ALU.subtract, [xl_b[j], muBcb], [tfb, gevb])
                        vtt(o1, tf, rsBc, ALU.mult, [tfb, rsBcb], [o1b])
                        vtt(ot, o1, x1f[j].bitcast(F32), ALU.add,
                            [o1b, x1fb[j]], [otb])
                        st(out[128 * j:128 * j + 128, t0:t0 + SC],
                           ot.bitcast(F32), [otb], writes=[B("outD")])

        ph = trunc.get("ph", 5) if trunc else 5
        if ph >= 1:
            sch.mark("A1")
            ffn_attn(0, x1s, xms[0])
            sch.barrier()
        if ph >= 2:
            sch.mark("B1")
            ffn_geglu(0, xms[0], xl0)
            sch.barrier()
        if ph >= 3:
            sch.mark("A2")
            ffn_attn(1, xl0, xms[1])
            sch.barrier()
        if ph >= 4:
            sch.mark("B2")
            ffn_geglu(1, xms[1], None, final=True)
            sch.barrier()
        for sem, v in sch.cnt.items():
            if v > 0:
                sch.items["S"].append(("w", sem, v))
        global _LAST_COUNTS, _LAST_LABS
        _LAST_COUNTS = dict(sch.cnt)
        _LAST_LABS = {e: list(v) for e, v in sch.labs.items()}

        for semname, v in sch.cnt.items():
            if v > 0 or semname in Sched.ENGS:
                sems[semname] = es.enter_context(nc.semaphore(
                    semname.replace(":", "_")))
        assert len(sems) <= 200, f"too many semaphores: {len(sems)}"
        block = es.enter_context(nc.Block())

        def replay(E, ename):
            with nc.allow_non_contiguous_dma(
                    reason="1-column lead/zero loads are tiny and rare"), \
                 nc.allow_low_precision(
                    reason="float32r views are fp32-storage; PSUM accumulation fp32"):
                cur = None
                for it in sch.items[ename]:
                    if it[0] == "m":
                        if cur is not None:
                            nc.leave_named_scope(cur[0], cur[1], False)
                        sid, _ = nc.enter_named_scope(it[1], False)
                        cur = (it[1], sid)
                    elif it[0] == "w":
                        E.wait_ge(sems[it[1]], it[2])
                    else:
                        _, fn, sem, inc = it
                        ins = fn(E)
                        if inc > 0:
                            ins.then_inc(sems[sem], inc)
                if cur is not None:
                    nc.leave_named_scope(cur[0], cur[1], False)

        @block.tensor
        def _(eng):
            replay(eng, "T")

        @block.vector
        def _(eng):
            replay(eng, "V")

        @block.scalar
        def _(eng):
            replay(eng, "A")

        @block.gpsimd
        def _(eng):
            replay(eng, "P")

        @block.sync
        def _(eng):
            replay(eng, "S")

    return nc


def kernel(**inputs):
    taps = dict(_TAPS)
    consts = host_constants()
    hw = host_layout(inputs)
    nc = build(taps)
    X = np.asarray(inputs["L0_attn_x"])
    ncores = 8
    shared = {}
    shared.update({k: np.ascontiguousarray(v) for k, v in consts.items()})
    shared.update(hw)
    in_maps = []
    for b in range(ncores):
        m = dict(shared)
        m["x"] = np.ascontiguousarray(X[b].T)
        in_maps.append(m)
    res = run_bass_kernel_spmd(nc, in_maps, core_ids=list(range(ncores)))
    outs = np.stack([np.asarray(r["out"]).T for r in res.results], 0)
    if taps:
        kernel.last_taps = [{nm: np.asarray(r["tap_" + nm]) for nm in taps}
                            for r in res.results]
    return outs.astype(np.float32)
